# version 01
# baseline (speedup 1.0000x reference)
import contextlib
import numpy as np
import ml_dtypes
import concourse.bass as bass
import concourse.mybir as mybir
from concourse.bass_utils import run_bass_kernel_spmd

F32 = mybir.dt.float32
BF16 = mybir.dt.bfloat16
U32 = mybir.dt.uint32
AF = mybir.ActivationFunctionType
ALU = mybir.AluOpType
AX = mybir.AxisListType
IOA = bass.IndirectOffsetOnAxis

BATCH, SEQ, D = 4, 8192, 2048
NCORE = 8
NCH = 16
CH = SEQ // NCH
TOK = SEQ * BATCH // NCORE
NT = TOK // 128
SLOTS = 8
TPS = CH // 128
DC = D // 128
HEADS, HD = 8, 256
ROPE_THETA = 1.0e6
E, HFF = 8, 2816
HCH = [512] * 5 + [256]
NHC = len(HCH)
HC = HFF // 128
CAP = 4096
KPAD = 640
KTILES = KPAD // 128
DISP_ROWS = E * KPAD + 128
EPS_LN = 1e-5
EPS_RMS = 1e-6
A_SET = [0, 3, 4, 7, 8, 11, 12, 15]
B_SET = [1, 2, 5, 6, 9, 10, 13, 14]
SETS = (A_SET, B_SET)

UPTO = 4
STRICT_EDGES = True


def tsl(i, n):
    return slice(i * n, (i + 1) * n)


class Op:
    __slots__ = ("q", "idx", "kind", "emit", "deps", "sdeps", "label")

    def __init__(self, q, idx, kind, emit, deps, sdeps, label):
        self.q, self.idx, self.kind = q, idx, kind
        self.emit, self.deps, self.sdeps, self.label = emit, deps, sdeps, label


class Queue:
    NROT = 8

    def __init__(self, B, name, attr):
        self.B, self.name, self.attr = B, name, attr
        self.ops = []
        self.sem = B.es.enter_context(B.nc.semaphore(f"q_{name}"))
        self.rsem = [B.es.enter_context(B.nc.semaphore(f"r_{name}{i}"))
                     for i in range(self.NROT)]
        self.n_dma = 0
        self.n_comp = 0

    def add(self, kind, emit, deps=(), sdeps=(), label=""):
        deps = [d for d in deps if d is not None]
        sdeps = [d for d in sdeps if d is not None]
        if kind == "dma":
            idx = self.n_dma
            self.n_dma += 1
        else:
            idx = self.n_comp
            self.n_comp += 1
        o = Op(self, idx, kind, emit, deps, sdeps, label)
        self.ops.append(o)
        return o


class Builder:
    def __init__(self, nc, es):
        self.nc, self.es = nc, es
        self.d = {}
        self.Q = {}
        self._uid = 0
        for name, attr in (("PE", "tensor"), ("DVE", "vector"), ("ACT", "scalar"),
                           ("POOL", "gpsimd"), ("SP", "sync")):
            self.Q[name] = Queue(self, name, attr)

    def sbuf(self, ph, name, shape, dt=F32):
        self._uid += 1
        return ph.enter_context(self.nc.sbuf_tensor(f"{name}_{self._uid}", shape, dt))

    def psum(self, ph, name, shape, dt=F32):
        self._uid += 1
        return ph.enter_context(self.nc.psum_tensor(f"{name}_{self._uid}", shape, dt))

    def dma(self, qname, out, in_, deps=(), label="", **kw):
        q = self.Q[qname]
        return q.add("dma", lambda e, out=out, in_=in_: e.dma_start(out=out, in_=in_, **kw),
                     deps=deps, label=label)

    def op(self, qname, emit, deps=(), sdeps=(), label=""):
        return self.Q[qname].add("comp", emit, deps=deps, sdeps=sdeps, label=label)

    def finalize(self, block):
        for q in self.Q.values():
            self._emit_queue(block, q)

    def _emit_queue(self, block, q):
        if not q.ops:
            return
        section = getattr(block, q.attr)

        def body(eng):
            hw = {}

            def wait(sem, val):
                k = id(sem)
                if hw.get(k, 0) >= val:
                    return
                eng.wait_ge(sem, val)
                hw[k] = val

            for o in q.ops:
                if o.kind == "dma" and o.idx >= Queue.NROT:
                    wait(q.rsem[o.idx % Queue.NROT], 16 * (o.idx // Queue.NROT))
                deps = list(o.deps) + list(o.sdeps)
                for didx, dep in enumerate(deps):
                    same_q = dep.q is q
                    forced = didx >= len(o.deps)
                    if (same_q and not forced and not STRICT_EDGES
                            and dep.kind == "comp"):
                        continue
                    if dep.kind == "dma":
                        sem = dep.q.rsem[dep.idx % Queue.NROT]
                        val = 16 * (dep.idx // Queue.NROT + 1)
                    elif dep.kind == "cc":
                        sem = self.cc_sem
                        val = dep.idx + 1
                    else:
                        sem = dep.q.sem
                        val = dep.idx + 1
                    wait(sem, val)
                ins = o.emit(eng)
                if o.kind == "dma":
                    ins.then_inc(q.rsem[o.idx % Queue.NROT], 16)
                elif o.kind == "cc":
                    ins.then_inc(self.cc_sem, 1)
                else:
                    ins.then_inc(q.sem, 1)

        section(body)

    def collective(self, kind, op, groups, ins_ap, outs_ap, deps=(), label=""):
        q = self.Q["POOL"]
        if not hasattr(self, "cc_sem"):
            self.cc_sem = self.es.enter_context(self.nc.semaphore("cc"))
            self.cc_n = 0
            self.cc_last = None
        idx = self.cc_n
        self.cc_n += 1
        deps = [d for d in deps if d is not None]
        if self.cc_last is not None:
            deps.append(self.cc_last)
        o = Op(q, idx, "cc",
               lambda e: e.collective_compute(kind, op, replica_groups=groups,
                                              ins=[ins_ap], outs=[outs_ap]),
               deps, [], label)
        q.ops.append(o)
        self.cc_last = o
        return o

    def fence(self, label="fence"):
        lasts = []
        for q in self.Q.values():
            lastc = next((o for o in reversed(q.ops) if o.kind != "dma"), None)
            lastd = next((o for o in reversed(q.ops) if o.kind == "dma"), None)
            if lastc is not None:
                lasts.append(lastc)
            if lastd is not None:
                lasts.append(lastd)
        for q in self.Q.values():
            q.add("comp", lambda e: e.nop(), deps=list(lasts), label=label)


def mm_group(B, ps_ap, pairs, deps, label=""):
    n = len(pairs)

    def emit(e):
        last = None
        for i, (l, r) in enumerate(pairs):
            last = e.matmul(ps_ap, l, r, start=(i == 0), stop=(i == n - 1))
        return last
    return B.op("PE", emit, deps=deps, label=label)


def tr_group(B, items, ident, deps, label=""):
    def emit(e):
        last = None
        for o, i in items:
            last = e.transpose(o, i, ident)
        return last
    return B.op("PE", emit, deps=deps, label=label)


def rsqrt(B, out_ap, in_ap, eps, scale, dep, label="rsqrt"):
    eps_ap = B.sb["eps"][:, 0:1] if eps == EPS_LN else B.sb["eps"][:, 1:2]
    o1 = B.op("ACT", lambda e: e.activation(out_ap, in_ap, AF.Ln, bias=eps_ap, scale=scale),
              deps=[dep, B.eps_dep], label=label + "_ln")
    return B.op("ACT", lambda e: e.activation(out_ap, out_ap, AF.Exp, scale=-0.5),
                deps=[o1], sdeps=[o1], label=label + "_exp")


def ln_tile(B, sb, x_ap, y_ap, scratch_ap, mean_ap, var_ap, rstd_ap, sum_ap, ss_ap,
            x_dep, y_war=None, scr_war=None, label="ln"):
    s1 = B.op("DVE", lambda e: e.reduce_sum(sum_ap, x_ap, axis=AX.X),
              deps=[x_dep], label=label + "_sum")
    s2 = B.op("DVE", lambda e: e.scalar_tensor_tensor(out=scratch_ap, in0=x_ap, scalar=1.0,
                                                      in1=x_ap, op0=ALU.bypass, op1=ALU.mult,
                                                      accum_out=ss_ap),
              deps=[x_dep, scr_war], label=label + "_ss")
    m = B.op("DVE", lambda e: e.tensor_scalar(out=mean_ap, in0=sum_ap, scalar1=1.0 / D,
                                              scalar2=None, op0=ALU.mult),
             deps=[s1], sdeps=[s1], label=label + "_mean")
    m2v = B.op("DVE", lambda e: e.tensor_tensor(out=var_ap, in0=mean_ap, in1=mean_ap, op=ALU.mult),
               deps=[m], sdeps=[m], label=label + "_m2")
    var = B.op("DVE", lambda e: e.scalar_tensor_tensor(out=var_ap, in0=ss_ap, scalar=1.0 / D,
                                                       in1=var_ap, op0=ALU.mult, op1=ALU.subtract),
               deps=[s2, m2v], sdeps=[s2, m2v], label=label + "_var")
    r = rsqrt(B, rstd_ap, var_ap, EPS_LN, 1.0, var, label=label + "_r")
    yw = y_war if isinstance(y_war, (list, tuple)) else [y_war]
    y = B.op("DVE", lambda e: e.tensor_scalar(out=y_ap, in0=x_ap, scalar1=mean_ap,
                                              scalar2=rstd_ap, op0=ALU.subtract, op1=ALU.mult),
             deps=[r] + list(yw), sdeps=[m], label=label + "_y")
    return y


def build_nc(upto=None, dbg=()):
    if upto is None:
        upto = UPTO
    nc = bass.Bass("TRN2", target_bir_lowering=False, debug=False)
    es = contextlib.ExitStack()
    B = Builder(nc, es)
    d = B.d

    def din(name, shape, dt=F32):
        d[name] = nc.dram_tensor(name, shape, dt, kind="ExternalInput").ap()

    B.dbg = set(dbg)

    def dint(name, shape, dt):
        kind = "ExternalOutput" if (name in dbg and "bag" not in name) else "Internal"
        d[name] = nc.dram_tensor(name, shape, dt, kind=kind).ap()
        if name in dbg and "bag" in name:
            d[name + "_dbg"] = nc.dram_tensor(name + "_dbg", shape, dt,
                                              kind="ExternalOutput").ap()

    din("x_in", [TOK, D])
    din("qTw", [D, D]); din("kTw", [D, HD]); din("vTw", [D, HD]); din("oTw", [D, D])
    din("qkvw", [D, 3 * D]); din("obw", [D, D])
    din("Wg", [D, E])
    din("W1", [E, D, HFF]); din("W3", [E, D, HFF]); din("W2", [E, HFF, D])
    din("sw1", [D, HFF]); din("sw3", [D, HFF]); din("sw2", [HFF, D])
    din("qgain", [1, HD]); din("kgain", [1, HD])
    din("gammaI", [1, D]); din("betaI", [1, D])
    din("cosT", [TOK, 128]); din("sinT", [TOK, 128])
    din("maskT", [SLOTS * CH, 1024])
    din("selT", [NCORE * SLOTS, E]); din("lts8", [SLOTS, SLOTS])
    din("lt128", [128, 128]); din("ones128", [128, 128])
    din("ident", [128, 128], BF16)
    d["out_x"] = nc.dram_tensor("out_x", [TOK, D], F32, kind="ExternalOutput").ap()

    dint("resid", [TOK, D], F32)
    dint("y_d", [TOK, D], F32)
    dint("QTa", [NT, D, 128], BF16)
    dint("QTb", [NT, D, 128], BF16)
    dint("ctxT_d", [NT, D, 128], BF16)
    dint("ka_in", [HD, TOK], BF16)
    dint("kag", [2 * HD, TOK], BF16)
    dint("va_in", [TOK, HD], BF16)
    dint("vag", [2 * TOK, HD], BF16)
    for g in range(4):
        dint(f"kb_in{g}", [D, 1024], BF16)
        dint(f"kbag{g}", [2 * D, 1024], BF16)
        dint(f"vb_in{g}", [1024, D], BF16)
        dint(f"vbag{g}", [2048, D], BF16)
    dint("counts_in", [SLOTS, E], F32)
    dint("counts_ag", [NCORE * SLOTS, E], F32)
    dint("base_d", [2 * SLOTS, E], F32)
    dint("xnorm_d", [TOK, D], BF16)
    dint("yT_d", [NT, D, 128], BF16)
    dint("hT_d", [NT, HFF, 128], BF16)
    dint("disp_d", [DISP_ROWS, D], BF16)
    dint("dispo_d", [DISP_ROWS, D], BF16)

    with es:
        with nc.Block() as block:
            sb = {}
            sb["ident"] = es.enter_context(nc.sbuf_tensor("c_ident", [128, 128], BF16))
            sb["iota8"] = es.enter_context(nc.sbuf_tensor("c_iota8", [128, 8], F32))
            sb["ones1"] = es.enter_context(nc.sbuf_tensor("c_ones1", [1, 128], BF16))
            sb["lt128"] = es.enter_context(nc.sbuf_tensor("c_lt128", [128, 128], BF16))
            sb["on128"] = es.enter_context(nc.sbuf_tensor("c_on128", [128, 128], BF16))
            sb["zs"] = es.enter_context(nc.sbuf_tensor("c_zs", [128, D], BF16))
            sb["onesc"] = es.enter_context(nc.sbuf_tensor("c_onesc", [128, 1], BF16))
            sb["eps"] = es.enter_context(nc.sbuf_tensor("c_eps", [128, 2], F32))
            B.sb = sb
            ld_i = B.dma("SP", sb["ident"][:], d["ident"], label="ld_ident")
            ld_lt = B.dma("POOL", sb["lt128"][:], d["lt128"], label="ld_lt")
            ld_on = B.dma("POOL", sb["on128"][:], d["ones128"], label="ld_on")
            B.op("POOL", lambda e: e.iota(sb["iota8"][:], pattern=[[1, E]], base=0,
                                          channel_multiplier=0,
                                          allow_small_or_imprecise_dtypes=True),
                 label="iota8")
            B.op("DVE", lambda e: e.memset(sb["ones1"][:], 1.0), label="ones1")
            B.op("DVE", lambda e: e.memset(sb["zs"][:], 0.0), label="zs")
            B.op("DVE", lambda e: e.memset(sb["onesc"][:], 1.0), label="onesc")
            e1_ = B.op("DVE", lambda e: e.memset(sb["eps"][:, 0:1], EPS_LN), label="eps1")
            B.eps_dep = B.op("DVE", lambda e: e.memset(sb["eps"][:, 1:2], EPS_RMS),
                             deps=[e1_], label="eps2")
            B.setup_deps = dict(ident=ld_i, lt=ld_lt, on=ld_on)
            B.fence("setup")

            phase_a1(B)
            B.fence("a1")
            B.ka_cc = B.collective("AllGather", ALU.bypass,
                                   [[0, 1], [2, 3], [4, 5], [6, 7]],
                                   d["ka_in"], d["kag"], label="ag_ka")
            B.va_cc = B.collective("AllGather", ALU.bypass,
                                   [[0, 1], [2, 3], [4, 5], [6, 7]],
                                   d["va_in"], d["vag"], label="ag_va")
            phase_flash(B, "a")
            B.fence("a2")
            phase_oproj(B, "oTw", "ctxT_d", src="x_in",
                        dst=("out_x" if upto == 1 else "resid"), add_y=False)
            B.fence("a3")
            if upto >= 2:
                phase_moe(B, src="resid", dst=("out_x" if upto == 2 else "resid"))
                B.fence("moe1")
            if upto >= 3:
                phase_b1(B)
                B.fence("b1x")
                phase_flash(B, "b")
                B.fence("b2")
                phase_oproj(B, "obw", "ctxT_d", src="resid",
                            dst=("out_x" if upto == 3 else "resid"), add_y=True)
                B.fence("b3")
            if upto >= 4:
                phase_moe(B, src="resid", dst="out_x")
                B.fence("moe2")
            B.fence("final")
            B.finalize(block)
    return nc


def phase_a1(B):
    nc, d, sb = B.nc, B.d, B.sb
    ph = contextlib.ExitStack()
    T = lambda n, s, dt=F32: B.sbuf(ph, n, s, dt)
    xsb = [T(f"a1x{i}", [128, D]) for i in range(2)]
    cs = [T(f"a1c{i}", [128, 128]) for i in range(2)]
    sn = [T(f"a1s{i}", [128, 128]) for i in range(2)]
    ysb = [T(f"a1y{i}", [128, D]) for i in range(2)]
    ybf = [T(f"a1yb{i}", [128, D], BF16) for i in range(2)]
    yT = [T(f"a1yT{i}", [128, D], BF16) for i in range(2)]
    sc1 = [T(f"a1t1{i}", [128, D]) for i in range(2)]
    qf = [T(f"a1qf{i}", [128, D]) for i in range(2)]
    qn_ = ysb
    qrot = [T(f"a1qr{i}", [128, D], BF16) for i in range(2)]
    qTr = [T(f"a1qTr{i}", [128, D], BF16) for i in range(2)]
    kv = [T(f"a1kv{i}", [128, 4 * HD]) for i in range(2)]
    krot = [T(f"a1kr{i}", [128, HD], BF16) for i in range(2)]
    kTr = [T(f"a1kTr{i}", [128, HD], BF16) for i in range(2)]
    vbf = [T(f"a1vb{i}", [128, HD], BF16) for i in range(2)]
    sm = [T(f"a1sm{i}", [128, 32]) for i in range(2)]
    qw = T("a1qw", [128, DC * D], BF16)
    kw = T("a1kw", [128, DC * HD], BF16)
    vw = T("a1vw", [128, DC * HD], BF16)
    qg = T("a1qg", [128, HD])
    kg = T("a1kg", [128, HD])
    pT = [B.psum(ph, f"a1pT{i}", [128, 1024], BF16) for i in range(2)]
    pQ = [B.psum(ph, f"a1pQ{i}", [128, 512], F32) for i in range(2)]
    pK = B.psum(ph, "a1pK", [128, HD], F32)
    pV = B.psum(ph, "a1pV", [128, HD], F32)

    with ph:
        wq = B.dma("POOL", qw[:].rearrange("p (c n) -> p c n", c=DC),
                   d["qTw"].rearrange("(c dd) n -> dd c n", dd=128), label="a1_wq")
        wk = B.dma("POOL", kw[:].rearrange("p (c n) -> p c n", c=DC),
                   d["kTw"].rearrange("(c dd) n -> dd c n", dd=128), label="a1_wk")
        wv = B.dma("POOL", vw[:].rearrange("p (c n) -> p c n", c=DC),
                   d["vTw"].rearrange("(c dd) n -> dd c n", dd=128), label="a1_wv")
        gq = B.dma("SP", qg[:], d["qgain"].broadcast_to([128, HD]), label="a1_gq")
        gk = B.dma("SP", kg[:], d["kgain"].broadcast_to([128, HD]), label="a1_gk")

        last = {}

        for t in range(NT):
            p = t % 2
            lx = B.dma("SP", xsb[p][:], d["x_in"][tsl(t, 128), :],
                       deps=[last.get(("x", p))], label=f"a1_lx{t}")
            lc = B.dma("SP", cs[p][:], d["cosT"][tsl(t, 128), :],
                       deps=[last.get(("rope", p))], label=f"a1_lc{t}")
            ls = B.dma("SP", sn[p][:], d["sinT"][tsl(t, 128), :],
                       deps=[last.get(("rope", p))], label=f"a1_ls{t}")
            yop = ln_tile(B, sb, xsb[p][:], ysb[p][:], sc1[p][:],
                          sm[p][:, 1:2], sm[p][:, 3:4], sm[p][:, 4:5],
                          sm[p][:, 0:1], sm[p][:, 2:3], lx,
                          y_war=last.get(("ybf", p)), scr_war=last.get(("sq", p)),
                          label=f"a1ln{t}")
            last[("x", p)] = yop
            cop = B.op("ACT", lambda e, p=p: e.copy(ybf[p][:], ysb[p][:]),
                       deps=[yop, last.get(("ybf", p))], label=f"a1cast{t}")
            trs = []
            for half in range(2):
                items = [(pT[half][:, tsl(i, 128)],
                          ybf[p][:, tsl(half * 8 + i, 128)]) for i in range(8)]
                g = tr_group(B, items, sb["ident"][:],
                             deps=[cop, B.setup_deps["ident"], last.get(("pT", half))],
                             label=f"a1trY{t}_{half}")
                ev = B.op("DVE", lambda e, p=p, half=half:
                          e.tensor_copy(yT[p][:, tsl(half, 1024)], pT[half][:]),
                          deps=[g, last.get(("yT", p))], label=f"a1evY{t}_{half}")
                last[("pT", half)] = ev
                last[("ybf", p)] = g
                trs.append(ev)
            qevs = []
            for nb in range(4):
                pb = nb % 2
                grp = mm_group(B, pQ[pb][:],
                               [(yT[p][:, tsl(c, 128)], qw[:, c * D + nb * 512: c * D + nb * 512 + 512])
                                for c in range(DC)],
                               deps=trs + [wq, last.get(("pQ", pb))], label=f"a1q{t}_{nb}")
                ev = B.op("DVE", lambda e, p=p, nb=nb, pb=pb:
                          e.tensor_copy(qf[p][:, tsl(nb, 512)], pQ[pb][:]),
                          deps=[grp, last.get(("qf", p))], label=f"a1qe{t}_{nb}")
                last[("pQ", pb)] = ev
                qevs.append(ev)
            kgrp = mm_group(B, pK[:],
                            [(yT[p][:, tsl(c, 128)], kw[:, tsl(c, HD)]) for c in range(DC)],
                            deps=trs + [wk, last.get("pK")], label=f"a1k{t}")
            kev = B.op("DVE", lambda e, p=p: e.tensor_copy(kv[p][:, 0:HD], pK[:]),
                       deps=[kgrp, last.get(("kv", p))], label=f"a1ke{t}")
            last["pK"] = kev
            vgrp = mm_group(B, pV[:],
                            [(yT[p][:, tsl(c, 128)], vw[:, tsl(c, HD)]) for c in range(DC)],
                            deps=trs + [wv, last.get("pV")], label=f"a1v{t}")
            vev = B.op("DVE", lambda e, p=p: e.tensor_copy(vbf[p][:], pV[:]),
                       deps=[vgrp, last.get(("vbf", p))], label=f"a1ve{t}")
            last["pV"] = vev
            last[("yT", p)] = vgrp
            sq = B.op("ACT", lambda e, p=p: e.square(sc1[p][:], qf[p][:]),
                      deps=qevs + [yop, last.get(("ropesc", p))],
                      label=f"a1sq{t}")
            qsss = []
            for h in range(HEADS):
                r = B.op("DVE", lambda e, p=p, h=h:
                         e.reduce_sum(sm[p][:, 8 + h: 9 + h], sc1[p][:, tsl(h, HD)], axis=AX.X),
                         deps=[sq], label=f"a1qss{t}_{h}")
                qsss.append(r)
            rq = rsqrt(B, sm[p][:, 16:24], sm[p][:, 8:16], EPS_RMS, 1.0 / HD,
                       qsss[-1], label=f"a1rq{t}")
            rq_all = [rq] + qsss
            qns = []
            for h in range(HEADS):
                o = B.op("DVE", lambda e, p=p, h=h:
                         e.scalar_tensor_tensor(out=qn_[p][:, tsl(h, HD)], in0=qf[p][:, tsl(h, HD)],
                                                scalar=sm[p][:, 16 + h: 17 + h], in1=qg[:],
                                                op0=ALU.mult, op1=ALU.mult),
                         deps=[rq, gq], sdeps=qsss, label=f"a1qn{t}_{h}")
                qns.append(o)
            last[("qf", p)] = qns[-1]
            kss = B.op("DVE", lambda e, p=p:
                       e.scalar_tensor_tensor(out=kv[p][:, HD: 2 * HD], in0=kv[p][:, 0:HD],
                                              scalar=1.0, in1=kv[p][:, 0:HD],
                                              op0=ALU.bypass, op1=ALU.mult,
                                              accum_out=sm[p][:, 24:25]),
                       deps=[kev], label=f"a1kss{t}")
            rk = rsqrt(B, sm[p][:, 25:26], sm[p][:, 24:25], EPS_RMS, 1.0 / HD,
                       kss, label=f"a1rk{t}")
            kn = B.op("DVE", lambda e, p=p:
                      e.scalar_tensor_tensor(out=kv[p][:, HD: 2 * HD], in0=kv[p][:, 0:HD],
                                             scalar=sm[p][:, 25:26], in1=kg[:],
                                             op0=ALU.mult, op1=ALU.mult),
                      deps=[rk, gk, kss], sdeps=[kss], label=f"a1kn{t}")
            last[("kv", p)] = kn
            H2 = HD // 2
            ropes = []

            def rope_emit(src_ap_fn, dst_ap_fn, scr_ap, n_heads, extra_deps):
                outs = []
                for h in range(n_heads):
                    s0 = src_ap_fn(h, 0)
                    s1 = src_ap_fn(h, 1)
                    t1 = B.op("DVE", lambda e, s0=s0, scr=scr_ap(0): e.tensor_mul(scr, s0, cs[p][:]),
                              deps=extra_deps, label=f"a1ro{t}")
                    t2 = B.op("DVE", lambda e, s1=s1, scr=scr_ap(1): e.tensor_mul(scr, s1, sn[p][:]),
                              deps=extra_deps, label=f"a1ro{t}")
                    o1 = B.op("DVE", lambda e, d0=dst_ap_fn(h, 0), a=scr_ap(0), b=scr_ap(1):
                              e.tensor_sub(d0, a, b), deps=[t1, t2], label=f"a1ro{t}")
                    t3 = B.op("DVE", lambda e, s0=s0, scr=scr_ap(0): e.tensor_mul(scr, s0, sn[p][:]),
                              deps=[o1], label=f"a1ro{t}")
                    t4 = B.op("DVE", lambda e, s1=s1, scr=scr_ap(1): e.tensor_mul(scr, s1, cs[p][:]),
                              deps=[o1], label=f"a1ro{t}")
                    o2 = B.op("DVE", lambda e, d1=dst_ap_fn(h, 1), a=scr_ap(0), b=scr_ap(1):
                              e.tensor_add(d1, a, b), deps=[t3, t4], label=f"a1ro{t}")
                    outs.append(o2)
                return outs

            q_src = lambda h, half: qn_[p][:, h * HD + half * H2: h * HD + half * H2 + H2]
            q_dst = lambda h, half: qrot[p][:, h * HD + half * H2: h * HD + half * H2 + H2]
            q_scr = lambda i: sc1[p][:, i * H2:(i + 1) * H2]
            ropes = rope_emit(q_src, q_dst, q_scr, HEADS,
                              [qns[0], qns[-1], lc, ls, last.get(("qrot", p)), sq])
            k_src = lambda h, half: kv[p][:, HD + half * H2: HD + half * H2 + H2]
            k_dst = lambda h, half: krot[p][:, half * H2: half * H2 + H2]
            k_scr = lambda i: kv[p][:, 2 * HD + i * H2: 2 * HD + (i + 1) * H2]
            kroped = rope_emit(k_src, k_dst, k_scr, 1,
                               [kn, lc, ls, last.get(("krot", p))])
            last[("rope", p)] = kroped[-1]
            last[("ropesc", p)] = ropes[-1]
            last[("sq", p)] = ropes[-1]
            qst = []
            for half in range(2):
                items = [(pT[half][:, tsl(i, 128)],
                          qrot[p][:, tsl(half * 8 + i, 128)]) for i in range(8)]
                g = tr_group(B, items, sb["ident"][:],
                             deps=ropes + [last.get(("pT", half))], label=f"a1trQ{t}_{half}")
                ev = B.op("DVE", lambda e, p=p, half=half:
                          e.tensor_copy(qTr[p][:, tsl(half, 1024)], pT[half][:]),
                          deps=[g, last.get(("qTr", p))], label=f"a1evQ{t}_{half}")
                last[("pT", half)] = ev
                qst.append(ev)
            last[("qrot", p)] = qst[-1]
            sq_dma = B.dma("POOL", d["QTa"][t].rearrange("(c dd) n -> dd c n", dd=128),
                           qTr[p][:].rearrange("p (c n) -> p c n", c=DC),
                           deps=qst, label=f"a1stQ{t}")
            last[("qTr", p)] = sq_dma
            gk2 = tr_group(B, [(pT[0][:, tsl(i, 128)], krot[p][:, tsl(i, 128)]) for i in range(2)],
                           sb["ident"][:], deps=kroped + [last.get(("pT", 0))],
                           label=f"a1trK{t}")
            kev2 = B.op("DVE", lambda e, p=p: e.tensor_copy(kTr[p][:], pT[0][:, 0:HD]),
                        deps=[gk2, last.get(("kTr", p))], label=f"a1evK{t}")
            last[("pT", 0)] = kev2
            last[("krot", p)] = kev2
            sk_dma = B.dma("POOL", d["ka_in"].rearrange("(c dd) n -> dd c n", dd=128)[:, :, tsl(t, 128)],
                           kTr[p][:].rearrange("p (c n) -> p c n", c=2),
                           deps=[kev2], label=f"a1stK{t}")
            last[("kTr", p)] = sk_dma
            sv_dma = B.dma("POOL", d["va_in"][tsl(t, 128), :], vbf[p][:],
                           deps=[vev], label=f"a1stV{t}")
            last[("vbf", p)] = sv_dma


def chunk_par(c):
    return 0 if c in A_SET else 1


def chunk_lslot(c):
    return SETS[chunk_par(c)].index(c)


def phase_flash(B, kind):
    nc, d, sb = B.nc, B.d, B.sb
    ph = contextlib.ExitStack()
    T = lambda n, s, dt=F32: B.sbuf(ph, n, s, dt)
    heads = HEADS if kind == "a" else 1
    hd = HD if kind == "a" else D
    scale = (1.0 / 16.0) if kind == "a" else (1.0 / float(np.sqrt(2048.0)))
    NU = TPS * heads
    q_dram = d["QTa"] if kind == "a" else d["QTb"]

    mask_sb = T("f_mask", [128, TPS * 1024])
    qts = [T(f"f_q{i}", [128, D], BF16) for i in range(TPS)]
    ctx = T("f_ctx", [128, 8192])
    stt_ = T("f_st", [128, 4 * 64])
    P = [T(f"f_P{i}", [128, 1024], BF16) for i in range(2)]
    PT = [T(f"f_PT{i}", [128, 1024], BF16) for i in range(2)]
    ctxb = T("f_ctxb", [128, 2048], BF16)
    ctxT = T("f_ctxT", [128, 2048], BF16)
    if kind == "a":
        KT = T("f_KT", [128, 2 * SEQ], BF16)
        VT = T("f_VT", [128, (SEQ // 128) * HD], BF16)
    else:
        KT = [T(f"f_KTb{i}", [128, DC * 1024], BF16) for i in range(2)]
        VT = T("f_VTb", [128, 8 * D], BF16)
    psS = B.psum(ph, "f_psS", [128, 1024], F32)
    npv = 1 if kind == "a" else 4
    psPV = [B.psum(ph, f"f_pv{i}", [128, 512], F32)
            for i in range(npv if kind == "b" else 2)]
    psT = [B.psum(ph, f"f_pT{i}", [128, 1024], BF16) for i in range(2)]

    def nmc(u):
        return stt_[:, u:u + 1]

    def lc(u):
        return stt_[:, 64 + u:64 + u + 1]

    def tmpc(u):
        return stt_[:, 128 + u:128 + u + 1]

    def t2c(u):
        return stt_[:, 192 + u:192 + u + 1]

    with ph:
        last = {}
        if kind == "a":
            kvdeps = []
            for g in range(NCH):
                par, ls_ = chunk_par(g), chunk_lslot(g)
                for c in range(2):
                    o = B.dma("SP", KT[:, c * SEQ + g * CH: c * SEQ + (g + 1) * CH],
                              d["kag"].rearrange("(pp c dd) n -> pp c dd n", pp=2, c=2)[par, c][:, tsl(ls_, CH)],
                              deps=[B.ka_cc], label=f"fA_k{g}_{c}")
                    kvdeps.append(o)
                for kt in range(4):
                    o = B.dma("SP", VT[:, (g * 4 + kt) * HD: (g * 4 + kt + 1) * HD],
                              d["vag"][par * TOK + ls_ * CH + kt * 128: par * TOK + ls_ * CH + (kt + 1) * 128, :],
                              deps=[B.va_cc], label=f"fA_v{g}_{kt}")
                    kvdeps.append(o)
            last["kv"] = kvdeps

        ucount = 0
        for j in range(SLOTS):
            lm = B.dma("SP", mask_sb[:].rearrange("p (q k) -> p q k", q=TPS),
                       d["maskT"].rearrange("(q pp) k -> pp q k", pp=128)[:, tsl(j, TPS), :],
                       deps=[last.get("mask")], label=f"f_m{j}")
            qlds = []
            for qt in range(TPS):
                tile = j * TPS + qt
                o = B.dma("SP", qts[qt][:].rearrange("p (c n) -> p c n", c=DC),
                          q_dram[tile].rearrange("(c dd) n -> dd c n", dd=128),
                          deps=[last.get(("q", qt))], label=f"f_q{j}_{qt}")
                qlds.append(o)
            for s in range(j + 1):
                islast = (s == j)
                if kind == "b":
                    sp = s % 2
                    kv_ops = []
                    for half in range(2):
                        c = 2 * s + half
                        par, ls_, g = chunk_par(c), chunk_lslot(c), chunk_lslot(c) // 2
                        col0 = (chunk_lslot(c) % 2) * CH
                        for cc_ in range(DC):
                            o = B.dma("SP",
                                      KT[sp][:, cc_ * 1024 + half * CH: cc_ * 1024 + half * CH + CH],
                                      d[f"kbag{g}"].rearrange("(pp c dd) n -> pp c dd n", pp=2, c=DC)[par, cc_][:, col0:col0 + CH],
                                      deps=[B.kb_cc[g], last.get(("KTb", sp))],
                                      label=f"fB_k{j}_{s}_{half}_{cc_}")
                            kv_ops.append(o)
                        for kt in range(4):
                            o = B.dma("SP",
                                      VT[:, (half * 4 + kt) * D: (half * 4 + kt + 1) * D],
                                      d[f"vbag{g}"][par * 1024 + col0 + kt * 128: par * 1024 + col0 + (kt + 1) * 128, :],
                                      deps=[B.vb_cc[g], last.get("VTb")],
                                      label=f"fB_v{j}_{s}_{half}_{kt}")
                            kv_ops.append(o)
                    last[("KTb_ld", sp)] = kv_ops
                for qt in range(TPS):
                    for h in range(heads):
                        u = qt * heads + h
                        up = ucount % 2
                        ucount += 1
                        qkg = []
                        for kb in range(2):
                            if kind == "a":
                                pairs = [(qts[qt][:, tsl(2 * h + c, 128)],
                                          KT[:, c * SEQ + (s * 1024 + kb * 512): c * SEQ + (s * 1024 + kb * 512) + 512])
                                         for c in range(2)]
                                kdeps = last["kv"] if s == 0 and j == 0 else [last["kv"][0]]
                                kdeps = last["kv"]
                            else:
                                pairs = [(qts[qt][:, tsl(c, 128)],
                                          KT[s % 2][:, c * 1024 + kb * 512: c * 1024 + kb * 512 + 512])
                                         for c in range(DC)]
                                kdeps = last[("KTb_ld", s % 2)]
                            g = mm_group(B, psS[:, tsl(kb, 512)], pairs,
                                         deps=[qlds[qt]] + kdeps +
                                              [last.get(("psS", kb))], label=f"fqk{j}_{s}_{u}_{kb}")
                            qkg.append(g)
                        if kind == "b":
                            last[("KTb", s % 2)] = qkg[-1]
                        bm = B.op("DVE", lambda e, u=u: e.reduce_max(tmpc(u), psS[:], axis=AX.X,
                                                                    negate=True),
                                  deps=qkg, label=f"fbm{j}_{s}_{u}")
                        if s == 0:
                            nmu = B.op("DVE", lambda e, u=u: e.tensor_copy(nmc(u), tmpc(u)),
                                       deps=[bm], sdeps=[bm], label=f"fnm0{j}_{u}")
                            bco = B.op("DVE", lambda e, u=u: e.tensor_scalar(
                                out=t2c(u), in0=nmc(u), scalar1=scale, scalar2=None, op0=ALU.mult),
                                deps=[nmu], sdeps=[nmu], label=f"fbc{j}_{u}")
                            alph = None
                        else:
                            dlt = B.op("DVE", lambda e, u=u: e.tensor_tensor(
                                out=t2c(u), in0=tmpc(u), in1=nmc(u), op=ALU.subtract),
                                deps=[bm], sdeps=[bm, last.get(("lupd", u))], label=f"fdl{j}_{s}_{u}")
                            dlt2 = B.op("DVE", lambda e, u=u: e.tensor_scalar_min(
                                out=t2c(u), in0=t2c(u), scalar1=0.0),
                                deps=[dlt], sdeps=[dlt], label=f"fdl2{j}_{s}_{u}")
                            alph = B.op("ACT", lambda e, u=u: e.activation(
                                tmpc(u), t2c(u), AF.Exp, scale=scale),
                                deps=[dlt2], label=f"fal{j}_{s}_{u}")
                            nmu = B.op("DVE", lambda e, u=u: e.tensor_tensor(
                                out=nmc(u), in0=nmc(u), in1=t2c(u), op=ALU.add),
                                deps=[dlt2, alph], sdeps=[dlt2], label=f"fnm{j}_{s}_{u}")
                            bco = B.op("DVE", lambda e, u=u: e.tensor_scalar(
                                out=t2c(u), in0=nmc(u), scalar1=scale, scalar2=None, op0=ALU.mult),
                                deps=[nmu, alph], sdeps=[nmu], label=f"fbc{j}_{s}_{u}")
                        plc = stt_[:, 254:255]
                        pexp = B.op("ACT", lambda e, u=u, up=up: e.activation(
                            P[up][:], psS[:], AF.Exp, bias=t2c(u), scale=scale,
                            accum_out=plc),
                            deps=[bco, last.get(("P", up)), last.get("plc")],
                            label=f"fex{j}_{s}_{u}")
                        last[("psS", 0)] = pexp
                        last[("psS", 1)] = pexp
                        pmask = None
                        if islast:
                            pmask = B.op("DVE", lambda e, u=u, up=up, qt=qt: e.scalar_tensor_tensor(
                                out=P[up][:], in0=P[up][:], scalar=1.0,
                                in1=mask_sb[:, tsl(qt, 1024)], op0=ALU.bypass, op1=ALU.mult,
                                accum_out=plc),
                                deps=[pexp, lm], label=f"fpm{j}_{s}_{u}")
                        if s == 0:
                            lup = B.op("DVE", lambda e, u=u: e.tensor_copy(lc(u), plc),
                                       deps=[pmask or pexp], sdeps=[pmask or pexp],
                                       label=f"flu0{j}_{u}")
                        else:
                            lup = B.op("DVE", lambda e, u=u: e.scalar_tensor_tensor(
                                out=lc(u), in0=lc(u), scalar=tmpc(u), in1=plc,
                                op0=ALU.mult, op1=ALU.add),
                                deps=[pmask or pexp, alph], sdeps=[pmask or pexp, alph],
                                label=f"flu{j}_{s}_{u}")
                        last[("lupd", u)] = lup
                        last["plc"] = lup
                        trg = tr_group(B, [(psT[up][:, tsl(i % 8, 128)], P[up][:, tsl(i, 128)])
                                           for i in range(8)],
                                       sb["ident"][:],
                                       deps=[pmask or pexp, last.get(("psT", up))],
                                       label=f"ftr{j}_{s}_{u}")
                        ptev = B.op("DVE", lambda e, up=up: e.tensor_copy(PT[up][:], psT[up][:]),
                                    deps=[trg, last.get(("PT", up))], label=f"fpte{j}_{s}_{u}")
                        last[("psT", up)] = ptev
                        last[("P", up)] = trg
                        if kind == "a":
                            pvg = mm_group(B, psPV[up][:, 0:256],
                                           [(PT[up][:, tsl(kt, 128)],
                                             VT[:, (s * 8 + kt) * HD: (s * 8 + kt + 1) * HD])
                                            for kt in range(8)],
                                           deps=[ptev] + ([] if s else []) +
                                                [last.get(("psPV", up))] + last["kv"],
                                           label=f"fpv{j}_{s}_{u}")
                            if s == 0:
                                cup = B.op("DVE", lambda e, u=u, up=up: e.tensor_copy(
                                    ctx[:, tsl(u, 256)], psPV[up][:, 0:256]),
                                    deps=[pvg], label=f"fcu0{j}_{u}")
                            else:
                                cup = B.op("DVE", lambda e, u=u, up=up: e.scalar_tensor_tensor(
                                    out=ctx[:, tsl(u, 256)], in0=ctx[:, tsl(u, 256)],
                                    scalar=tmpc(u), in1=psPV[up][:, 0:256],
                                    op0=ALU.mult, op1=ALU.add),
                                    deps=[pvg, alph], sdeps=[alph, last.get(("ctx", u))],
                                    label=f"fcu{j}_{s}_{u}")
                            last[("psPV", up)] = cup
                            last[("ctx", u)] = cup
                            last[("PT", up)] = pvg
                        else:
                            vdeps = last[("KTb_ld", s % 2)]
                            cups = []
                            for n2 in range(4):
                                pvg = mm_group(B, psPV[n2][:],
                                               [(PT[up][:, tsl(kt, 128)],
                                                 VT[:, kt * D + n2 * 512: kt * D + n2 * 512 + 512])
                                                for kt in range(8)],
                                               deps=[ptev, last.get(("psPV", n2))] + vdeps,
                                               label=f"fpvb{j}_{s}_{u}_{n2}")
                                if s == 0:
                                    cup = B.op("DVE", lambda e, u=u, n2=n2: e.tensor_copy(
                                        ctx[:, u * 2048 + n2 * 512: u * 2048 + (n2 + 1) * 512],
                                        psPV[n2][:]),
                                        deps=[pvg], label=f"fcb0{j}_{u}_{n2}")
                                else:
                                    cup = B.op("DVE", lambda e, u=u, n2=n2: e.scalar_tensor_tensor(
                                        out=ctx[:, u * 2048 + n2 * 512: u * 2048 + (n2 + 1) * 512],
                                        in0=ctx[:, u * 2048 + n2 * 512: u * 2048 + (n2 + 1) * 512],
                                        scalar=tmpc(u), in1=psPV[n2][:],
                                        op0=ALU.mult, op1=ALU.add),
                                        deps=[pvg, alph], sdeps=[alph, last.get(("ctxb", (u, n2)))],
                                        label=f"fcb{j}_{s}_{u}_{n2}")
                                last[("psPV", n2)] = cup
                                last[("ctxb", (u, n2))] = cup
                                cups.append(cup)
                            last[("PT", up)] = cups[-1]
                            last["VTb"] = cups[-1]
            for qt in range(TPS):
                tile = j * TPS + qt
                fins = []
                for h in range(heads):
                    u = qt * heads + h
                    rl = B.op("DVE", lambda e, u=u: e.reciprocal(tmpc(u), lc(u)),
                              deps=[last[("lupd", u)]], sdeps=[last[("lupd", u)]],
                              label=f"frl{j}_{u}")
                    span = 256 if kind == "a" else 2048
                    cb = B.op("DVE", lambda e, u=u, h=h, span=span: e.tensor_scalar(
                        out=ctxb[:, h * span:(h + 1) * span] if kind == "a" else ctxb[:],
                        in0=ctx[:, u * span:(u + 1) * span],
                        scalar1=tmpc(u), scalar2=None, op0=ALU.mult),
                        deps=[rl, last.get(("ctx", u)), last.get("ctxb_war")],
                        sdeps=[rl], label=f"fcb{j}_{u}")
                    fins.append(cb)
                trs = []
                for half in range(2):
                    items = [(psT[half][:, tsl(i, 128)], ctxb[:, tsl(half * 8 + i, 128)])
                             for i in range(8)]
                    g = tr_group(B, items, sb["ident"][:],
                                 deps=fins + [last.get(("psT", half))], label=f"fct{j}_{qt}_{half}")
                    ev = B.op("DVE", lambda e, half=half: e.tensor_copy(
                        ctxT[:, tsl(half, 1024)], psT[half][:]),
                        deps=[g, last.get("ctxT")], label=f"fcte{j}_{qt}_{half}")
                    last[("psT", half)] = ev
                    trs.append(ev)
                last["ctxb_war"] = trs[-1]
                st = B.dma("POOL", d["ctxT_d"][tile].rearrange("(c dd) n -> dd c n", dd=128),
                           ctxT[:].rearrange("p (c n) -> p c n", c=DC),
                           deps=trs, label=f"fst{j}_{qt}")
                last["ctxT"] = st
                last[("q", qt)] = st
            last["mask"] = st


def phase_oproj(B, wname, ctx_name, src, dst, add_y):
    nc, d, sb = B.nc, B.d, B.sb
    ph = contextlib.ExitStack()
    T = lambda n, s, dt=F32: B.sbuf(ph, n, s, dt)
    wsb = T("o_w", [128, DC * D], BF16)
    ctxT = [T(f"o_c{i}", [128, D], BF16) for i in range(2)]
    xs = [T(f"o_x{i}", [128, D]) for i in range(2)]
    ys = [T(f"o_y{i}", [128, D]) for i in range(2)] if add_y else None
    xn = [T(f"o_n{i}", [128, D]) for i in range(2)]
    pO = [B.psum(ph, f"o_p{i}", [128, 512], F32) for i in range(2)]
    with ph:
        wl = B.dma("POOL", wsb[:].rearrange("p (c n) -> p c n", c=DC),
                   d[wname].rearrange("(c dd) n -> dd c n", dd=128), label="o_wl")
        last = {}
        for t in range(NT):
            p = t % 2
            lc_ = B.dma("SP", ctxT[p][:].rearrange("p (c n) -> p c n", c=DC),
                        d[ctx_name][t].rearrange("(c dd) n -> dd c n", dd=128),
                        deps=[last.get(("ct", p))], label=f"o_lc{t}")
            lx = B.dma("SP", xs[p][:], d[src][tsl(t, 128), :],
                       deps=[last.get(("x", p))], label=f"o_lx{t}")
            ly = None
            if add_y:
                ly = B.dma("SP", ys[p][:], d["y_d"][tsl(t, 128), :],
                           deps=[last.get(("y", p))], label=f"o_ly{t}")
            combs = []
            for nb in range(4):
                pb = nb % 2
                grp = mm_group(B, pO[pb][:],
                               [(ctxT[p][:, tsl(c, 128)], wsb[:, c * D + nb * 512: c * D + nb * 512 + 512])
                                for c in range(DC)],
                               deps=[lc_, wl, last.get(("pO", pb))], label=f"o_g{t}_{nb}")
                c1 = B.op("DVE", lambda e, p=p, nb=nb, pb=pb: e.scalar_tensor_tensor(
                    out=xn[p][:, tsl(nb, 512)], in0=xs[p][:, tsl(nb, 512)], scalar=1.0,
                    in1=pO[pb][:], op0=ALU.bypass, op1=ALU.add),
                    deps=[grp, lx, last.get(("xn", p))], label=f"o_c1{t}_{nb}")
                last[("pO", pb)] = c1
                if add_y:
                    c2 = B.op("DVE", lambda e, p=p, nb=nb: e.tensor_add(
                        xn[p][:, tsl(nb, 512)], xn[p][:, tsl(nb, 512)], ys[p][:, tsl(nb, 512)]),
                        deps=[c1, ly], label=f"o_c2{t}_{nb}")
                    combs.append(c2)
                else:
                    combs.append(c1)
            last[("ct", p)] = combs[-1]
            last[("x", p)] = combs[-1]
            if add_y:
                last[("y", p)] = combs[-1]
            st = B.dma("POOL", d[dst][tsl(t, 128), :], xn[p][:], deps=combs, label=f"o_st{t}")
            last[("xn", p)] = st


def phase_moe(B, src, dst):
    nc, d, sb = B.nc, B.d, B.sb
    TRASH = float(E * KPAD)
    ph = contextlib.ExitStack()
    T = lambda n, s, dt=F32: B.sbuf(ph, n, s, dt)
    eidx_sb = T("m_eidx", [128, NT])
    gate_sb = T("m_gate", [128, NT])
    gk_sb = T("m_gk", [128, NT])
    idx_sb = T("m_idx", [128, NT], U32)
    with ph:
        _moe_m1(B, ph, src, eidx_sb, gate_sb)
        B.fence("m1")
        _moe_m1b(B, ph)
        B.fence("m1b")
        _moe_m2(B, ph, eidx_sb, gate_sb, gk_sb, idx_sb)
        B.fence("m2")
        _moe_m3(B, ph)
        B.fence("m3")
        _moe_m4(B, ph, src, dst, gk_sb, idx_sb)


def _moe_m1(B, ph, src, eidx_sb, gate_sb):
    nc, d, sb = B.nc, B.d, B.sb
    T = lambda n, s, dt=F32: B.sbuf(ph, n, s, dt)
    p2 = contextlib.ExitStack()
    T2 = lambda n, s, dt=F32: B.sbuf(p2, n, s, dt)
    xsb = [T2(f"m1x{i}", [128, D]) for i in range(2)]
    ysb = [T2(f"m1y{i}", [128, D]) for i in range(2)]
    ybf = [T2(f"m1yb{i}", [128, D], BF16) for i in range(2)]
    ylo = [T2(f"m1yl{i}", [128, D]) for i in range(2)]
    ylob = [T2(f"m1ylb{i}", [128, D], BF16) for i in range(2)]
    yT = [T2(f"m1yT{i}", [128, D], BF16) for i in range(2)]
    yloT = [T2(f"m1yLT{i}", [128, D], BF16) for i in range(2)]
    sc1 = [T2(f"m1s1{i}", [128, D]) for i in range(2)]
    sm = [T2(f"m1sm{i}", [128, 48]) for i in range(2)]
    oh_sb = T2("m1oh", [128, TPS * E], BF16)
    wg_hi = T2("m1wgh", [128, DC * E], BF16)
    wg_lo = T2("m1wgl", [128, DC * E], BF16)
    wg_f = T2("m1wgf", [128, DC * E])
    pT = [B.psum(p2, f"m1pT{i}", [128, 1024], BF16) for i in range(2)]
    pZ = B.psum(p2, "m1pZ", [128, E], F32)
    pC = B.psum(p2, "m1pC", [1, E], F32)
    with p2:
        wgl = B.dma("SP", wg_f[:].rearrange("p (c n) -> p c n", c=DC),
                    d["Wg"].rearrange("(c dd) n -> dd c n", dd=128), label="m1_wg")
        h1 = B.op("DVE", lambda e: e.tensor_copy(wg_hi[:], wg_f[:]), deps=[wgl], label="m1_wgh")
        h2 = B.op("DVE", lambda e: e.tensor_sub(wg_f[:], wg_f[:], wg_hi[:]),
                  deps=[h1], label="m1_wgs")
        h3 = B.op("DVE", lambda e: e.tensor_copy(wg_lo[:], wg_f[:]), deps=[h2], label="m1_wgl")
        last = {}
        zero_st = []
        for zt in range(DISP_ROWS // 128):
            o = B.dma("POOL", d["disp_d"][tsl(zt, 128), :], sb["zs"][:], label=f"m1_z{zt}")
            zero_st.append(o)
        for t in range(NT):
            p = t % 2
            slot, tin = t // TPS, t % TPS
            lx = B.dma("SP", xsb[p][:], d[src][tsl(t, 128), :],
                       deps=[last.get(("x", p))], label=f"m1_lx{t}")
            yop = ln_tile(B, sb, xsb[p][:], ysb[p][:], sc1[p][:],
                          sm[p][:, 1:2], sm[p][:, 3:4], sm[p][:, 4:5],
                          sm[p][:, 0:1], sm[p][:, 2:3], lx,
                          y_war=last.get(("y", p)), label=f"m1ln{t}")
            last[("x", p)] = yop
            cop = B.op("ACT", lambda e, p=p: e.copy(ybf[p][:], ysb[p][:]),
                       deps=[yop, last.get(("ybf", p))], label=f"m1c{t}")
            lo1 = B.op("DVE", lambda e, p=p: e.tensor_sub(ylo[p][:], ysb[p][:], ybf[p][:]),
                       deps=[cop, last.get(("ylo", p))], label=f"m1lo{t}")
            lo2 = B.op("ACT", lambda e, p=p: e.copy(ylob[p][:], ylo[p][:]),
                       deps=[lo1, last.get(("ylob", p))], label=f"m1lo2{t}")
            last[("ylo", p)] = lo2
            last[("y", p)] = lo1
            sx = B.dma("POOL", d["xnorm_d"][tsl(t, 128), :], ybf[p][:],
                       deps=[cop], label=f"m1sx{t}")
            trs = []
            for half in range(2):
                g = tr_group(B, [(pT[half][:, tsl(i, 128)], ybf[p][:, tsl(half * 8 + i, 128)])
                                 for i in range(8)], sb["ident"][:],
                             deps=[cop, last.get(("pT", half))], label=f"m1tr{t}_{half}")
                ev = B.op("DVE", lambda e, p=p, half=half: e.tensor_copy(
                    yT[p][:, tsl(half, 1024)], pT[half][:]),
                    deps=[g, last.get(("yT", p))], label=f"m1ev{t}_{half}")
                last[("pT", half)] = ev
                trs.append(ev)
            last[("ybf", p)] = sx
            sy = B.dma("POOL", d["yT_d"][t].rearrange("(c dd) n -> dd c n", dd=128),
                       yT[p][:].rearrange("p (c n) -> p c n", c=DC),
                       deps=trs, label=f"m1sy{t}")
            trl = []
            for half in range(2):
                g = tr_group(B, [(pT[half][:, tsl(i, 128)], ylob[p][:, tsl(half * 8 + i, 128)])
                                 for i in range(8)], sb["ident"][:],
                             deps=[lo2, last.get(("pT", half))], label=f"m1trl{t}_{half}")
                ev = B.op("DVE", lambda e, p=p, half=half: e.tensor_copy(
                    yloT[p][:, tsl(half, 1024)], pT[half][:]),
                    deps=[g, last.get(("yloT", p))], label=f"m1evl{t}_{half}")
                last[("pT", half)] = ev
                last[("ylob", p)] = g
                trl.append(ev)
            pairs = ([(yT[p][:, tsl(c, 128)], wg_hi[:, tsl(c, E)]) for c in range(DC)] +
                     [(yT[p][:, tsl(c, 128)], wg_lo[:, tsl(c, E)]) for c in range(DC)] +
                     [(yloT[p][:, tsl(c, 128)], wg_hi[:, tsl(c, E)]) for c in range(DC)])
            zg = mm_group(B, pZ[:], pairs,
                          deps=trs + trl + [h1, h3, last.get("pZ")], label=f"m1z{t}")
            last[("yT", p)] = zg
            last[("yloT", p)] = zg
            zev = B.op("DVE", lambda e, p=p: e.tensor_copy(sm[p][:, 8:16], pZ[:]),
                       deps=[zg], label=f"m1ze{t}")
            last["pZ"] = zev
            mx = B.op("DVE", lambda e, p=p: e.reduce_max(sm[p][:, 16:17], sm[p][:, 8:16], axis=AX.X),
                      deps=[zev], sdeps=[zev], label=f"m1mx{t}")
            ng = B.op("DVE", lambda e, p=p: e.tensor_scalar(
                out=sm[p][:, 17:18], in0=sm[p][:, 16:17], scalar1=-1.0, scalar2=None, op0=ALU.mult),
                deps=[mx], sdeps=[mx], label=f"m1ng{t}")
            ex = B.op("ACT", lambda e, p=p: e.activation(
                sm[p][:, 24:32], sm[p][:, 8:16], AF.Exp, bias=sm[p][:, 17:18],
                accum_out=sm[p][:, 18:19]),
                deps=[ng], label=f"m1ex{t}")
            gt = B.op("DVE", lambda e, p=p, t=t: e.reciprocal(gate_sb[:, t:t + 1], sm[p][:, 18:19]),
                      deps=[ex], sdeps=[ex], label=f"m1g{t}")
            oh = B.op("DVE", lambda e, p=p, tin=tin: e.tensor_scalar(
                out=oh_sb[:, tsl(tin, E)], in0=sm[p][:, 8:16], scalar1=sm[p][:, 16:17],
                scalar2=None, op0=ALU.is_equal),
                deps=[mx, last.get(("oh", tin))], sdeps=[mx], label=f"m1oh{t}")
            ei = B.op("DVE", lambda e, p=p, tin=tin, t=t: e.scalar_tensor_tensor(
                out=sm[p][:, 24:32], in0=oh_sb[:, tsl(tin, E)], scalar=1.0, in1=sb["iota8"][:],
                op0=ALU.bypass, op1=ALU.mult, accum_out=eidx_sb[:, t:t + 1]),
                deps=[oh, ex], sdeps=[oh], label=f"m1ei{t}")
            pass
            cg = B.op("PE", lambda e, tin=tin: e.matmul(
                pC[:], sb["onesc"][:], oh_sb[:, tsl(tin, E)],
                start=(tin == 0), stop=(tin == TPS - 1)),
                deps=[oh, last.get("pC")], label=f"m1cg{t}")
            last[("oh", tin)] = cg
            if tin == TPS - 1:
                cev = B.op("DVE", lambda e, p=p: e.tensor_copy(sm[p][0:1, 32:40], pC[:]),
                           deps=[cg], label=f"m1ce{t}")
                last["pC"] = cev
                stc = B.dma("POOL", d["counts_in"][slot:slot + 1, :], sm[p][0:1, 32:40],
                            deps=[cev], label=f"m1stc{t}")
                last[("cnt", p)] = stc
            else:
                last["pC"] = cg


def _moe_m1b(B, ph):
    nc, d, sb = B.nc, B.d, B.sb
    p2 = contextlib.ExitStack()
    T2 = lambda n, s, dt=F32: B.sbuf(p2, n, s, dt)
    cag = T2("mbc", [64, E])
    sel = T2("mbs", [64, SLOTS])
    own = T2("mbo", [SLOTS, E])
    lts = T2("mbl", [SLOTS, SLOTS])
    bg = T2("mbg", [SLOTS, E])
    pB_ = B.psum(p2, "mbp", [SLOTS, E], F32)
    with p2:
        cc = B.collective("AllGather", ALU.bypass, [list(range(NCORE))],
                          d["counts_in"], d["counts_ag"], label="m1b_ag")
        l1 = B.dma("SP", cag[:], d["counts_ag"], deps=[cc], label="m1b_l1")
        l2 = B.dma("SP", sel[:], d["selT"], label="m1b_l2")
        l3 = B.dma("SP", own[:], d["counts_in"], deps=[cc], label="m1b_l3")
        l4 = B.dma("SP", lts[:], d["lts8"], label="m1b_l4")
        g1 = mm_group(B, pB_[:], [(sel[:], cag[:])], deps=[l1, l2], label="m1b_g1")
        e1 = B.op("DVE", lambda e: e.tensor_copy(bg[:], pB_[:]), deps=[g1], label="m1b_e1")
        s1 = B.dma("POOL", d["base_d"][0:SLOTS, :], bg[:], deps=[e1], label="m1b_s1")
        g2 = mm_group(B, pB_[:], [(lts[:], own[:])], deps=[l3, l4, e1], label="m1b_g2")
        e2 = B.op("DVE", lambda e: e.tensor_copy(bg[:], pB_[:]), deps=[g2, s1], label="m1b_e2")
        B.dma("POOL", d["base_d"][SLOTS:2 * SLOTS, :], bg[:], deps=[e2], label="m1b_s2")


def _moe_m2(B, ph, eidx_sb, gate_sb, gk_sb, idx_sb):
    nc, d, sb = B.nc, B.d, B.sb
    TRASH = float(E * KPAD)
    p2 = contextlib.ExitStack()
    T2 = lambda n, s, dt=F32: B.sbuf(p2, n, s, dt)
    bsg = T2("m2bg", [128, E])
    bsl = T2("m2bl", [128, E])
    oh = [T2(f"m2oh{i}", [128, E], BF16) for i in range(TPS)]
    smv = [T2(f"m2sm{i}", [128, 32]) for i in range(2)]
    xnb = [T2(f"m2xn{i}", [128, D], BF16) for i in range(2)]
    pU = B.psum(p2, "m2pU", [128, E], F32)
    pV_ = B.psum(p2, "m2pV", [128, E], F32)
    with p2:
        last = {}
        for slot in range(SLOTS):
            lg = B.dma("SP", bsg[:], d["base_d"][slot:slot + 1, :].broadcast_to([128, E]),
                       deps=[last.get("baseg")], label=f"m2lg{slot}")
            ll = B.dma("SP", bsl[:], d["base_d"][SLOTS + slot:SLOTS + slot + 1, :].broadcast_to([128, E]),
                       deps=[last.get("basel")], label=f"m2ll{slot}")
            ohops = []
            for tin in range(TPS):
                t = slot * TPS + tin
                o = B.op("DVE", lambda e, tin=tin, t=t: e.tensor_scalar(
                    out=oh[tin][:], in0=sb["iota8"][:], scalar1=eidx_sb[:, t:t + 1],
                    scalar2=None, op0=ALU.is_equal),
                    deps=[last.get(("oh", tin))], label=f"m2oh{t}")
                ohops.append(o)
            for tin in range(TPS):
                t = slot * TPS + tin
                p = tin % 2

                def emit(e, tin=tin, ps_=None):
                    last_i = None
                    last_i = e.matmul(ps_[:], sb["lt128"][:], oh[tin][:],
                                      start=True, stop=(tin == 0))
                    for q in range(tin):
                        last_i = e.matmul(ps_[:], sb["on128"][:], oh[q][:],
                                          start=False, stop=(q == tin - 1))
                    return last_i
                gg = B.op("PE", lambda e, tin=tin: emit(e, tin=tin, ps_=pU),
                          deps=ohops[:tin + 1] + [last.get(("ps", "g"))],
                          label=f"m2pfg{t}")
                gl = B.op("PE", lambda e, tin=tin: emit(e, tin=tin, ps_=pV_),
                          deps=ohops[:tin + 1] + [last.get(("ps", "l"))],
                          label=f"m2pfl{t}")
                tg = B.op("DVE", lambda e, p=p: e.tensor_add(smv[p][:, 8:16], pU[:], bsg[:]),
                          deps=[gg, lg, last.get(("tmp", p))], label=f"m2tg{t}")
                gpos = B.op("DVE", lambda e, p=p, tin=tin: e.scalar_tensor_tensor(
                    out=smv[p][:, 16:24], in0=oh[tin][:], scalar=1.0, in1=smv[p][:, 8:16],
                    op0=ALU.bypass, op1=ALU.mult, accum_out=smv[p][:, 0:1]),
                    deps=[tg], sdeps=[tg], label=f"m2pos{t}")
                last[("ps", "g")] = tg
                tl = B.op("DVE", lambda e, p=p: e.tensor_add(smv[p][:, 8:16], pV_[:], bsl[:]),
                          deps=[gl, ll, gpos], sdeps=[gpos], label=f"m2tl{t}")
                lpos = B.op("DVE", lambda e, p=p, tin=tin: e.scalar_tensor_tensor(
                    out=smv[p][:, 16:24], in0=oh[tin][:], scalar=1.0, in1=smv[p][:, 8:16],
                    op0=ALU.bypass, op1=ALU.mult, accum_out=smv[p][:, 1:2]),
                    deps=[tl], sdeps=[tl, gpos], label=f"m2lr{t}")
                last[("ps", "l")] = tl
                last[("tmp", p)] = lpos
                last["lastgl"] = gl
                last["baseg"] = tg
                last["basel"] = tl
                kp = B.op("DVE", lambda e, p=p: e.tensor_scalar(
                    out=smv[p][:, 2:3], in0=smv[p][:, 0:1], scalar1=float(CAP),
                    scalar2=None, op0=ALU.is_lt),
                    deps=[gpos], sdeps=[gpos], label=f"m2kp{t}")
                gk = B.op("DVE", lambda e, p=p, t=t: e.tensor_tensor(
                    out=gk_sb[:, t:t + 1], in0=gate_sb[:, t:t + 1], in1=smv[p][:, 2:3],
                    op=ALU.mult),
                    deps=[kp], sdeps=[kp], label=f"m2gk{t}")
                a1 = B.op("DVE", lambda e, p=p, t=t: e.scalar_tensor_tensor(
                    out=smv[p][:, 3:4], in0=eidx_sb[:, t:t + 1], scalar=float(KPAD),
                    in1=smv[p][:, 1:2], op0=ALU.mult, op1=ALU.add),
                    deps=[lpos], sdeps=[lpos], label=f"m2a{t}")
                mk = B.op("DVE", lambda e, p=p: e.tensor_scalar(
                    out=smv[p][:, 4:5], in0=smv[p][:, 1:2], scalar1=float(KPAD),
                    scalar2=None, op0=ALU.is_lt),
                    deps=[lpos], sdeps=[lpos], label=f"m2mk{t}")
                i1 = B.op("DVE", lambda e, p=p: e.tensor_scalar(
                    out=smv[p][:, 5:6], in0=smv[p][:, 3:4], scalar1=-TRASH, scalar2=None,
                    op0=ALU.add),
                    deps=[a1], sdeps=[a1], label=f"m2i1{t}")
                i2 = B.op("DVE", lambda e, p=p: e.tensor_tensor(
                    out=smv[p][:, 5:6], in0=smv[p][:, 5:6], in1=smv[p][:, 4:5], op=ALU.mult),
                    deps=[i1, mk], sdeps=[i1, mk], label=f"m2i2{t}")
                i3 = B.op("DVE", lambda e, p=p: e.tensor_scalar(
                    out=smv[p][:, 5:6], in0=smv[p][:, 5:6], scalar1=TRASH, scalar2=None,
                    op0=ALU.add),
                    deps=[i2], sdeps=[i2], label=f"m2i3{t}")
                iu = B.op("DVE", lambda e, p=p, t=t: e.tensor_copy(idx_sb[:, t:t + 1],
                                                                  smv[p][:, 5:6]),
                          deps=[i3], sdeps=[i3], label=f"m2iu{t}")
                lx = B.dma("SP", xnb[p][:], d["xnorm_d"][tsl(t, 128), :],
                           deps=[last.get(("xnb", p))], label=f"m2lx{t}")
                sc_ = B.Q["POOL"].add("dma", lambda e, p=p, t=t: e.indirect_dma_start(
                    out=d["disp_d"], out_offset=IOA(idx_sb[:, t:t + 1], 0),
                    in_=xnb[p][:], in_offset=None),
                    deps=[iu, lx], label=f"m2sc{t}")
                last[("xnb", p)] = sc_
            for tin in range(TPS):
                last[("oh", tin)] = last["lastgl"]


def _moe_m3(B, ph):
    nc, d, sb = B.nc, B.d, B.sb
    p2 = contextlib.ExitStack()
    T2 = lambda n, s, dt=F32: B.sbuf(p2, n, s, dt)
    xdT = [T2(f"m3x{i}", [128, D], BF16) for i in range(KTILES)]
    xrow = [T2(f"m3r{i}", [128, D], BF16) for i in range(2)]
    w1 = T2("m3w1", [128, DC * 512], BF16)
    w3 = T2("m3w3", [128, DC * 512], BF16)
    w2 = T2("m3w2", [128, HC * D], BF16)
    hsil = [T2(f"m3h{i}", [128, 512], BF16) for i in range(2)]
    htmp = [T2(f"m3ht{i}", [128, 512], BF16) for i in range(2)]
    hT = [T2(f"m3hT{i}", [128, HC * 128], BF16) for i in range(KTILES)]
    oe = [T2(f"m3oe{i}", [128, D], BF16) for i in range(2)]
    pH = [B.psum(p2, f"m3pH{i}", [128, 512], F32) for i in range(2)]
    pTT = [B.psum(p2, f"m3pT{i}", [128, 1024], BF16) for i in range(2)]
    pO = [B.psum(p2, f"m3pO{i}", [128, 512], F32) for i in range(2)]
    with p2:
        last = {}
        for ex in range(E):
            w2l = B.dma("POOL", w2[:].rearrange("p (c n) -> p c n", c=HC),
                        d["W2"][ex].rearrange("(c dd) n -> dd c n", dd=128),
                        deps=[last.get("w2")], label=f"m3w2{ex}")
            for kt in range(KTILES):
                p = kt % 2
                lr = B.dma("SP", xrow[p][:], d["disp_d"][tsl(ex * KTILES + kt, 128), :],
                           deps=[last.get(("xr", p))], label=f"m3lr{ex}_{kt}")
                evs = []
                for half in range(2):
                    g = tr_group(B, [(pTT[half][:, tsl(i, 128)],
                                      xrow[p][:, tsl(half * 8 + i, 128)]) for i in range(8)],
                                 sb["ident"][:],
                                 deps=[lr, last.get(("pTT", half))], label=f"m3tr{ex}_{kt}_{half}")
                    ev = B.op("DVE", lambda e, kt=kt, half=half: e.tensor_copy(
                        xdT[kt][:, tsl(half, 1024)], pTT[half][:]),
                        deps=[g, last.get(("xdT", kt))], label=f"m3te{ex}_{kt}_{half}")
                    last[("pTT", half)] = ev
                    evs.append(ev)
                last[("xr", p)] = evs[-1]
                last[("xdTw", kt)] = evs
            hc0 = 0
            for nh, ncols in enumerate(HCH):
                l1 = B.dma("POOL", w1[:, 0:DC * ncols].rearrange("p (c n) -> p c n", c=DC),
                           d["W1"][ex].rearrange("(c dd) n -> dd c n", dd=128)[:, :, nh * 512: nh * 512 + ncols],
                           deps=[last.get("w1")], label=f"m3l1{ex}_{nh}")
                l3 = B.dma("POOL", w3[:, 0:DC * ncols].rearrange("p (c n) -> p c n", c=DC),
                           d["W3"][ex].rearrange("(c dd) n -> dd c n", dd=128)[:, :, nh * 512: nh * 512 + ncols],
                           deps=[last.get("w3")], label=f"m3l3{ex}_{nh}")
                nturn = ncols // 128
                g1l = g3l = None
                for kt in range(KTILES):
                    hp_ = kt % 2
                    g1 = mm_group(B, pH[0][:, 0:ncols],
                                  [(xdT[kt][:, tsl(c, 128)], w1[:, c * ncols: (c + 1) * ncols])
                                   for c in range(DC)],
                                  deps=last[("xdTw", kt)] + [l1, last.get(("pH", 0))],
                                  label=f"m3g1{ex}_{nh}_{kt}")
                    g3 = mm_group(B, pH[1][:, 0:ncols],
                                  [(xdT[kt][:, tsl(c, 128)], w3[:, c * ncols: (c + 1) * ncols])
                                   for c in range(DC)],
                                  deps=last[("xdTw", kt)] + [l3, last.get(("pH", 1))],
                                  label=f"m3g3{ex}_{nh}_{kt}")
                    sil = B.op("ACT", lambda e, hp_=hp_, ncols=ncols: e.activation(
                        hsil[hp_][:, 0:ncols], pH[0][:, 0:ncols], AF.Silu),
                        deps=[g1, last.get(("hsil", hp_))], label=f"m3si{ex}_{nh}_{kt}")
                    hp = B.op("DVE", lambda e, hp_=hp_, ncols=ncols: e.tensor_tensor(
                        out=htmp[hp_][:, 0:ncols], in0=pH[1][:, 0:ncols],
                        in1=hsil[hp_][:, 0:ncols], op=ALU.mult),
                        deps=[g3, sil, last.get(("htmp", hp_))], label=f"m3hp{ex}_{nh}_{kt}")
                    last[("pH", 0)] = hp
                    last[("pH", 1)] = hp
                    last[("hsil", hp_)] = hp
                    g = tr_group(B, [(pTT[hp_][:, tsl(i, 128)], htmp[hp_][:, tsl(i, 128)])
                                     for i in range(nturn)],
                                 sb["ident"][:],
                                 deps=[hp, last.get(("pTT", hp_))], label=f"m3trh{ex}_{nh}_{kt}")
                    ev = B.op("DVE", lambda e, kt=kt, hc0=hc0, nturn=nturn, hp_=hp_: e.tensor_copy(
                        hT[kt][:, hc0 * 128:(hc0 + nturn) * 128], pTT[hp_][:, 0:nturn * 128]),
                        deps=[g, last.get(("hT", kt))], label=f"m3the{ex}_{nh}_{kt}")
                    last[("pTT", hp_)] = ev
                    last[("htmp", hp_)] = g
                    last.setdefault(("hTw", kt), []).append(ev)
                last["w1"] = g1
                last["w3"] = g3
                hc0 += nturn
            for kt in range(KTILES):
                p = kt % 2
                oevs = []
                for n2 in range(4):
                    pb = n2 % 2
                    gO = mm_group(B, pO[pb][:],
                                  [(hT[kt][:, tsl(hc, 128)], w2[:, hc * D + n2 * 512: hc * D + n2 * 512 + 512])
                                   for hc in range(HC)],
                                  deps=last[("hTw", kt)] + [w2l, last.get(("pO", pb))],
                                  label=f"m3go{ex}_{kt}_{n2}")
                    ev = B.op("DVE", lambda e, p=p, n2=n2, pb=pb: e.tensor_copy(
                        oe[p][:, tsl(n2, 512)], pO[pb][:]),
                        deps=[gO, last.get(("oe", p))], label=f"m3oe{ex}_{kt}_{n2}")
                    last[("pO", pb)] = ev
                    oevs.append(ev)
                st = B.dma("POOL", d["dispo_d"][tsl(ex * KTILES + kt, 128), :], oe[p][:],
                           deps=oevs, label=f"m3st{ex}_{kt}")
                last[("oe", p)] = st
                last[("hT", kt)] = gO
            last["w2"] = gO
            for kt in range(KTILES):
                last[("xdT", kt)] = last[("hTw", kt)][-1]
                del last[("hTw", kt)]
        B.dma("POOL", d["dispo_d"][E * KPAD:E * KPAD + 128, :], sb["zs"][:],
              deps=[], label="m3ztrash")


def _moe_m4(B, ph, src, dst, gk_sb, idx_sb):
    nc, d, sb = B.nc, B.d, B.sb
    p2 = contextlib.ExitStack()
    T2 = lambda n, s, dt=F32: B.sbuf(p2, n, s, dt)
    yTt = [T2(f"m4y{i}", [128, D], BF16) for i in range(2)]
    w1 = T2("m4w1", [128, DC * 512], BF16)
    w3 = T2("m4w3", [128, DC * 512], BF16)
    w2 = T2("m4w2", [128, HC * D], BF16)
    hsil = [T2(f"m4h{i}", [128, 512], BF16) for i in range(2)]
    htmp = [T2(f"m4ht{i}", [128, 512], BF16) for i in range(2)]
    hTt = [T2(f"m4hT{i}", [128, HC * 128], BF16) for i in range(2)]
    mo = [T2(f"m4mo{i}", [128, D], BF16) for i in range(2)]
    xs = [T2(f"m4x{i}", [128, D]) for i in range(2)]
    xn = [T2(f"m4xn{i}", [128, D]) for i in range(2)]
    pH = [B.psum(p2, f"m4pH{i}", [128, 512], F32) for i in range(2)]
    pTT = [B.psum(p2, f"m4pT{i}", [128, 1024], BF16) for i in range(2)]
    pO = [B.psum(p2, f"m4pO{i}", [128, 512], F32) for i in range(2)]
    with p2:
        last = {}
        hc0 = 0
        for nh, ncols in enumerate(HCH):
            nturn = ncols // 128
            l1 = B.dma("POOL", w1[:, 0:DC * ncols].rearrange("p (c n) -> p c n", c=DC),
                       d["sw1"].rearrange("(c dd) n -> dd c n", dd=128)[:, :, nh * 512: nh * 512 + ncols],
                       deps=[last.get("w1")], label=f"m4l1{nh}")
            l3 = B.dma("POOL", w3[:, 0:DC * ncols].rearrange("p (c n) -> p c n", c=DC),
                       d["sw3"].rearrange("(c dd) n -> dd c n", dd=128)[:, :, nh * 512: nh * 512 + ncols],
                       deps=[last.get("w3")], label=f"m4l3{nh}")
            for t in range(NT):
                p = t % 2
                ly = B.dma("SP", yTt[p][:].rearrange("p (c n) -> p c n", c=DC),
                           d["yT_d"][t].rearrange("(c dd) n -> dd c n", dd=128),
                           deps=[last.get(("yT", p))], label=f"m4ly{nh}_{t}")
                g1 = mm_group(B, pH[0][:, 0:ncols],
                              [(yTt[p][:, tsl(c, 128)], w1[:, c * ncols:(c + 1) * ncols])
                               for c in range(DC)],
                              deps=[ly, l1, last.get(("pH", 0))], label=f"m4g1{nh}_{t}")
                g3 = mm_group(B, pH[1][:, 0:ncols],
                              [(yTt[p][:, tsl(c, 128)], w3[:, c * ncols:(c + 1) * ncols])
                               for c in range(DC)],
                              deps=[ly, l3, last.get(("pH", 1))], label=f"m4g3{nh}_{t}")
                last[("yT", p)] = g3
                sil = B.op("ACT", lambda e, p=p, ncols=ncols: e.activation(
                    hsil[p][:, 0:ncols], pH[0][:, 0:ncols], AF.Silu),
                    deps=[g1, last.get(("hsil", p))], label=f"m4si{nh}_{t}")
                hp = B.op("DVE", lambda e, p=p, ncols=ncols: e.tensor_tensor(
                    out=htmp[p][:, 0:ncols], in0=pH[1][:, 0:ncols],
                    in1=hsil[p][:, 0:ncols], op=ALU.mult),
                    deps=[g3, sil, last.get(("htmp", p))], label=f"m4hp{nh}_{t}")
                last[("pH", 0)] = hp
                last[("pH", 1)] = hp
                last[("hsil", p)] = hp
                g = tr_group(B, [(pTT[p][:, tsl(i, 128)], htmp[p][:, tsl(i, 128)])
                                 for i in range(nturn)],
                             sb["ident"][:],
                             deps=[hp, last.get(("pTT", p))], label=f"m4trh{nh}_{t}")
                ev = B.op("DVE", lambda e, p=p, nturn=nturn: e.tensor_copy(
                    hTt[p][:, 0:nturn * 128], pTT[p][:, 0:nturn * 128]),
                    deps=[g, last.get(("hTt", p))], label=f"m4the{nh}_{t}")
                last[("pTT", p)] = ev
                last[("htmp", p)] = g
                sh = B.dma("POOL", d["hT_d"][t][hc0 * 128:(hc0 + nturn) * 128, :]
                           .rearrange("(c dd) n -> dd c n", dd=128),
                           hTt[p][:, 0:nturn * 128].rearrange("p (c n) -> p c n", c=nturn),
                           deps=[ev], label=f"m4sh{nh}_{t}")
                last[("hTt", p)] = sh
            last["w1"] = g1
            last["w3"] = g3
            hc0 += nturn
        w2l = B.dma("POOL", w2[:].rearrange("p (c n) -> p c n", c=HC),
                    d["sw2"].rearrange("(c dd) n -> dd c n", dd=128),
                    label="m4w2")
        hTf = [T2(f"m4hf{i}", [128, HC * 128], BF16) for i in range(2)]
        for t in range(NT):
            p = t % 2
            lh = B.dma("SP", hTf[p][:].rearrange("p (c n) -> p c n", c=HC),
                       d["hT_d"][t].rearrange("(c dd) n -> dd c n", dd=128),
                       deps=[last.get(("hTf", p))], label=f"m4lh{t}")
            lx = B.dma("SP", xs[p][:], d[src][tsl(t, 128), :],
                       deps=[last.get(("x", p))], label=f"m4lx{t}")
            lmo = B.Q["POOL"].add("dma", lambda e, p=p, t=t: e.indirect_dma_start(
                out=mo[p][:], out_offset=None,
                in_=d["dispo_d"], in_offset=IOA(idx_sb[:, t:t + 1], 0)),
                deps=[last.get(("mo", p))], label=f"m4gm{t}")
            combs = []
            for n2 in range(4):
                pb = n2 % 2
                gO = mm_group(B, pO[pb][:],
                              [(hTf[p][:, tsl(hc, 128)], w2[:, hc * D + n2 * 512: hc * D + n2 * 512 + 512])
                               for hc in range(HC)],
                              deps=[lh, w2l, last.get(("pO", pb))], label=f"m4go{t}_{n2}")
                c1 = B.op("DVE", lambda e, p=p, n2=n2, pb=pb, t=t: e.scalar_tensor_tensor(
                    out=xn[p][:, tsl(n2, 512)], in0=mo[p][:, tsl(n2, 512)],
                    scalar=gk_sb[:, t:t + 1], in1=pO[pb][:], op0=ALU.mult, op1=ALU.add),
                    deps=[gO, lmo, last.get(("xn", p))], label=f"m4c1{t}_{n2}")
                c2 = B.op("DVE", lambda e, p=p, n2=n2: e.tensor_add(
                    xn[p][:, tsl(n2, 512)], xn[p][:, tsl(n2, 512)], xs[p][:, tsl(n2, 512)]),
                    deps=[c1, lx], sdeps=[c1], label=f"m4c2{t}_{n2}")
                last[("pO", pb)] = c2
                combs.append(c2)
            last[("hTf", p)] = gO
            last[("x", p)] = combs[-1]
            last[("mo", p)] = combs[-1]
            st = B.dma("POOL", d[dst][tsl(t, 128), :], xn[p][:], deps=combs, label=f"m4st{t}")
            last[("xn", p)] = st


def phase_b1(B):
    nc, d, sb = B.nc, B.d, B.sb
    p2 = contextlib.ExitStack()
    T2 = lambda n, s, dt=F32: B.sbuf(p2, n, s, dt)
    xsb = [T2(f"b1x{i}", [128, D]) for i in range(2)]
    ysb = [T2(f"b1y{i}", [128, D]) for i in range(2)]
    xnb = [T2(f"b1n{i}", [128, D]) for i in range(2)]
    xbf = [T2(f"b1b{i}", [128, D], BF16) for i in range(2)]
    sc1 = [T2(f"b1s1{i}", [128, D]) for i in range(2)]
    sm = [T2(f"b1sm{i}", [128, 16]) for i in range(2)]
    gam = T2("b1g", [128, D])
    bet = T2("b1be", [128, D])
    xnT = T2("b1xT", [128, DC * 2048], BF16)
    wsb = [T2(f"b1w{i}", [128, DC * 512], BF16) for i in range(2)]
    ev_sb = [T2(f"b1qe{i}", [128, 512], BF16) for i in range(2)]
    pT = [B.psum(p2, f"b1pT{i}", [128, 1024], BF16) for i in range(2)]
    pP = [B.psum(p2, f"b1pP{i}", [128, 512], F32) for i in range(2)]
    with p2:
        lgm = B.dma("SP", gam[:], d["gammaI"].broadcast_to([128, D]), label="b1lg")
        lbt = B.dma("SP", bet[:], d["betaI"].broadcast_to([128, D]), label="b1lb")
        last = {}
        B.kb_cc, B.vb_cc = {}, {}
        for g2 in range(2):
            tevs = []
            for ti in range(16):
                t = g2 * 16 + ti
                p = t % 2
                lx = B.dma("SP", xsb[p][:], d["resid"][tsl(t, 128), :],
                           deps=[last.get(("x", p))], label=f"b1lx{t}")
                yop = ln_tile(B, sb, xsb[p][:], ysb[p][:], sc1[p][:],
                              sm[p][:, 1:2], sm[p][:, 3:4], sm[p][:, 4:5],
                              sm[p][:, 0:1], sm[p][:, 2:3], lx,
                              y_war=last.get(("y", p)), label=f"b1ln{t}")
                last[("x", p)] = yop
                sy = B.dma("POOL", d["y_d"][tsl(t, 128), :], ysb[p][:],
                           deps=[yop], label=f"b1sy{t}")
                yop2 = ln_tile(B, sb, ysb[p][:], xnb[p][:], sc1[p][:],
                               sm[p][:, 5:6], sm[p][:, 6:7], sm[p][:, 7:8],
                               sm[p][:, 8:9], sm[p][:, 9:10], yop,
                               y_war=last.get(("xnb", p)), scr_war=yop,
                               label=f"b1li{t}")
                last[("y", p)] = [yop2, sy]
                ga = B.op("DVE", lambda e, p=p: e.tensor_mul(xnb[p][:], xnb[p][:], gam[:]),
                          deps=[yop2, lgm], sdeps=[yop2], label=f"b1ga{t}")
                be = B.op("DVE", lambda e, p=p: e.tensor_add(xnb[p][:], xnb[p][:], bet[:]),
                          deps=[ga, lbt], sdeps=[ga], label=f"b1be{t}")
                cop = B.op("ACT", lambda e, p=p: e.copy(xbf[p][:], xnb[p][:]),
                           deps=[be, last.get(("xbf", p))], label=f"b1c{t}")
                last[("xnb", p)] = cop
                for half in range(2):
                    g = tr_group(B, [(pT[half][:, tsl(i, 128)],
                                      xbf[p][:, tsl(half * 8 + i, 128)]) for i in range(8)],
                                 sb["ident"][:],
                                 deps=[cop, last.get(("pT", half))], label=f"b1tr{t}_{half}")
                    ev = B.op("DVE", lambda e, ti=ti, half=half: e.tensor_copy(
                        xnT[:].rearrange("p (c n) -> p c n", c=DC)
                        [:, tsl(half, 8), ti * 128: ti * 128 + 128],
                        pT[half][:].rearrange("p (c n) -> p c n", c=8)),
                        deps=[g, last.get("xnTw")], label=f"b1ev{t}_{half}")
                    last[("pT", half)] = ev
                    last[("xbf", p)] = g
                    tevs.append(ev)
            wlast = None
            for slab in range(12):
                wp = slab % 2
                wl = B.dma("POOL", wsb[wp][:].rearrange("p (c n) -> p c n", c=DC),
                           d["qkvw"].rearrange("(c dd) n -> dd c n", dd=128)[:, :, tsl(slab, 512)],
                           deps=[last.get(("w", wp))], label=f"b1w{g2}_{slab}")
                if slab < 8:
                    for ob in range(4):
                        for tsl_ in range(4):
                            pb = (ob * 4 + tsl_) % 2
                            grp = mm_group(B, pP[pb][:],
                                           [(wsb[wp][:, c * 512 + ob * 128: c * 512 + ob * 128 + 128],
                                             xnT[:, c * 2048 + tsl_ * 512: c * 2048 + tsl_ * 512 + 512])
                                            for c in range(DC)],
                                           deps=tevs + [wl, last.get(("pP", pb))],
                                           label=f"b1g{g2}_{slab}_{ob}_{tsl_}")
                            ev = B.op("DVE", lambda e, pb=pb: e.tensor_copy(
                                ev_sb[pb][:], pP[pb][:]),
                                deps=[grp, last.get(("evs", pb))], label=f"b1pe{g2}_{slab}_{ob}_{tsl_}")
                            last[("pP", pb)] = ev
                            o0 = slab * 512 + ob * 128 if slab < 4 else (slab - 4) * 512 + ob * 128
                            if slab < 4:
                                sts = []
                                for i in range(4):
                                    tile = g2 * 16 + tsl_ * 4 + i
                                    st = B.dma("POOL",
                                               d["QTb"][tile][o0:o0 + 128, :],
                                               ev_sb[pb][:, tsl(i, 128)],
                                               deps=[ev], label=f"b1sq{g2}_{slab}_{ob}_{tsl_}_{i}")
                                    sts.append(st)
                                last[("evs", pb)] = sts[-1]
                            else:
                                blk = g2 * 2 + tsl_ // 2
                                col0 = (tsl_ % 2) * 512
                                st = B.dma("POOL", d[f"kb_in{blk}"][o0:o0 + 128, col0:col0 + 512],
                                           ev_sb[pb][:], deps=[ev],
                                           label=f"b1sk{g2}_{slab}_{ob}_{tsl_}")
                                last[("evs", pb)] = st
                                last.setdefault(("kst", blk), []).append(st)
                else:
                    for ti in range(16):
                        pb = ti % 2
                        grp = mm_group(B, pP[pb][:],
                                       [(xnT[:, c * 2048 + ti * 128: c * 2048 + ti * 128 + 128],
                                         wsb[wp][:, tsl(c, 512)])
                                        for c in range(DC)],
                                       deps=tevs + [wl, last.get(("pP", pb))],
                                       label=f"b1v{g2}_{slab}_{ti}")
                        ev = B.op("DVE", lambda e, pb=pb: e.tensor_copy(ev_sb[pb][:], pP[pb][:]),
                                  deps=[grp, last.get(("evs", pb))], label=f"b1ve{g2}_{slab}_{ti}")
                        last[("pP", pb)] = ev
                        blk = g2 * 2 + ti // 8
                        r0 = (ti % 8) * 128
                        st = B.dma("POOL", d[f"vb_in{blk}"][r0:r0 + 128, tsl(slab - 8, 512)],
                                   ev_sb[pb][:], deps=[ev], label=f"b1sv{g2}_{slab}_{ti}")
                        last[("evs", pb)] = st
                        last.setdefault(("vst", blk), []).append(st)
                last[("w", wp)] = grp
                wlast = grp
            last["xnTw"] = wlast
            for blk in (g2 * 2, g2 * 2 + 1):
                B.kb_cc[blk] = B.collective("AllGather", ALU.bypass,
                                            [[0, 1], [2, 3], [4, 5], [6, 7]],
                                            d[f"kb_in{blk}"], d[f"kbag{blk}"],
                                            deps=last[("kst", blk)], label=f"b1agk{blk}")
                B.vb_cc[blk] = B.collective("AllGather", ALU.bypass,
                                            [[0, 1], [2, 3], [4, 5], [6, 7]],
                                            d[f"vb_in{blk}"], d[f"vbag{blk}"],
                                            deps=last[("vst", blk)], label=f"b1agv{blk}")
                if f"kbag{blk}_dbg" in d:
                    B.dma("POOL", d[f"kbag{blk}_dbg"], d[f"kbag{blk}"],
                          deps=[B.kb_cc[blk]], label=f"dbgk{blk}")
                if f"vbag{blk}_dbg" in d:
                    B.dma("POOL", d[f"vbag{blk}_dbg"], d[f"vbag{blk}"],
                          deps=[B.vb_cc[blk]], label=f"dbgv{blk}")


_NC_CACHE = {}


def _get_nc(upto):
    if upto not in _NC_CACHE:
        _NC_CACHE[upto] = build_nc(upto)
    return _NC_CACHE[upto]


def _host_constants():
    c = {}
    c["ident"] = np.eye(128, dtype=ml_dtypes.bfloat16)
    c["lt128"] = np.triu(np.ones((128, 128), np.float32), k=1)
    c["ones128"] = np.ones((128, 128), np.float32)
    c["lts8"] = np.triu(np.ones((SLOTS, SLOTS), np.float32), k=1).T.copy()
    c["lts8"] = np.triu(np.ones((SLOTS, SLOTS), np.float32), k=1)
    return c


def _core_constants(core):
    b, par = core // 2, core % 2
    cset = SETS[par]
    pos = np.concatenate([np.arange(c * CH, (c + 1) * CH) for c in cset]).astype(np.float64)
    idx = np.arange(HD // 2, dtype=np.float64)
    inv_freq = 1.0 / ROPE_THETA ** (2.0 * idx / HD)
    fr = pos[:, None] * inv_freq[None, :]
    cosT = np.cos(fr).astype(np.float32)
    sinT = np.sin(fr).astype(np.float32)
    mask = np.zeros((SLOTS * CH, 1024), np.float32)
    for j in range(SLOTS):
        cg = cset[j]
        bound = (j + 1) * 1024
        k_global = np.arange(bound - 1024, bound)
        q_global = cg * CH + np.arange(CH)
        mask[j * CH:(j + 1) * CH, :] = (k_global[None, :] <= q_global[:, None])
    sel = np.zeros((NCORE * SLOTS, SLOTS), np.float32)
    for r in range(NCORE):
        for jj in range(SLOTS):
            gc_r = (r // 2) * NCH + SETS[r % 2][jj]
            for j in range(SLOTS):
                gc_me = b * NCH + cset[j]
                if gc_r < gc_me:
                    sel[r * SLOTS + jj, j] = 1.0
    return dict(cosT=cosT, sinT=sinT, maskT=mask, selT=sel), cset


def prep_inmaps(L0_attn_a_x, L0_attn_a_q_proj, L0_attn_a_k_proj, L0_attn_a_v_proj,
                L0_attn_a_o_proj, L0_attn_a_q_norm, L0_attn_a_k_norm,
                L0_attn_b_qkv, L0_attn_b_o_proj, L0_attn_b_gamma, L0_attn_b_beta,
                L0_ffn_Wg, L0_ffn_W1, L0_ffn_W3, L0_ffn_W2,
                L0_ffn_sw1, L0_ffn_sw3, L0_ffn_sw2):
    cst = _host_constants()
    asc = np.ascontiguousarray
    shared = {
        "qTw": asc(np.asarray(L0_attn_a_q_proj).T),
        "kTw": asc(np.asarray(L0_attn_a_k_proj).T),
        "vTw": asc(np.asarray(L0_attn_a_v_proj).T),
        "oTw": asc(np.asarray(L0_attn_a_o_proj).T),
        "qkvw": asc(np.asarray(L0_attn_b_qkv)),
        "obw": asc(np.asarray(L0_attn_b_o_proj)),
        "Wg": asc(np.asarray(L0_ffn_Wg)),
        "W1": asc(np.asarray(L0_ffn_W1)),
        "W3": asc(np.asarray(L0_ffn_W3)),
        "W2": asc(np.asarray(L0_ffn_W2)),
        "sw1": asc(np.asarray(L0_ffn_sw1)),
        "sw3": asc(np.asarray(L0_ffn_sw3)),
        "sw2": asc(np.asarray(L0_ffn_sw2)),
        "qgain": asc(np.asarray(L0_attn_a_q_norm).reshape(1, HD)),
        "kgain": asc(np.asarray(L0_attn_a_k_norm).reshape(1, HD)),
        "gammaI": asc(np.asarray(L0_attn_b_gamma).reshape(1, D)),
        "betaI": asc(np.asarray(L0_attn_b_beta).reshape(1, D)),
    }
    shared.update({k: v for k, v in cst.items()})
    x = np.asarray(L0_attn_a_x)
    in_maps = []
    csets = []
    for core in range(NCORE):
        b, par = core // 2, core % 2
        cc, cset = _core_constants(core)
        csets.append(cset)
        xc = x[b].reshape(NCH, CH, D)[cset].reshape(TOK, D)
        m = dict(shared)
        m.update(cc)
        m["x_in"] = asc(xc)
        in_maps.append(m)
    return in_maps, csets


def assemble(res_list, csets, key="out_x"):
    out = np.empty((BATCH, SEQ, D), np.float32)
    for core in range(NCORE):
        b = core // 2
        oc = np.asarray(res_list[core][key]).reshape(SLOTS, CH, D)
        for j, cg in enumerate(csets[core]):
            out[b, cg * CH:(cg + 1) * CH] = oc[j]
    return out


def kernel(**inputs):
    nc = _get_nc(UPTO)
    in_maps, csets = prep_inmaps(**inputs)
    res = run_bass_kernel_spmd(nc, in_maps, core_ids=list(range(NCORE)))
    return assemble(res.results, csets)


# version 03
# speedup vs baseline: 1.0070x; 1.0070x over previous
import contextlib
import numpy as np
import ml_dtypes
import concourse.bass as bass
import concourse.mybir as mybir
from concourse.bass_utils import run_bass_kernel_spmd

F32 = mybir.dt.float32
BF16 = mybir.dt.bfloat16
U32 = mybir.dt.uint32
AF = mybir.ActivationFunctionType
ALU = mybir.AluOpType
AX = mybir.AxisListType
IOA = bass.IndirectOffsetOnAxis

BATCH, SEQ, D = 4, 8192, 2048
NCORE = 8
NCH = 16
CH = SEQ // NCH
TOK = SEQ * BATCH // NCORE
NT = TOK // 128
SLOTS = 8
TPS = CH // 128
DC = D // 128
HEADS, HD = 8, 256
ROPE_THETA = 1.0e6
E, HFF = 8, 2816
HCH = [512] * 5 + [256]
NHC = len(HCH)
HC = HFF // 128
CAP = 4096
KPAD = 640
KTILES = KPAD // 128
DISP_ROWS = E * KPAD + 128
EPS_LN = 1e-5
EPS_RMS = 1e-6
A_SET = [0, 3, 4, 7, 8, 11, 12, 15]
B_SET = [1, 2, 5, 6, 9, 10, 13, 14]
SETS = (A_SET, B_SET)

UPTO = 4
STRICT_EDGES = True


def tsl(i, n):
    return slice(i * n, (i + 1) * n)


class Op:
    __slots__ = ("q", "idx", "kind", "emit", "deps", "sdeps", "label")

    def __init__(self, q, idx, kind, emit, deps, sdeps, label):
        self.q, self.idx, self.kind = q, idx, kind
        self.emit, self.deps, self.sdeps, self.label = emit, deps, sdeps, label


class Queue:
    NROT = 8

    def __init__(self, B, name, attr):
        self.B, self.name, self.attr = B, name, attr
        self.ops = []
        self.sem = B.es.enter_context(B.nc.semaphore(f"q_{name}"))
        self.rsem = [B.es.enter_context(B.nc.semaphore(f"r_{name}{i}"))
                     for i in range(self.NROT)]
        self.n_dma = 0
        self.n_comp = 0

    def add(self, kind, emit, deps=(), sdeps=(), label=""):
        deps = [d for d in deps if d is not None]
        sdeps = [d for d in sdeps if d is not None]
        if kind == "dma":
            idx = self.n_dma
            self.n_dma += 1
        else:
            idx = self.n_comp
            self.n_comp += 1
        o = Op(self, idx, kind, emit, deps, sdeps, label)
        self.ops.append(o)
        return o


class Builder:
    def __init__(self, nc, es):
        self.nc, self.es = nc, es
        self.d = {}
        self.Q = {}
        self._uid = 0
        for name, attr in (("PE", "tensor"), ("DVE", "vector"), ("ACT", "scalar"),
                           ("POOL", "gpsimd"), ("SP", "sync")):
            self.Q[name] = Queue(self, name, attr)

    def sbuf(self, ph, name, shape, dt=F32):
        self._uid += 1
        return ph.enter_context(self.nc.sbuf_tensor(f"{name}_{self._uid}", shape, dt))

    def psum(self, ph, name, shape, dt=F32):
        self._uid += 1
        return ph.enter_context(self.nc.psum_tensor(f"{name}_{self._uid}", shape, dt))

    def dma(self, qname, out, in_, deps=(), label="", **kw):
        q = self.Q[qname]
        return q.add("dma", lambda e, out=out, in_=in_: e.dma_start(out=out, in_=in_, **kw),
                     deps=deps, label=label)

    def op(self, qname, emit, deps=(), sdeps=(), label=""):
        return self.Q[qname].add("comp", emit, deps=deps, sdeps=sdeps, label=label)

    def finalize(self, block):
        for q in self.Q.values():
            self._emit_queue(block, q)

    def _emit_queue(self, block, q):
        if not q.ops:
            return
        section = getattr(block, q.attr)

        def body(eng):
            hw = {}

            def wait(sem, val):
                k = id(sem)
                if hw.get(k, 0) >= val:
                    return
                eng.wait_ge(sem, val)
                hw[k] = val

            for o in q.ops:
                if o.kind == "dma" and o.idx >= Queue.NROT:
                    wait(q.rsem[o.idx % Queue.NROT], 16 * (o.idx // Queue.NROT))
                deps = list(o.deps) + list(o.sdeps)
                for didx, dep in enumerate(deps):
                    same_q = dep.q is q
                    forced = didx >= len(o.deps)
                    if (same_q and not forced and not STRICT_EDGES
                            and dep.kind == "comp"):
                        continue
                    if dep.kind == "dma":
                        sem = dep.q.rsem[dep.idx % Queue.NROT]
                        val = 16 * (dep.idx // Queue.NROT + 1)
                    elif dep.kind == "cc":
                        sem = self.cc_sem
                        val = dep.idx + 1
                    else:
                        sem = dep.q.sem
                        val = dep.idx + 1
                    wait(sem, val)
                ins = o.emit(eng)
                if o.kind == "dma":
                    ins.then_inc(q.rsem[o.idx % Queue.NROT], 16)
                elif o.kind == "cc":
                    ins.then_inc(self.cc_sem, 1)
                else:
                    ins.then_inc(q.sem, 1)

        section(body)

    def collective(self, kind, op, groups, ins_ap, outs_ap, deps=(), label=""):
        q = self.Q["POOL"]
        if not hasattr(self, "cc_sem"):
            self.cc_sem = self.es.enter_context(self.nc.semaphore("cc"))
            self.cc_n = 0
            self.cc_last = None
        idx = self.cc_n
        self.cc_n += 1
        deps = [d for d in deps if d is not None]
        if self.cc_last is not None:
            deps.append(self.cc_last)
        o = Op(q, idx, "cc",
               lambda e: e.collective_compute(kind, op, replica_groups=groups,
                                              ins=[ins_ap], outs=[outs_ap]),
               deps, [], label)
        q.ops.append(o)
        self.cc_last = o
        return o

    def fence(self, label="fence"):
        lasts = []
        for q in self.Q.values():
            lastc = next((o for o in reversed(q.ops) if o.kind != "dma"), None)
            lastd = next((o for o in reversed(q.ops) if o.kind == "dma"), None)
            if lastc is not None:
                lasts.append(lastc)
            if lastd is not None:
                lasts.append(lastd)
        for q in self.Q.values():
            q.add("comp", lambda e: e.nop(), deps=list(lasts), label=label)


def mm_group(B, ps_ap, pairs, deps, label=""):
    n = len(pairs)

    def emit(e):
        last = None
        for i, (l, r) in enumerate(pairs):
            last = e.matmul(ps_ap, l, r, start=(i == 0), stop=(i == n - 1))
        return last
    return B.op("PE", emit, deps=deps, label=label)


def tr_group(B, items, ident, deps, label=""):
    def emit(e):
        last = None
        for o, i in items:
            last = e.transpose(o, i, ident)
        return last
    return B.op("PE", emit, deps=deps, label=label)


def rsqrt(B, out_ap, in_ap, eps, scale, dep, label="rsqrt"):
    eps_ap = B.sb["eps"][:, 0:1] if eps == EPS_LN else B.sb["eps"][:, 1:2]
    o1 = B.op("ACT", lambda e: e.activation(out_ap, in_ap, AF.Ln, bias=eps_ap, scale=scale),
              deps=[dep, B.eps_dep], label=label + "_ln")
    return B.op("ACT", lambda e: e.activation(out_ap, out_ap, AF.Exp, scale=-0.5),
                deps=[o1], sdeps=[o1], label=label + "_exp")


def ln_tile(B, sb, x_ap, y_ap, scratch_ap, mean_ap, var_ap, rstd_ap, sum_ap, ss_ap,
            x_dep, y_war=None, scr_war=None, label="ln"):
    s1 = B.op("DVE", lambda e: e.reduce_sum(sum_ap, x_ap, axis=AX.X),
              deps=[x_dep], label=label + "_sum")
    s2 = B.op("DVE", lambda e: e.scalar_tensor_tensor(out=scratch_ap, in0=x_ap, scalar=1.0,
                                                      in1=x_ap, op0=ALU.bypass, op1=ALU.mult,
                                                      accum_out=ss_ap),
              deps=[x_dep, scr_war], label=label + "_ss")
    m = B.op("DVE", lambda e: e.tensor_scalar(out=mean_ap, in0=sum_ap, scalar1=1.0 / D,
                                              scalar2=None, op0=ALU.mult),
             deps=[s1], sdeps=[s1], label=label + "_mean")
    m2v = B.op("DVE", lambda e: e.tensor_tensor(out=var_ap, in0=mean_ap, in1=mean_ap, op=ALU.mult),
               deps=[m], sdeps=[m], label=label + "_m2")
    var = B.op("DVE", lambda e: e.scalar_tensor_tensor(out=var_ap, in0=ss_ap, scalar=1.0 / D,
                                                       in1=var_ap, op0=ALU.mult, op1=ALU.subtract),
               deps=[s2, m2v], sdeps=[s2, m2v], label=label + "_var")
    r = rsqrt(B, rstd_ap, var_ap, EPS_LN, 1.0, var, label=label + "_r")
    yw = y_war if isinstance(y_war, (list, tuple)) else [y_war]
    y = B.op("DVE", lambda e: e.tensor_scalar(out=y_ap, in0=x_ap, scalar1=mean_ap,
                                              scalar2=rstd_ap, op0=ALU.subtract, op1=ALU.mult),
             deps=[r] + list(yw), sdeps=[m], label=label + "_y")
    return y


def build_nc(upto=None, dbg=()):
    if upto is None:
        upto = UPTO
    nc = bass.Bass("TRN2", target_bir_lowering=False, debug=False)
    es = contextlib.ExitStack()
    B = Builder(nc, es)
    d = B.d

    def din(name, shape, dt=F32):
        d[name] = nc.dram_tensor(name, shape, dt, kind="ExternalInput").ap()

    B.dbg = set(dbg)

    def dint(name, shape, dt):
        kind = "ExternalOutput" if (name in dbg and "bag" not in name) else "Internal"
        d[name] = nc.dram_tensor(name, shape, dt, kind=kind).ap()
        if name in dbg and "bag" in name:
            d[name + "_dbg"] = nc.dram_tensor(name + "_dbg", shape, dt,
                                              kind="ExternalOutput").ap()

    din("x_in", [TOK, D])
    din("qTw", [D, D]); din("kTw", [D, HD]); din("vTw", [D, HD]); din("oTw", [D, D])
    din("qkvw", [D, 3 * D]); din("obw", [D, D])
    din("Wg", [D, E])
    din("W1", [E, D, HFF]); din("W3", [E, D, HFF]); din("W2", [E, HFF, D])
    din("sw1", [D, HFF]); din("sw3", [D, HFF]); din("sw2", [HFF, D])
    din("qgain", [1, HD]); din("kgain", [1, HD])
    din("gammaI", [1, D]); din("betaI", [1, D])
    din("cosT", [TOK, 128]); din("sinT", [TOK, 128])
    din("maskT", [SLOTS * CH, 1024])
    din("selT", [NCORE * SLOTS, E]); din("lts8", [SLOTS, SLOTS])
    din("lt128", [128, 128]); din("ones128", [128, 128])
    din("ident", [128, 128], BF16)
    d["out_x"] = nc.dram_tensor("out_x", [TOK, D], F32, kind="ExternalOutput").ap()

    dint("resid", [TOK, D], F32)
    dint("y_d", [TOK, D], F32)
    dint("QTa", [NT, D, 128], BF16)
    dint("QTb", [NT, D, 128], BF16)
    dint("ctxT_d", [NT, D, 128], BF16)
    dint("ka_in", [HD, TOK], BF16)
    dint("kag", [2 * HD, TOK], BF16)
    dint("va_in", [TOK, HD], BF16)
    dint("vag", [2 * TOK, HD], BF16)
    for g in range(4):
        dint(f"kb_in{g}", [D, 1024], BF16)
        dint(f"kbag{g}", [2 * D, 1024], BF16)
        dint(f"vb_in{g}", [1024, D], BF16)
        dint(f"vbag{g}", [2048, D], BF16)
    dint("counts_in", [SLOTS, E], F32)
    dint("counts_ag", [NCORE * SLOTS, E], F32)
    dint("base_d", [2 * SLOTS, E], F32)
    dint("xnorm_d", [TOK, D], BF16)
    dint("yT_d", [NT, D, 128], BF16)
    dint("hT_d", [NT, HFF, 128], BF16)
    dint("disp_d", [DISP_ROWS, D], BF16)
    dint("dispo_d", [DISP_ROWS, D], BF16)

    with es:
        with nc.Block() as block:
            sb = {}
            sb["ident"] = es.enter_context(nc.sbuf_tensor("c_ident", [128, 128], BF16))
            sb["iota8"] = es.enter_context(nc.sbuf_tensor("c_iota8", [128, 8], F32))
            sb["ones1"] = es.enter_context(nc.sbuf_tensor("c_ones1", [1, 128], BF16))
            sb["lt128"] = es.enter_context(nc.sbuf_tensor("c_lt128", [128, 128], BF16))
            sb["on128"] = es.enter_context(nc.sbuf_tensor("c_on128", [128, 128], BF16))
            sb["zs"] = es.enter_context(nc.sbuf_tensor("c_zs", [128, D], BF16))
            sb["onesc"] = es.enter_context(nc.sbuf_tensor("c_onesc", [128, 1], BF16))
            sb["eps"] = es.enter_context(nc.sbuf_tensor("c_eps", [128, 2], F32))
            B.sb = sb
            ld_i = B.dma("SP", sb["ident"][:], d["ident"], label="ld_ident")
            ld_lt = B.dma("POOL", sb["lt128"][:], d["lt128"], label="ld_lt")
            ld_on = B.dma("POOL", sb["on128"][:], d["ones128"], label="ld_on")
            B.op("POOL", lambda e: e.iota(sb["iota8"][:], pattern=[[1, E]], base=0,
                                          channel_multiplier=0,
                                          allow_small_or_imprecise_dtypes=True),
                 label="iota8")
            B.op("DVE", lambda e: e.memset(sb["ones1"][:], 1.0), label="ones1")
            B.op("DVE", lambda e: e.memset(sb["zs"][:], 0.0), label="zs")
            B.op("DVE", lambda e: e.memset(sb["onesc"][:], 1.0), label="onesc")
            e1_ = B.op("DVE", lambda e: e.memset(sb["eps"][:, 0:1], EPS_LN), label="eps1")
            B.eps_dep = B.op("DVE", lambda e: e.memset(sb["eps"][:, 1:2], EPS_RMS),
                             deps=[e1_], label="eps2")
            B.setup_deps = dict(ident=ld_i, lt=ld_lt, on=ld_on)
            B.fence("setup")

            phase_a1(B)
            B.fence("a1")
            B.ka_cc = B.collective("AllGather", ALU.bypass,
                                   [[0, 1], [2, 3], [4, 5], [6, 7]],
                                   d["ka_in"], d["kag"], label="ag_ka")
            B.va_cc = B.collective("AllGather", ALU.bypass,
                                   [[0, 1], [2, 3], [4, 5], [6, 7]],
                                   d["va_in"], d["vag"], label="ag_va")
            phase_flash(B, "a")
            B.fence("a2")
            phase_oproj(B, "oTw", "ctxT_d", src="x_in",
                        dst=("out_x" if upto == 1 else "resid"), add_y=False)
            B.fence("a3")
            if upto >= 2:
                phase_moe(B, src="resid", dst=("out_x" if upto == 2 else "resid"))
                B.fence("moe1")
            if upto >= 3:
                phase_b1(B)
                B.fence("b1x")
                phase_flash(B, "b")
                B.fence("b2")
                phase_oproj(B, "obw", "ctxT_d", src="resid",
                            dst=("out_x" if upto == 3 else "resid"), add_y=True)
                B.fence("b3")
            if upto >= 4:
                phase_moe(B, src="resid", dst="out_x")
                B.fence("moe2")
            B.fence("final")
            B.finalize(block)
    return nc


def phase_a1(B):
    nc, d, sb = B.nc, B.d, B.sb
    ph = contextlib.ExitStack()
    T = lambda n, s, dt=F32: B.sbuf(ph, n, s, dt)
    xsb = [T(f"a1x{i}", [128, D]) for i in range(2)]
    cs = [T(f"a1c{i}", [128, 128]) for i in range(2)]
    sn = [T(f"a1s{i}", [128, 128]) for i in range(2)]
    ysb = [T(f"a1y{i}", [128, D]) for i in range(2)]
    ybf = [T(f"a1yb{i}", [128, D], BF16) for i in range(2)]
    yT = [T(f"a1yT{i}", [128, D], BF16) for i in range(2)]
    sc1 = [T(f"a1t1{i}", [128, D]) for i in range(2)]
    qf = [T(f"a1qf{i}", [128, D]) for i in range(2)]
    qn_ = ysb
    qrot = [T(f"a1qr{i}", [128, D], BF16) for i in range(2)]
    qTr = [T(f"a1qTr{i}", [128, D], BF16) for i in range(2)]
    kv = [T(f"a1kv{i}", [128, 4 * HD]) for i in range(2)]
    krot = [T(f"a1kr{i}", [128, HD], BF16) for i in range(2)]
    kTr = [T(f"a1kTr{i}", [128, HD], BF16) for i in range(2)]
    vbf = [T(f"a1vb{i}", [128, HD], BF16) for i in range(2)]
    sm = [T(f"a1sm{i}", [128, 32]) for i in range(2)]
    qw = T("a1qw", [128, DC * D], BF16)
    kw = T("a1kw", [128, DC * HD], BF16)
    vw = T("a1vw", [128, DC * HD], BF16)
    qg = T("a1qg", [128, HD])
    kg = T("a1kg", [128, HD])
    pT = [B.psum(ph, f"a1pT{i}", [128, 1024], BF16) for i in range(2)]
    pQ = [B.psum(ph, f"a1pQ{i}", [128, 512], F32) for i in range(2)]
    pK = B.psum(ph, "a1pK", [128, HD], F32)
    pV = B.psum(ph, "a1pV", [128, HD], F32)

    with ph:
        wq = B.dma("POOL", qw[:].rearrange("p (c n) -> p c n", c=DC),
                   d["qTw"].rearrange("(c dd) n -> dd c n", dd=128), label="a1_wq")
        wk = B.dma("POOL", kw[:].rearrange("p (c n) -> p c n", c=DC),
                   d["kTw"].rearrange("(c dd) n -> dd c n", dd=128), label="a1_wk")
        wv = B.dma("POOL", vw[:].rearrange("p (c n) -> p c n", c=DC),
                   d["vTw"].rearrange("(c dd) n -> dd c n", dd=128), label="a1_wv")
        gq = B.dma("SP", qg[:], d["qgain"].broadcast_to([128, HD]), label="a1_gq")
        gk = B.dma("SP", kg[:], d["kgain"].broadcast_to([128, HD]), label="a1_gk")

        last = {}

        for t in range(NT):
            p = t % 2
            lx = B.dma("SP", xsb[p][:], d["x_in"][tsl(t, 128), :],
                       deps=[last.get(("x", p))], label=f"a1_lx{t}")
            lc = B.dma("SP", cs[p][:], d["cosT"][tsl(t, 128), :],
                       deps=[last.get(("rope", p))], label=f"a1_lc{t}")
            ls = B.dma("SP", sn[p][:], d["sinT"][tsl(t, 128), :],
                       deps=[last.get(("rope", p))], label=f"a1_ls{t}")
            yop = ln_tile(B, sb, xsb[p][:], ysb[p][:], sc1[p][:],
                          sm[p][:, 1:2], sm[p][:, 3:4], sm[p][:, 4:5],
                          sm[p][:, 0:1], sm[p][:, 2:3], lx,
                          y_war=last.get(("ybf", p)), scr_war=last.get(("sq", p)),
                          label=f"a1ln{t}")
            last[("x", p)] = yop
            cop = B.op("ACT", lambda e, p=p: e.copy(ybf[p][:], ysb[p][:]),
                       deps=[yop, last.get(("ybf", p))], label=f"a1cast{t}")
            trs = []
            for half in range(2):
                items = [(pT[half][:, tsl(i, 128)],
                          ybf[p][:, tsl(half * 8 + i, 128)]) for i in range(8)]
                g = tr_group(B, items, sb["ident"][:],
                             deps=[cop, B.setup_deps["ident"], last.get(("pT", half))],
                             label=f"a1trY{t}_{half}")
                ev = B.op("DVE", lambda e, p=p, half=half:
                          e.tensor_copy(yT[p][:, tsl(half, 1024)], pT[half][:]),
                          deps=[g, last.get(("yT", p))], label=f"a1evY{t}_{half}")
                last[("pT", half)] = ev
                last[("ybf", p)] = g
                trs.append(ev)
            qevs = []
            for nb in range(4):
                pb = nb % 2
                grp = mm_group(B, pQ[pb][:],
                               [(yT[p][:, tsl(c, 128)], qw[:, c * D + nb * 512: c * D + nb * 512 + 512])
                                for c in range(DC)],
                               deps=trs + [wq, last.get(("pQ", pb))], label=f"a1q{t}_{nb}")
                ev = B.op("DVE", lambda e, p=p, nb=nb, pb=pb:
                          e.tensor_copy(qf[p][:, tsl(nb, 512)], pQ[pb][:]),
                          deps=[grp, last.get(("qf", p))], label=f"a1qe{t}_{nb}")
                last[("pQ", pb)] = ev
                qevs.append(ev)
            kgrp = mm_group(B, pK[:],
                            [(yT[p][:, tsl(c, 128)], kw[:, tsl(c, HD)]) for c in range(DC)],
                            deps=trs + [wk, last.get("pK")], label=f"a1k{t}")
            kev = B.op("DVE", lambda e, p=p: e.tensor_copy(kv[p][:, 0:HD], pK[:]),
                       deps=[kgrp, last.get(("kv", p))], label=f"a1ke{t}")
            last["pK"] = kev
            vgrp = mm_group(B, pV[:],
                            [(yT[p][:, tsl(c, 128)], vw[:, tsl(c, HD)]) for c in range(DC)],
                            deps=trs + [wv, last.get("pV")], label=f"a1v{t}")
            vev = B.op("DVE", lambda e, p=p: e.tensor_copy(vbf[p][:], pV[:]),
                       deps=[vgrp, last.get(("vbf", p))], label=f"a1ve{t}")
            last["pV"] = vev
            last[("yT", p)] = vgrp
            sq = B.op("ACT", lambda e, p=p: e.square(sc1[p][:], qf[p][:]),
                      deps=qevs + [yop, last.get(("ropesc", p))],
                      label=f"a1sq{t}")
            qsss = []
            for h in range(HEADS):
                r = B.op("DVE", lambda e, p=p, h=h:
                         e.reduce_sum(sm[p][:, 8 + h: 9 + h], sc1[p][:, tsl(h, HD)], axis=AX.X),
                         deps=[sq], label=f"a1qss{t}_{h}")
                qsss.append(r)
            rq = rsqrt(B, sm[p][:, 16:24], sm[p][:, 8:16], EPS_RMS, 1.0 / HD,
                       qsss[-1], label=f"a1rq{t}")
            rq_all = [rq] + qsss
            qns = []
            for h in range(HEADS):
                o = B.op("DVE", lambda e, p=p, h=h:
                         e.scalar_tensor_tensor(out=qn_[p][:, tsl(h, HD)], in0=qf[p][:, tsl(h, HD)],
                                                scalar=sm[p][:, 16 + h: 17 + h], in1=qg[:],
                                                op0=ALU.mult, op1=ALU.mult),
                         deps=[rq, gq], sdeps=qsss, label=f"a1qn{t}_{h}")
                qns.append(o)
            last[("qf", p)] = qns[-1]
            kss = B.op("DVE", lambda e, p=p:
                       e.scalar_tensor_tensor(out=kv[p][:, HD: 2 * HD], in0=kv[p][:, 0:HD],
                                              scalar=1.0, in1=kv[p][:, 0:HD],
                                              op0=ALU.bypass, op1=ALU.mult,
                                              accum_out=sm[p][:, 24:25]),
                       deps=[kev], label=f"a1kss{t}")
            rk = rsqrt(B, sm[p][:, 25:26], sm[p][:, 24:25], EPS_RMS, 1.0 / HD,
                       kss, label=f"a1rk{t}")
            kn = B.op("DVE", lambda e, p=p:
                      e.scalar_tensor_tensor(out=kv[p][:, HD: 2 * HD], in0=kv[p][:, 0:HD],
                                             scalar=sm[p][:, 25:26], in1=kg[:],
                                             op0=ALU.mult, op1=ALU.mult),
                      deps=[rk, gk, kss], sdeps=[kss], label=f"a1kn{t}")
            last[("kv", p)] = kn
            H2 = HD // 2
            ropes = []

            def rope_emit(src_ap_fn, dst_ap_fn, scr_ap, n_heads, extra_deps):
                outs = []
                for h in range(n_heads):
                    s0 = src_ap_fn(h, 0)
                    s1 = src_ap_fn(h, 1)
                    t1 = B.op("DVE", lambda e, s0=s0, scr=scr_ap(0): e.tensor_mul(scr, s0, cs[p][:]),
                              deps=extra_deps, label=f"a1ro{t}")
                    t2 = B.op("DVE", lambda e, s1=s1, scr=scr_ap(1): e.tensor_mul(scr, s1, sn[p][:]),
                              deps=extra_deps, label=f"a1ro{t}")
                    o1 = B.op("DVE", lambda e, d0=dst_ap_fn(h, 0), a=scr_ap(0), b=scr_ap(1):
                              e.tensor_sub(d0, a, b), deps=[t1, t2], label=f"a1ro{t}")
                    t3 = B.op("DVE", lambda e, s0=s0, scr=scr_ap(0): e.tensor_mul(scr, s0, sn[p][:]),
                              deps=[o1], label=f"a1ro{t}")
                    t4 = B.op("DVE", lambda e, s1=s1, scr=scr_ap(1): e.tensor_mul(scr, s1, cs[p][:]),
                              deps=[o1], label=f"a1ro{t}")
                    o2 = B.op("DVE", lambda e, d1=dst_ap_fn(h, 1), a=scr_ap(0), b=scr_ap(1):
                              e.tensor_add(d1, a, b), deps=[t3, t4], label=f"a1ro{t}")
                    outs.append(o2)
                return outs

            q_src = lambda h, half: qn_[p][:, h * HD + half * H2: h * HD + half * H2 + H2]
            q_dst = lambda h, half: qrot[p][:, h * HD + half * H2: h * HD + half * H2 + H2]
            q_scr = lambda i: sc1[p][:, i * H2:(i + 1) * H2]
            ropes = rope_emit(q_src, q_dst, q_scr, HEADS,
                              [qns[0], qns[-1], lc, ls, last.get(("qrot", p)), sq])
            k_src = lambda h, half: kv[p][:, HD + half * H2: HD + half * H2 + H2]
            k_dst = lambda h, half: krot[p][:, half * H2: half * H2 + H2]
            k_scr = lambda i: kv[p][:, 2 * HD + i * H2: 2 * HD + (i + 1) * H2]
            kroped = rope_emit(k_src, k_dst, k_scr, 1,
                               [kn, lc, ls, last.get(("krot", p))])
            last[("rope", p)] = kroped[-1]
            last[("ropesc", p)] = ropes[-1]
            last[("sq", p)] = ropes[-1]
            qst = []
            for half in range(2):
                items = [(pT[half][:, tsl(i, 128)],
                          qrot[p][:, tsl(half * 8 + i, 128)]) for i in range(8)]
                g = tr_group(B, items, sb["ident"][:],
                             deps=ropes + [last.get(("pT", half))], label=f"a1trQ{t}_{half}")
                ev = B.op("DVE", lambda e, p=p, half=half:
                          e.tensor_copy(qTr[p][:, tsl(half, 1024)], pT[half][:]),
                          deps=[g, last.get(("qTr", p))], label=f"a1evQ{t}_{half}")
                last[("pT", half)] = ev
                qst.append(ev)
            last[("qrot", p)] = qst[-1]
            sq_dma = B.dma("POOL", d["QTa"][t].rearrange("(c dd) n -> dd c n", dd=128),
                           qTr[p][:].rearrange("p (c n) -> p c n", c=DC),
                           deps=qst, label=f"a1stQ{t}")
            last[("qTr", p)] = sq_dma
            gk2 = tr_group(B, [(pT[0][:, tsl(i, 128)], krot[p][:, tsl(i, 128)]) for i in range(2)],
                           sb["ident"][:], deps=kroped + [last.get(("pT", 0))],
                           label=f"a1trK{t}")
            kev2 = B.op("DVE", lambda e, p=p: e.tensor_copy(kTr[p][:], pT[0][:, 0:HD]),
                        deps=[gk2, last.get(("kTr", p))], label=f"a1evK{t}")
            last[("pT", 0)] = kev2
            last[("krot", p)] = kev2
            sk_dma = B.dma("POOL", d["ka_in"].rearrange("(c dd) n -> dd c n", dd=128)[:, :, tsl(t, 128)],
                           kTr[p][:].rearrange("p (c n) -> p c n", c=2),
                           deps=[kev2], label=f"a1stK{t}")
            last[("kTr", p)] = sk_dma
            sv_dma = B.dma("POOL", d["va_in"][tsl(t, 128), :], vbf[p][:],
                           deps=[vev], label=f"a1stV{t}")
            last[("vbf", p)] = sv_dma


def chunk_par(c):
    return 0 if c in A_SET else 1


def chunk_lslot(c):
    return SETS[chunk_par(c)].index(c)


def phase_flash(B, kind):
    nc, d, sb = B.nc, B.d, B.sb
    ph = contextlib.ExitStack()
    T = lambda n, s, dt=F32: B.sbuf(ph, n, s, dt)
    heads = HEADS if kind == "a" else 1
    hd = HD if kind == "a" else D
    scale = (1.0 / 16.0) if kind == "a" else (1.0 / float(np.sqrt(2048.0)))
    NU = TPS * heads
    q_dram = d["QTa"] if kind == "a" else d["QTb"]

    mask_sb = T("f_mask", [128, TPS * 1024])
    qts = [T(f"f_q{i}", [128, D], BF16) for i in range(TPS)]
    ctx = T("f_ctx", [128, 8192])
    stt_ = T("f_st", [128, 4 * 64])
    P = [T(f"f_P{i}", [128, 1024], BF16) for i in range(2)]
    PT = [T(f"f_PT{i}", [128, 1024], BF16) for i in range(2)]
    ctxb = T("f_ctxb", [128, 2048], BF16)
    ctxT = T("f_ctxT", [128, 2048], BF16)
    if kind == "a":
        KT = T("f_KT", [128, 2 * SEQ], BF16)
        VT = T("f_VT", [128, (SEQ // 128) * HD], BF16)
    else:
        KT = [T(f"f_KTb{i}", [128, DC * 1024], BF16) for i in range(2)]
        VT = T("f_VTb", [128, 8 * D], BF16)
    psS2 = [B.psum(ph, f"f_psS{i}", [128, 1024], F32) for i in range(2)]
    psPV = [B.psum(ph, f"f_pv{i}", [128, 512], F32) for i in range(2)]
    psT = [B.psum(ph, f"f_pT{i}", [128, 1024], BF16) for i in range(2)]

    def nmc(u):
        return stt_[:, u:u + 1]

    def lc(u):
        return stt_[:, 64 + u:64 + u + 1]

    def tmpc(u):
        return stt_[:, 128 + u:128 + u + 1]

    def t2c(u):
        return stt_[:, 192 + u:192 + u + 1]

    with ph:
        last = {}
        if kind == "a":
            kvdeps = []
            for g in range(NCH):
                par, ls_ = chunk_par(g), chunk_lslot(g)
                for c in range(2):
                    o = B.dma("SP", KT[:, c * SEQ + g * CH: c * SEQ + (g + 1) * CH],
                              d["kag"].rearrange("(pp c dd) n -> pp c dd n", pp=2, c=2)[par, c][:, tsl(ls_, CH)],
                              deps=[B.ka_cc], label=f"fA_k{g}_{c}")
                    kvdeps.append(o)
                for kt in range(4):
                    o = B.dma("SP", VT[:, (g * 4 + kt) * HD: (g * 4 + kt + 1) * HD],
                              d["vag"][par * TOK + ls_ * CH + kt * 128: par * TOK + ls_ * CH + (kt + 1) * 128, :],
                              deps=[B.va_cc], label=f"fA_v{g}_{kt}")
                    kvdeps.append(o)
            last["kv"] = kvdeps

        ucount = 0
        for j in range(SLOTS):
            lm = B.dma("SP", mask_sb[:].rearrange("p (q k) -> p q k", q=TPS),
                       d["maskT"].rearrange("(q pp) k -> pp q k", pp=128)[:, tsl(j, TPS), :],
                       deps=[last.get("mask")], label=f"f_m{j}")
            qlds = []
            for qt in range(TPS):
                tile = j * TPS + qt
                o = B.dma("SP", qts[qt][:].rearrange("p (c n) -> p c n", c=DC),
                          q_dram[tile].rearrange("(c dd) n -> dd c n", dd=128),
                          deps=[last.get(("q", qt))], label=f"f_q{j}_{qt}")
                qlds.append(o)
            for s in range(j + 1):
                islast = (s == j)
                if kind == "b":
                    sp = s % 2
                    kv_ops = []
                    for half in range(2):
                        c = 2 * s + half
                        par, ls_, g = chunk_par(c), chunk_lslot(c), chunk_lslot(c) // 2
                        col0 = (chunk_lslot(c) % 2) * CH
                        for cc_ in range(DC):
                            o = B.dma("SP",
                                      KT[sp][:, cc_ * 1024 + half * CH: cc_ * 1024 + half * CH + CH],
                                      d[f"kbag{g}"].rearrange("(pp c dd) n -> pp c dd n", pp=2, c=DC)[par, cc_][:, col0:col0 + CH],
                                      deps=[B.kb_cc[g], last.get(("KTb", sp))],
                                      label=f"fB_k{j}_{s}_{half}_{cc_}")
                            kv_ops.append(o)
                        for kt in range(4):
                            o = B.dma("SP",
                                      VT[:, (half * 4 + kt) * D: (half * 4 + kt + 1) * D],
                                      d[f"vbag{g}"][par * 1024 + col0 + kt * 128: par * 1024 + col0 + (kt + 1) * 128, :],
                                      deps=[B.vb_cc[g], last.get("VTb")],
                                      label=f"fB_v{j}_{s}_{half}_{kt}")
                            kv_ops.append(o)
                    last[("KTb_ld", sp)] = kv_ops
                for qt in range(TPS):
                    for h in range(heads):
                        u = qt * heads + h
                        up = ucount % 2
                        ucount += 1
                        psS = psS2[up]
                        qkg = []
                        for kb in range(2):
                            if kind == "a":
                                pairs = [(qts[qt][:, tsl(2 * h + c, 128)],
                                          KT[:, c * SEQ + (s * 1024 + kb * 512): c * SEQ + (s * 1024 + kb * 512) + 512])
                                         for c in range(2)]
                                kdeps = last["kv"] if s == 0 and j == 0 else [last["kv"][0]]
                                kdeps = last["kv"]
                            else:
                                pairs = [(qts[qt][:, tsl(c, 128)],
                                          KT[s % 2][:, c * 1024 + kb * 512: c * 1024 + kb * 512 + 512])
                                         for c in range(DC)]
                                kdeps = last[("KTb_ld", s % 2)]
                            g = mm_group(B, psS[:, tsl(kb, 512)], pairs,
                                         deps=[qlds[qt]] + kdeps +
                                              [last.get(("psS", up))], label=f"fqk{j}_{s}_{u}_{kb}")
                            qkg.append(g)
                        if kind == "b":
                            last[("KTb", s % 2)] = qkg[-1]
                        bm = B.op("DVE", lambda e, u=u, psS=psS: e.reduce_max(
                            tmpc(u), psS[:], axis=AX.X, negate=True),
                            deps=qkg, label=f"fbm{j}_{s}_{u}")
                        if s == 0:
                            nmu = B.op("DVE", lambda e, u=u: e.tensor_copy(nmc(u), tmpc(u)),
                                       deps=[bm], sdeps=[bm], label=f"fnm0{j}_{u}")
                            bco = B.op("DVE", lambda e, u=u: e.tensor_scalar(
                                out=t2c(u), in0=nmc(u), scalar1=scale, scalar2=None, op0=ALU.mult),
                                deps=[nmu], sdeps=[nmu], label=f"fbc{j}_{u}")
                            alph = None
                        else:
                            dlt = B.op("DVE", lambda e, u=u: e.tensor_tensor(
                                out=t2c(u), in0=tmpc(u), in1=nmc(u), op=ALU.subtract),
                                deps=[bm], sdeps=[bm, last.get(("lupd", u))], label=f"fdl{j}_{s}_{u}")
                            dlt2 = B.op("DVE", lambda e, u=u: e.tensor_scalar_min(
                                out=t2c(u), in0=t2c(u), scalar1=0.0),
                                deps=[dlt], sdeps=[dlt], label=f"fdl2{j}_{s}_{u}")
                            alph = B.op("ACT", lambda e, u=u: e.activation(
                                tmpc(u), t2c(u), AF.Exp, scale=scale),
                                deps=[dlt2], label=f"fal{j}_{s}_{u}")
                            nmu = B.op("DVE", lambda e, u=u: e.tensor_tensor(
                                out=nmc(u), in0=nmc(u), in1=t2c(u), op=ALU.add),
                                deps=[dlt2, alph], sdeps=[dlt2], label=f"fnm{j}_{s}_{u}")
                            bco = B.op("DVE", lambda e, u=u: e.tensor_scalar(
                                out=t2c(u), in0=nmc(u), scalar1=scale, scalar2=None, op0=ALU.mult),
                                deps=[nmu, alph], sdeps=[nmu], label=f"fbc{j}_{s}_{u}")
                        plc = stt_[:, 254:255]
                        pexp = B.op("ACT", lambda e, u=u, up=up, psS=psS: e.activation(
                            P[up][:], psS[:], AF.Exp, bias=t2c(u), scale=scale,
                            accum_out=plc),
                            deps=[bco, last.get(("P", up)), last.get("plc")],
                            label=f"fex{j}_{s}_{u}")
                        last[("psS", up)] = pexp
                        pmask = None
                        if islast:
                            pmask = B.op("DVE", lambda e, u=u, up=up, qt=qt: e.scalar_tensor_tensor(
                                out=P[up][:], in0=P[up][:], scalar=1.0,
                                in1=mask_sb[:, tsl(qt, 1024)], op0=ALU.bypass, op1=ALU.mult,
                                accum_out=plc),
                                deps=[pexp, lm], label=f"fpm{j}_{s}_{u}")
                        if s == 0:
                            lup = B.op("DVE", lambda e, u=u: e.tensor_copy(lc(u), plc),
                                       deps=[pmask or pexp], sdeps=[pmask or pexp],
                                       label=f"flu0{j}_{u}")
                        else:
                            lup = B.op("DVE", lambda e, u=u: e.scalar_tensor_tensor(
                                out=lc(u), in0=lc(u), scalar=tmpc(u), in1=plc,
                                op0=ALU.mult, op1=ALU.add),
                                deps=[pmask or pexp, alph], sdeps=[pmask or pexp, alph],
                                label=f"flu{j}_{s}_{u}")
                        last[("lupd", u)] = lup
                        last["plc"] = lup
                        trg = tr_group(B, [(psT[up][:, tsl(i % 8, 128)], P[up][:, tsl(i, 128)])
                                           for i in range(8)],
                                       sb["ident"][:],
                                       deps=[pmask or pexp, last.get(("psT", up))],
                                       label=f"ftr{j}_{s}_{u}")
                        ptev = B.op("DVE", lambda e, up=up: e.tensor_copy(PT[up][:], psT[up][:]),
                                    deps=[trg, last.get(("PT", up))], label=f"fpte{j}_{s}_{u}")
                        last[("psT", up)] = ptev
                        last[("P", up)] = trg
                        if kind == "a":
                            pvg = mm_group(B, psPV[up][:, 0:256],
                                           [(PT[up][:, tsl(kt, 128)],
                                             VT[:, (s * 8 + kt) * HD: (s * 8 + kt + 1) * HD])
                                            for kt in range(8)],
                                           deps=[ptev] + ([] if s else []) +
                                                [last.get(("psPV", up))] + last["kv"],
                                           label=f"fpv{j}_{s}_{u}")
                            if s == 0:
                                cup = B.op("DVE", lambda e, u=u, up=up: e.tensor_copy(
                                    ctx[:, tsl(u, 256)], psPV[up][:, 0:256]),
                                    deps=[pvg], label=f"fcu0{j}_{u}")
                            else:
                                cup = B.op("DVE", lambda e, u=u, up=up: e.scalar_tensor_tensor(
                                    out=ctx[:, tsl(u, 256)], in0=ctx[:, tsl(u, 256)],
                                    scalar=tmpc(u), in1=psPV[up][:, 0:256],
                                    op0=ALU.mult, op1=ALU.add),
                                    deps=[pvg, alph], sdeps=[alph, last.get(("ctx", u))],
                                    label=f"fcu{j}_{s}_{u}")
                            last[("psPV", up)] = cup
                            last[("ctx", u)] = cup
                            last[("PT", up)] = pvg
                        else:
                            vdeps = last[("KTb_ld", s % 2)]
                            cups = []
                            for n2 in range(4):
                                pvb = n2 % 2
                                pvg = mm_group(B, psPV[pvb][:],
                                               [(PT[up][:, tsl(kt, 128)],
                                                 VT[:, kt * D + n2 * 512: kt * D + n2 * 512 + 512])
                                                for kt in range(8)],
                                               deps=[ptev, last.get(("psPV", pvb))] + vdeps,
                                               label=f"fpvb{j}_{s}_{u}_{n2}")
                                if s == 0:
                                    cup = B.op("DVE", lambda e, u=u, n2=n2, pvb=pvb: e.tensor_copy(
                                        ctx[:, u * 2048 + n2 * 512: u * 2048 + (n2 + 1) * 512],
                                        psPV[pvb][:]),
                                        deps=[pvg], label=f"fcb0{j}_{u}_{n2}")
                                else:
                                    cup = B.op("DVE", lambda e, u=u, n2=n2, pvb=pvb: e.scalar_tensor_tensor(
                                        out=ctx[:, u * 2048 + n2 * 512: u * 2048 + (n2 + 1) * 512],
                                        in0=ctx[:, u * 2048 + n2 * 512: u * 2048 + (n2 + 1) * 512],
                                        scalar=tmpc(u), in1=psPV[pvb][:],
                                        op0=ALU.mult, op1=ALU.add),
                                        deps=[pvg, alph], sdeps=[alph, last.get(("ctxb", (u, n2)))],
                                        label=f"fcb{j}_{s}_{u}_{n2}")
                                last[("psPV", pvb)] = cup
                                last[("ctxb", (u, n2))] = cup
                                cups.append(cup)
                            last[("PT", up)] = cups[-1]
                            last["VTb"] = cups[-1]
            for qt in range(TPS):
                tile = j * TPS + qt
                fins = []
                for h in range(heads):
                    u = qt * heads + h
                    rl = B.op("DVE", lambda e, u=u: e.reciprocal(tmpc(u), lc(u)),
                              deps=[last[("lupd", u)]], sdeps=[last[("lupd", u)]],
                              label=f"frl{j}_{u}")
                    span = 256 if kind == "a" else 2048
                    cb = B.op("DVE", lambda e, u=u, h=h, span=span: e.tensor_scalar(
                        out=ctxb[:, h * span:(h + 1) * span] if kind == "a" else ctxb[:],
                        in0=ctx[:, u * span:(u + 1) * span],
                        scalar1=tmpc(u), scalar2=None, op0=ALU.mult),
                        deps=[rl, last.get(("ctx", u)), last.get("ctxb_war")],
                        sdeps=[rl], label=f"fcb{j}_{u}")
                    fins.append(cb)
                trs = []
                for half in range(2):
                    items = [(psT[half][:, tsl(i, 128)], ctxb[:, tsl(half * 8 + i, 128)])
                             for i in range(8)]
                    g = tr_group(B, items, sb["ident"][:],
                                 deps=fins + [last.get(("psT", half))], label=f"fct{j}_{qt}_{half}")
                    ev = B.op("DVE", lambda e, half=half: e.tensor_copy(
                        ctxT[:, tsl(half, 1024)], psT[half][:]),
                        deps=[g, last.get("ctxT")], label=f"fcte{j}_{qt}_{half}")
                    last[("psT", half)] = ev
                    trs.append(ev)
                last["ctxb_war"] = trs[-1]
                st = B.dma("POOL", d["ctxT_d"][tile].rearrange("(c dd) n -> dd c n", dd=128),
                           ctxT[:].rearrange("p (c n) -> p c n", c=DC),
                           deps=trs, label=f"fst{j}_{qt}")
                last["ctxT"] = st
                last[("q", qt)] = st
            last["mask"] = st


def phase_oproj(B, wname, ctx_name, src, dst, add_y):
    nc, d, sb = B.nc, B.d, B.sb
    ph = contextlib.ExitStack()
    T = lambda n, s, dt=F32: B.sbuf(ph, n, s, dt)
    wsb = T("o_w", [128, DC * D], BF16)
    ctxT = [T(f"o_c{i}", [128, D], BF16) for i in range(2)]
    xs = [T(f"o_x{i}", [128, D]) for i in range(2)]
    ys = [T(f"o_y{i}", [128, D]) for i in range(2)] if add_y else None
    xn = [T(f"o_n{i}", [128, D]) for i in range(2)]
    pO = [B.psum(ph, f"o_p{i}", [128, 512], F32) for i in range(2)]
    with ph:
        wl = B.dma("POOL", wsb[:].rearrange("p (c n) -> p c n", c=DC),
                   d[wname].rearrange("(c dd) n -> dd c n", dd=128), label="o_wl")
        last = {}
        for t in range(NT):
            p = t % 2
            lc_ = B.dma("SP", ctxT[p][:].rearrange("p (c n) -> p c n", c=DC),
                        d[ctx_name][t].rearrange("(c dd) n -> dd c n", dd=128),
                        deps=[last.get(("ct", p))], label=f"o_lc{t}")
            lx = B.dma("SP", xs[p][:], d[src][tsl(t, 128), :],
                       deps=[last.get(("x", p))], label=f"o_lx{t}")
            ly = None
            if add_y:
                ly = B.dma("SP", ys[p][:], d["y_d"][tsl(t, 128), :],
                           deps=[last.get(("y", p))], label=f"o_ly{t}")
            combs = []
            for nb in range(4):
                pb = nb % 2
                grp = mm_group(B, pO[pb][:],
                               [(ctxT[p][:, tsl(c, 128)], wsb[:, c * D + nb * 512: c * D + nb * 512 + 512])
                                for c in range(DC)],
                               deps=[lc_, wl, last.get(("pO", pb))], label=f"o_g{t}_{nb}")
                c1 = B.op("DVE", lambda e, p=p, nb=nb, pb=pb: e.scalar_tensor_tensor(
                    out=xn[p][:, tsl(nb, 512)], in0=xs[p][:, tsl(nb, 512)], scalar=1.0,
                    in1=pO[pb][:], op0=ALU.bypass, op1=ALU.add),
                    deps=[grp, lx, last.get(("xn", p))], label=f"o_c1{t}_{nb}")
                last[("pO", pb)] = c1
                if add_y:
                    c2 = B.op("DVE", lambda e, p=p, nb=nb: e.tensor_add(
                        xn[p][:, tsl(nb, 512)], xn[p][:, tsl(nb, 512)], ys[p][:, tsl(nb, 512)]),
                        deps=[c1, ly], label=f"o_c2{t}_{nb}")
                    combs.append(c2)
                else:
                    combs.append(c1)
            last[("ct", p)] = combs[-1]
            last[("x", p)] = combs[-1]
            if add_y:
                last[("y", p)] = combs[-1]
            st = B.dma("POOL", d[dst][tsl(t, 128), :], xn[p][:], deps=combs, label=f"o_st{t}")
            last[("xn", p)] = st


def phase_moe(B, src, dst):
    nc, d, sb = B.nc, B.d, B.sb
    TRASH = float(E * KPAD)
    ph = contextlib.ExitStack()
    T = lambda n, s, dt=F32: B.sbuf(ph, n, s, dt)
    eidx_sb = T("m_eidx", [128, NT])
    gate_sb = T("m_gate", [128, NT])
    gk_sb = T("m_gk", [128, NT])
    idx_sb = T("m_idx", [128, NT], U32)
    with ph:
        _moe_m1(B, ph, src, eidx_sb, gate_sb)
        B.fence("m1")
        _moe_m1b(B, ph)
        B.fence("m1b")
        _moe_m2(B, ph, eidx_sb, gate_sb, gk_sb, idx_sb)
        B.fence("m2")
        _moe_m3(B, ph)
        B.fence("m3")
        _moe_m4(B, ph, src, dst, gk_sb, idx_sb)


def _moe_m1(B, ph, src, eidx_sb, gate_sb):
    nc, d, sb = B.nc, B.d, B.sb
    T = lambda n, s, dt=F32: B.sbuf(ph, n, s, dt)
    p2 = contextlib.ExitStack()
    T2 = lambda n, s, dt=F32: B.sbuf(p2, n, s, dt)
    xsb = [T2(f"m1x{i}", [128, D]) for i in range(2)]
    ysb = [T2(f"m1y{i}", [128, D]) for i in range(2)]
    ybf = [T2(f"m1yb{i}", [128, D], BF16) for i in range(2)]
    ylo = [T2(f"m1yl{i}", [128, D]) for i in range(2)]
    ylob = [T2(f"m1ylb{i}", [128, D], BF16) for i in range(2)]
    yT = [T2(f"m1yT{i}", [128, D], BF16) for i in range(2)]
    yloT = [T2(f"m1yLT{i}", [128, D], BF16) for i in range(2)]
    sc1 = [T2(f"m1s1{i}", [128, D]) for i in range(2)]
    sm = [T2(f"m1sm{i}", [128, 48]) for i in range(2)]
    oh_sb = T2("m1oh", [128, TPS * E], BF16)
    wg_hi = T2("m1wgh", [128, DC * E], BF16)
    wg_lo = T2("m1wgl", [128, DC * E], BF16)
    wg_f = T2("m1wgf", [128, DC * E])
    pT = [B.psum(p2, f"m1pT{i}", [128, 1024], BF16) for i in range(2)]
    pZ = B.psum(p2, "m1pZ", [128, E], F32)
    pC = B.psum(p2, "m1pC", [1, E], F32)
    with p2:
        wgl = B.dma("SP", wg_f[:].rearrange("p (c n) -> p c n", c=DC),
                    d["Wg"].rearrange("(c dd) n -> dd c n", dd=128), label="m1_wg")
        h1 = B.op("DVE", lambda e: e.tensor_copy(wg_hi[:], wg_f[:]), deps=[wgl], label="m1_wgh")
        h2 = B.op("DVE", lambda e: e.tensor_sub(wg_f[:], wg_f[:], wg_hi[:]),
                  deps=[h1], label="m1_wgs")
        h3 = B.op("DVE", lambda e: e.tensor_copy(wg_lo[:], wg_f[:]), deps=[h2], label="m1_wgl")
        last = {}
        zero_st = []
        for zt in range(DISP_ROWS // 128):
            o = B.dma("POOL", d["disp_d"][tsl(zt, 128), :], sb["zs"][:], label=f"m1_z{zt}")
            zero_st.append(o)
        for t in range(NT):
            p = t % 2
            slot, tin = t // TPS, t % TPS
            lx = B.dma("SP", xsb[p][:], d[src][tsl(t, 128), :],
                       deps=[last.get(("x", p))], label=f"m1_lx{t}")
            yop = ln_tile(B, sb, xsb[p][:], ysb[p][:], sc1[p][:],
                          sm[p][:, 1:2], sm[p][:, 3:4], sm[p][:, 4:5],
                          sm[p][:, 0:1], sm[p][:, 2:3], lx,
                          y_war=last.get(("y", p)), label=f"m1ln{t}")
            last[("x", p)] = yop
            cop = B.op("ACT", lambda e, p=p: e.copy(ybf[p][:], ysb[p][:]),
                       deps=[yop, last.get(("ybf", p))], label=f"m1c{t}")
            lo1 = B.op("DVE", lambda e, p=p: e.tensor_sub(ylo[p][:], ysb[p][:], ybf[p][:]),
                       deps=[cop, last.get(("ylo", p))], label=f"m1lo{t}")
            lo2 = B.op("ACT", lambda e, p=p: e.copy(ylob[p][:], ylo[p][:]),
                       deps=[lo1, last.get(("ylob", p))], label=f"m1lo2{t}")
            last[("ylo", p)] = lo2
            last[("y", p)] = lo1
            sx = B.dma("POOL", d["xnorm_d"][tsl(t, 128), :], ybf[p][:],
                       deps=[cop], label=f"m1sx{t}")
            trs = []
            for half in range(2):
                g = tr_group(B, [(pT[half][:, tsl(i, 128)], ybf[p][:, tsl(half * 8 + i, 128)])
                                 for i in range(8)], sb["ident"][:],
                             deps=[cop, last.get(("pT", half))], label=f"m1tr{t}_{half}")
                ev = B.op("DVE", lambda e, p=p, half=half: e.tensor_copy(
                    yT[p][:, tsl(half, 1024)], pT[half][:]),
                    deps=[g, last.get(("yT", p))], label=f"m1ev{t}_{half}")
                last[("pT", half)] = ev
                trs.append(ev)
            last[("ybf", p)] = sx
            sy = B.dma("POOL", d["yT_d"][t].rearrange("(c dd) n -> dd c n", dd=128),
                       yT[p][:].rearrange("p (c n) -> p c n", c=DC),
                       deps=trs, label=f"m1sy{t}")
            trl = []
            for half in range(2):
                g = tr_group(B, [(pT[half][:, tsl(i, 128)], ylob[p][:, tsl(half * 8 + i, 128)])
                                 for i in range(8)], sb["ident"][:],
                             deps=[lo2, last.get(("pT", half))], label=f"m1trl{t}_{half}")
                ev = B.op("DVE", lambda e, p=p, half=half: e.tensor_copy(
                    yloT[p][:, tsl(half, 1024)], pT[half][:]),
                    deps=[g, last.get(("yloT", p))], label=f"m1evl{t}_{half}")
                last[("pT", half)] = ev
                last[("ylob", p)] = g
                trl.append(ev)
            pairs = ([(yT[p][:, tsl(c, 128)], wg_hi[:, tsl(c, E)]) for c in range(DC)] +
                     [(yT[p][:, tsl(c, 128)], wg_lo[:, tsl(c, E)]) for c in range(DC)] +
                     [(yloT[p][:, tsl(c, 128)], wg_hi[:, tsl(c, E)]) for c in range(DC)])
            zg = mm_group(B, pZ[:], pairs,
                          deps=trs + trl + [h1, h3, last.get("pZ")], label=f"m1z{t}")
            last[("yT", p)] = zg
            last[("yloT", p)] = zg
            zev = B.op("DVE", lambda e, p=p: e.tensor_copy(sm[p][:, 8:16], pZ[:]),
                       deps=[zg], label=f"m1ze{t}")
            last["pZ"] = zev
            mx = B.op("DVE", lambda e, p=p: e.reduce_max(sm[p][:, 16:17], sm[p][:, 8:16], axis=AX.X),
                      deps=[zev], sdeps=[zev], label=f"m1mx{t}")
            ng = B.op("DVE", lambda e, p=p: e.tensor_scalar(
                out=sm[p][:, 17:18], in0=sm[p][:, 16:17], scalar1=-1.0, scalar2=None, op0=ALU.mult),
                deps=[mx], sdeps=[mx], label=f"m1ng{t}")
            ex = B.op("ACT", lambda e, p=p: e.activation(
                sm[p][:, 24:32], sm[p][:, 8:16], AF.Exp, bias=sm[p][:, 17:18],
                accum_out=sm[p][:, 18:19]),
                deps=[ng], label=f"m1ex{t}")
            gt = B.op("DVE", lambda e, p=p, t=t: e.reciprocal(gate_sb[:, t:t + 1], sm[p][:, 18:19]),
                      deps=[ex], sdeps=[ex], label=f"m1g{t}")
            oh = B.op("DVE", lambda e, p=p, tin=tin: e.tensor_scalar(
                out=oh_sb[:, tsl(tin, E)], in0=sm[p][:, 8:16], scalar1=sm[p][:, 16:17],
                scalar2=None, op0=ALU.is_equal),
                deps=[mx, last.get(("oh", tin))], sdeps=[mx], label=f"m1oh{t}")
            ei = B.op("DVE", lambda e, p=p, tin=tin, t=t: e.scalar_tensor_tensor(
                out=sm[p][:, 24:32], in0=oh_sb[:, tsl(tin, E)], scalar=1.0, in1=sb["iota8"][:],
                op0=ALU.bypass, op1=ALU.mult, accum_out=eidx_sb[:, t:t + 1]),
                deps=[oh, ex], sdeps=[oh], label=f"m1ei{t}")
            pass
            cg = B.op("PE", lambda e, tin=tin: e.matmul(
                pC[:], sb["onesc"][:], oh_sb[:, tsl(tin, E)],
                start=(tin == 0), stop=(tin == TPS - 1)),
                deps=[oh, last.get("pC")], label=f"m1cg{t}")
            last[("oh", tin)] = cg
            if tin == TPS - 1:
                cev = B.op("DVE", lambda e, p=p: e.tensor_copy(sm[p][0:1, 32:40], pC[:]),
                           deps=[cg], label=f"m1ce{t}")
                last["pC"] = cev
                stc = B.dma("POOL", d["counts_in"][slot:slot + 1, :], sm[p][0:1, 32:40],
                            deps=[cev], label=f"m1stc{t}")
                last[("cnt", p)] = stc
            else:
                last["pC"] = cg


def _moe_m1b(B, ph):
    nc, d, sb = B.nc, B.d, B.sb
    p2 = contextlib.ExitStack()
    T2 = lambda n, s, dt=F32: B.sbuf(p2, n, s, dt)
    cag = T2("mbc", [64, E])
    sel = T2("mbs", [64, SLOTS])
    own = T2("mbo", [SLOTS, E])
    lts = T2("mbl", [SLOTS, SLOTS])
    bg = T2("mbg", [SLOTS, E])
    pB_ = B.psum(p2, "mbp", [SLOTS, E], F32)
    with p2:
        cc = B.collective("AllGather", ALU.bypass, [list(range(NCORE))],
                          d["counts_in"], d["counts_ag"], label="m1b_ag")
        l1 = B.dma("SP", cag[:], d["counts_ag"], deps=[cc], label="m1b_l1")
        l2 = B.dma("SP", sel[:], d["selT"], label="m1b_l2")
        l3 = B.dma("SP", own[:], d["counts_in"], deps=[cc], label="m1b_l3")
        l4 = B.dma("SP", lts[:], d["lts8"], label="m1b_l4")
        g1 = mm_group(B, pB_[:], [(sel[:], cag[:])], deps=[l1, l2], label="m1b_g1")
        e1 = B.op("DVE", lambda e: e.tensor_copy(bg[:], pB_[:]), deps=[g1], label="m1b_e1")
        s1 = B.dma("POOL", d["base_d"][0:SLOTS, :], bg[:], deps=[e1], label="m1b_s1")
        g2 = mm_group(B, pB_[:], [(lts[:], own[:])], deps=[l3, l4, e1], label="m1b_g2")
        e2 = B.op("DVE", lambda e: e.tensor_copy(bg[:], pB_[:]), deps=[g2, s1], label="m1b_e2")
        B.dma("POOL", d["base_d"][SLOTS:2 * SLOTS, :], bg[:], deps=[e2], label="m1b_s2")


def _moe_m2(B, ph, eidx_sb, gate_sb, gk_sb, idx_sb):
    nc, d, sb = B.nc, B.d, B.sb
    TRASH = float(E * KPAD)
    p2 = contextlib.ExitStack()
    T2 = lambda n, s, dt=F32: B.sbuf(p2, n, s, dt)
    bsg = T2("m2bg", [128, E])
    bsl = T2("m2bl", [128, E])
    oh = [T2(f"m2oh{i}", [128, E], BF16) for i in range(TPS)]
    smv = [T2(f"m2sm{i}", [128, 32]) for i in range(2)]
    xnb = [T2(f"m2xn{i}", [128, D], BF16) for i in range(2)]
    pU = B.psum(p2, "m2pU", [128, E], F32)
    pV_ = B.psum(p2, "m2pV", [128, E], F32)
    with p2:
        last = {}
        for slot in range(SLOTS):
            lg = B.dma("SP", bsg[:], d["base_d"][slot:slot + 1, :].broadcast_to([128, E]),
                       deps=[last.get("baseg")], label=f"m2lg{slot}")
            ll = B.dma("SP", bsl[:], d["base_d"][SLOTS + slot:SLOTS + slot + 1, :].broadcast_to([128, E]),
                       deps=[last.get("basel")], label=f"m2ll{slot}")
            ohops = []
            for tin in range(TPS):
                t = slot * TPS + tin
                o = B.op("DVE", lambda e, tin=tin, t=t: e.tensor_scalar(
                    out=oh[tin][:], in0=sb["iota8"][:], scalar1=eidx_sb[:, t:t + 1],
                    scalar2=None, op0=ALU.is_equal),
                    deps=[last.get(("oh", tin))], label=f"m2oh{t}")
                ohops.append(o)
            for tin in range(TPS):
                t = slot * TPS + tin
                p = tin % 2

                def emit(e, tin=tin, ps_=None):
                    last_i = None
                    last_i = e.matmul(ps_[:], sb["lt128"][:], oh[tin][:],
                                      start=True, stop=(tin == 0))
                    for q in range(tin):
                        last_i = e.matmul(ps_[:], sb["on128"][:], oh[q][:],
                                          start=False, stop=(q == tin - 1))
                    return last_i
                gg = B.op("PE", lambda e, tin=tin: emit(e, tin=tin, ps_=pU),
                          deps=ohops[:tin + 1] + [last.get(("ps", "g"))],
                          label=f"m2pfg{t}")
                gl = B.op("PE", lambda e, tin=tin: emit(e, tin=tin, ps_=pV_),
                          deps=ohops[:tin + 1] + [last.get(("ps", "l"))],
                          label=f"m2pfl{t}")
                tg = B.op("DVE", lambda e, p=p: e.tensor_add(smv[p][:, 8:16], pU[:], bsg[:]),
                          deps=[gg, lg, last.get(("tmp", p))], label=f"m2tg{t}")
                gpos = B.op("DVE", lambda e, p=p, tin=tin: e.scalar_tensor_tensor(
                    out=smv[p][:, 16:24], in0=oh[tin][:], scalar=1.0, in1=smv[p][:, 8:16],
                    op0=ALU.bypass, op1=ALU.mult, accum_out=smv[p][:, 0:1]),
                    deps=[tg], sdeps=[tg], label=f"m2pos{t}")
                last[("ps", "g")] = tg
                tl = B.op("DVE", lambda e, p=p: e.tensor_add(smv[p][:, 8:16], pV_[:], bsl[:]),
                          deps=[gl, ll, gpos], sdeps=[gpos], label=f"m2tl{t}")
                lpos = B.op("DVE", lambda e, p=p, tin=tin: e.scalar_tensor_tensor(
                    out=smv[p][:, 16:24], in0=oh[tin][:], scalar=1.0, in1=smv[p][:, 8:16],
                    op0=ALU.bypass, op1=ALU.mult, accum_out=smv[p][:, 1:2]),
                    deps=[tl], sdeps=[tl, gpos], label=f"m2lr{t}")
                last[("ps", "l")] = tl
                last[("tmp", p)] = lpos
                last["lastgl"] = gl
                last["baseg"] = tg
                last["basel"] = tl
                kp = B.op("DVE", lambda e, p=p: e.tensor_scalar(
                    out=smv[p][:, 2:3], in0=smv[p][:, 0:1], scalar1=float(CAP),
                    scalar2=None, op0=ALU.is_lt),
                    deps=[gpos], sdeps=[gpos], label=f"m2kp{t}")
                gk = B.op("DVE", lambda e, p=p, t=t: e.tensor_tensor(
                    out=gk_sb[:, t:t + 1], in0=gate_sb[:, t:t + 1], in1=smv[p][:, 2:3],
                    op=ALU.mult),
                    deps=[kp], sdeps=[kp], label=f"m2gk{t}")
                a1 = B.op("DVE", lambda e, p=p, t=t: e.scalar_tensor_tensor(
                    out=smv[p][:, 3:4], in0=eidx_sb[:, t:t + 1], scalar=float(KPAD),
                    in1=smv[p][:, 1:2], op0=ALU.mult, op1=ALU.add),
                    deps=[lpos], sdeps=[lpos], label=f"m2a{t}")
                mk = B.op("DVE", lambda e, p=p: e.tensor_scalar(
                    out=smv[p][:, 4:5], in0=smv[p][:, 1:2], scalar1=float(KPAD),
                    scalar2=None, op0=ALU.is_lt),
                    deps=[lpos], sdeps=[lpos], label=f"m2mk{t}")
                i1 = B.op("DVE", lambda e, p=p: e.tensor_scalar(
                    out=smv[p][:, 5:6], in0=smv[p][:, 3:4], scalar1=-TRASH, scalar2=None,
                    op0=ALU.add),
                    deps=[a1], sdeps=[a1], label=f"m2i1{t}")
                i2 = B.op("DVE", lambda e, p=p: e.tensor_tensor(
                    out=smv[p][:, 5:6], in0=smv[p][:, 5:6], in1=smv[p][:, 4:5], op=ALU.mult),
                    deps=[i1, mk], sdeps=[i1, mk], label=f"m2i2{t}")
                i3 = B.op("DVE", lambda e, p=p: e.tensor_scalar(
                    out=smv[p][:, 5:6], in0=smv[p][:, 5:6], scalar1=TRASH, scalar2=None,
                    op0=ALU.add),
                    deps=[i2], sdeps=[i2], label=f"m2i3{t}")
                iu = B.op("DVE", lambda e, p=p, t=t: e.tensor_copy(idx_sb[:, t:t + 1],
                                                                  smv[p][:, 5:6]),
                          deps=[i3], sdeps=[i3], label=f"m2iu{t}")
                lx = B.dma("SP", xnb[p][:], d["xnorm_d"][tsl(t, 128), :],
                           deps=[last.get(("xnb", p))], label=f"m2lx{t}")
                sc_ = B.Q["POOL"].add("dma", lambda e, p=p, t=t: e.indirect_dma_start(
                    out=d["disp_d"], out_offset=IOA(idx_sb[:, t:t + 1], 0),
                    in_=xnb[p][:], in_offset=None),
                    deps=[iu, lx], label=f"m2sc{t}")
                last[("xnb", p)] = sc_
            for tin in range(TPS):
                last[("oh", tin)] = last["lastgl"]


def _moe_m3(B, ph):
    nc, d, sb = B.nc, B.d, B.sb
    p2 = contextlib.ExitStack()
    T2 = lambda n, s, dt=F32: B.sbuf(p2, n, s, dt)
    xdT = [T2(f"m3x{i}", [128, D], BF16) for i in range(KTILES)]
    xrow = [T2(f"m3r{i}", [128, D], BF16) for i in range(2)]
    w1 = T2("m3w1", [128, DC * 512], BF16)
    w3 = T2("m3w3", [128, DC * 512], BF16)
    w2 = T2("m3w2", [128, HC * D], BF16)
    hsil = [T2(f"m3h{i}", [128, 512], BF16) for i in range(2)]
    htmp = [T2(f"m3ht{i}", [128, 512], BF16) for i in range(2)]
    hT = [T2(f"m3hT{i}", [128, HC * 128], BF16) for i in range(KTILES)]
    oe = [T2(f"m3oe{i}", [128, D], BF16) for i in range(2)]
    pH = [B.psum(p2, f"m3pH{i}", [128, 512], F32) for i in range(4)]
    pTT = [B.psum(p2, f"m3pT{i}", [128, 1024], BF16) for i in range(2)]
    pO = [B.psum(p2, f"m3pO{i}", [128, 512], F32) for i in range(2)]
    with p2:
        last = {}
        for ex in range(E):
            w2l = B.dma("POOL", w2[:].rearrange("p (c n) -> p c n", c=HC),
                        d["W2"][ex].rearrange("(c dd) n -> dd c n", dd=128),
                        deps=[last.get("w2")], label=f"m3w2{ex}")
            for kt in range(KTILES):
                p = kt % 2
                lr = B.dma("SP", xrow[p][:], d["disp_d"][tsl(ex * KTILES + kt, 128), :],
                           deps=[last.get(("xr", p))], label=f"m3lr{ex}_{kt}")
                evs = []
                for half in range(2):
                    g = tr_group(B, [(pTT[half][:, tsl(i, 128)],
                                      xrow[p][:, tsl(half * 8 + i, 128)]) for i in range(8)],
                                 sb["ident"][:],
                                 deps=[lr, last.get(("pTT", half))], label=f"m3tr{ex}_{kt}_{half}")
                    ev = B.op("DVE", lambda e, kt=kt, half=half: e.tensor_copy(
                        xdT[kt][:, tsl(half, 1024)], pTT[half][:]),
                        deps=[g, last.get(("xdT", kt))], label=f"m3te{ex}_{kt}_{half}")
                    last[("pTT", half)] = ev
                    evs.append(ev)
                last[("xr", p)] = evs[-1]
                last[("xdTw", kt)] = evs
            hc0 = 0
            for nh, ncols in enumerate(HCH):
                l1 = B.dma("POOL", w1[:, 0:DC * ncols].rearrange("p (c n) -> p c n", c=DC),
                           d["W1"][ex].rearrange("(c dd) n -> dd c n", dd=128)[:, :, nh * 512: nh * 512 + ncols],
                           deps=[last.get("w1")], label=f"m3l1{ex}_{nh}")
                l3 = B.dma("POOL", w3[:, 0:DC * ncols].rearrange("p (c n) -> p c n", c=DC),
                           d["W3"][ex].rearrange("(c dd) n -> dd c n", dd=128)[:, :, nh * 512: nh * 512 + ncols],
                           deps=[last.get("w3")], label=f"m3l3{ex}_{nh}")
                nturn = ncols // 128
                g1l = g3l = None
                for kt in range(KTILES):
                    hp_ = kt % 2
                    b0 = hp_ * 2
                    g1 = mm_group(B, pH[b0][:, 0:ncols],
                                  [(xdT[kt][:, tsl(c, 128)], w1[:, c * ncols: (c + 1) * ncols])
                                   for c in range(DC)],
                                  deps=last[("xdTw", kt)] + [l1, last.get(("pH", b0))],
                                  label=f"m3g1{ex}_{nh}_{kt}")
                    g3 = mm_group(B, pH[b0 + 1][:, 0:ncols],
                                  [(xdT[kt][:, tsl(c, 128)], w3[:, c * ncols: (c + 1) * ncols])
                                   for c in range(DC)],
                                  deps=last[("xdTw", kt)] + [l3, last.get(("pH", b0 + 1))],
                                  label=f"m3g3{ex}_{nh}_{kt}")
                    sil = B.op("ACT", lambda e, hp_=hp_, b0=b0, ncols=ncols: e.activation(
                        hsil[hp_][:, 0:ncols], pH[b0][:, 0:ncols], AF.Silu),
                        deps=[g1, last.get(("hsil", hp_))], label=f"m3si{ex}_{nh}_{kt}")
                    hp = B.op("DVE", lambda e, hp_=hp_, b0=b0, ncols=ncols: e.tensor_tensor(
                        out=htmp[hp_][:, 0:ncols], in0=pH[b0 + 1][:, 0:ncols],
                        in1=hsil[hp_][:, 0:ncols], op=ALU.mult),
                        deps=[g3, sil, last.get(("htmp", hp_))], label=f"m3hp{ex}_{nh}_{kt}")
                    last[("pH", b0)] = hp
                    last[("pH", b0 + 1)] = hp
                    last[("hsil", hp_)] = hp
                    g = tr_group(B, [(pTT[hp_][:, tsl(i, 128)], htmp[hp_][:, tsl(i, 128)])
                                     for i in range(nturn)],
                                 sb["ident"][:],
                                 deps=[hp, last.get(("pTT", hp_))], label=f"m3trh{ex}_{nh}_{kt}")
                    ev = B.op("DVE", lambda e, kt=kt, hc0=hc0, nturn=nturn, hp_=hp_: e.tensor_copy(
                        hT[kt][:, hc0 * 128:(hc0 + nturn) * 128], pTT[hp_][:, 0:nturn * 128]),
                        deps=[g, last.get(("hT", kt))], label=f"m3the{ex}_{nh}_{kt}")
                    last[("pTT", hp_)] = ev
                    last[("htmp", hp_)] = g
                    last.setdefault(("hTw", kt), []).append(ev)
                last["w1"] = g1
                last["w3"] = g3
                hc0 += nturn
            for kt in range(KTILES):
                p = kt % 2
                oevs = []
                for n2 in range(4):
                    pb = n2 % 2
                    gO = mm_group(B, pO[pb][:],
                                  [(hT[kt][:, tsl(hc, 128)], w2[:, hc * D + n2 * 512: hc * D + n2 * 512 + 512])
                                   for hc in range(HC)],
                                  deps=last[("hTw", kt)] + [w2l, last.get(("pO", pb))],
                                  label=f"m3go{ex}_{kt}_{n2}")
                    ev = B.op("DVE", lambda e, p=p, n2=n2, pb=pb: e.tensor_copy(
                        oe[p][:, tsl(n2, 512)], pO[pb][:]),
                        deps=[gO, last.get(("oe", p))], label=f"m3oe{ex}_{kt}_{n2}")
                    last[("pO", pb)] = ev
                    oevs.append(ev)
                st = B.dma("POOL", d["dispo_d"][tsl(ex * KTILES + kt, 128), :], oe[p][:],
                           deps=oevs, label=f"m3st{ex}_{kt}")
                last[("oe", p)] = st
                last[("hT", kt)] = gO
            last["w2"] = gO
            for kt in range(KTILES):
                last[("xdT", kt)] = last[("hTw", kt)][-1]
                del last[("hTw", kt)]
        B.dma("POOL", d["dispo_d"][E * KPAD:E * KPAD + 128, :], sb["zs"][:],
              deps=[], label="m3ztrash")


def _moe_m4(B, ph, src, dst, gk_sb, idx_sb):
    nc, d, sb = B.nc, B.d, B.sb
    p2 = contextlib.ExitStack()
    T2 = lambda n, s, dt=F32: B.sbuf(p2, n, s, dt)
    yTt = [T2(f"m4y{i}", [128, D], BF16) for i in range(2)]
    w1 = T2("m4w1", [128, DC * 512], BF16)
    w3 = T2("m4w3", [128, DC * 512], BF16)
    w2 = T2("m4w2", [128, HC * D], BF16)
    hsil = [T2(f"m4h{i}", [128, 512], BF16) for i in range(2)]
    htmp = [T2(f"m4ht{i}", [128, 512], BF16) for i in range(2)]
    hTt = [T2(f"m4hT{i}", [128, HC * 128], BF16) for i in range(2)]
    mo = [T2(f"m4mo{i}", [128, D], BF16) for i in range(2)]
    xs = [T2(f"m4x{i}", [128, D]) for i in range(2)]
    xn = [T2(f"m4xn{i}", [128, D]) for i in range(2)]
    pH = [B.psum(p2, f"m4pH{i}", [128, 512], F32) for i in range(4)]
    pTT = [B.psum(p2, f"m4pT{i}", [128, 1024], BF16) for i in range(2)]
    pO = [B.psum(p2, f"m4pO{i}", [128, 512], F32) for i in range(2)]
    with p2:
        last = {}
        hc0 = 0
        for nh, ncols in enumerate(HCH):
            nturn = ncols // 128
            l1 = B.dma("POOL", w1[:, 0:DC * ncols].rearrange("p (c n) -> p c n", c=DC),
                       d["sw1"].rearrange("(c dd) n -> dd c n", dd=128)[:, :, nh * 512: nh * 512 + ncols],
                       deps=[last.get("w1")], label=f"m4l1{nh}")
            l3 = B.dma("POOL", w3[:, 0:DC * ncols].rearrange("p (c n) -> p c n", c=DC),
                       d["sw3"].rearrange("(c dd) n -> dd c n", dd=128)[:, :, nh * 512: nh * 512 + ncols],
                       deps=[last.get("w3")], label=f"m4l3{nh}")
            for t in range(NT):
                p = t % 2
                ly = B.dma("SP", yTt[p][:].rearrange("p (c n) -> p c n", c=DC),
                           d["yT_d"][t].rearrange("(c dd) n -> dd c n", dd=128),
                           deps=[last.get(("yT", p))], label=f"m4ly{nh}_{t}")
                b0 = p * 2
                g1 = mm_group(B, pH[b0][:, 0:ncols],
                              [(yTt[p][:, tsl(c, 128)], w1[:, c * ncols:(c + 1) * ncols])
                               for c in range(DC)],
                              deps=[ly, l1, last.get(("pH", b0))], label=f"m4g1{nh}_{t}")
                g3 = mm_group(B, pH[b0 + 1][:, 0:ncols],
                              [(yTt[p][:, tsl(c, 128)], w3[:, c * ncols:(c + 1) * ncols])
                               for c in range(DC)],
                              deps=[ly, l3, last.get(("pH", b0 + 1))], label=f"m4g3{nh}_{t}")
                last[("yT", p)] = g3
                sil = B.op("ACT", lambda e, p=p, b0=b0, ncols=ncols: e.activation(
                    hsil[p][:, 0:ncols], pH[b0][:, 0:ncols], AF.Silu),
                    deps=[g1, last.get(("hsil", p))], label=f"m4si{nh}_{t}")
                hp = B.op("DVE", lambda e, p=p, b0=b0, ncols=ncols: e.tensor_tensor(
                    out=htmp[p][:, 0:ncols], in0=pH[b0 + 1][:, 0:ncols],
                    in1=hsil[p][:, 0:ncols], op=ALU.mult),
                    deps=[g3, sil, last.get(("htmp", p))], label=f"m4hp{nh}_{t}")
                last[("pH", b0)] = hp
                last[("pH", b0 + 1)] = hp
                last[("hsil", p)] = hp
                g = tr_group(B, [(pTT[p][:, tsl(i, 128)], htmp[p][:, tsl(i, 128)])
                                 for i in range(nturn)],
                             sb["ident"][:],
                             deps=[hp, last.get(("pTT", p))], label=f"m4trh{nh}_{t}")
                ev = B.op("DVE", lambda e, p=p, nturn=nturn: e.tensor_copy(
                    hTt[p][:, 0:nturn * 128], pTT[p][:, 0:nturn * 128]),
                    deps=[g, last.get(("hTt", p))], label=f"m4the{nh}_{t}")
                last[("pTT", p)] = ev
                last[("htmp", p)] = g
                sh = B.dma("POOL", d["hT_d"][t][hc0 * 128:(hc0 + nturn) * 128, :]
                           .rearrange("(c dd) n -> dd c n", dd=128),
                           hTt[p][:, 0:nturn * 128].rearrange("p (c n) -> p c n", c=nturn),
                           deps=[ev], label=f"m4sh{nh}_{t}")
                last[("hTt", p)] = sh
            last["w1"] = g1
            last["w3"] = g3
            hc0 += nturn
        w2l = B.dma("POOL", w2[:].rearrange("p (c n) -> p c n", c=HC),
                    d["sw2"].rearrange("(c dd) n -> dd c n", dd=128),
                    label="m4w2")
        hTf = [T2(f"m4hf{i}", [128, HC * 128], BF16) for i in range(2)]
        for t in range(NT):
            p = t % 2
            lh = B.dma("SP", hTf[p][:].rearrange("p (c n) -> p c n", c=HC),
                       d["hT_d"][t].rearrange("(c dd) n -> dd c n", dd=128),
                       deps=[last.get(("hTf", p))], label=f"m4lh{t}")
            lx = B.dma("SP", xs[p][:], d[src][tsl(t, 128), :],
                       deps=[last.get(("x", p))], label=f"m4lx{t}")
            lmo = B.Q["POOL"].add("dma", lambda e, p=p, t=t: e.indirect_dma_start(
                out=mo[p][:], out_offset=None,
                in_=d["dispo_d"], in_offset=IOA(idx_sb[:, t:t + 1], 0)),
                deps=[last.get(("mo", p))], label=f"m4gm{t}")
            combs = []
            for n2 in range(4):
                pb = n2 % 2
                gO = mm_group(B, pO[pb][:],
                              [(hTf[p][:, tsl(hc, 128)], w2[:, hc * D + n2 * 512: hc * D + n2 * 512 + 512])
                               for hc in range(HC)],
                              deps=[lh, w2l, last.get(("pO", pb))], label=f"m4go{t}_{n2}")
                c1 = B.op("DVE", lambda e, p=p, n2=n2, pb=pb, t=t: e.scalar_tensor_tensor(
                    out=xn[p][:, tsl(n2, 512)], in0=mo[p][:, tsl(n2, 512)],
                    scalar=gk_sb[:, t:t + 1], in1=pO[pb][:], op0=ALU.mult, op1=ALU.add),
                    deps=[gO, lmo, last.get(("xn", p))], label=f"m4c1{t}_{n2}")
                c2 = B.op("DVE", lambda e, p=p, n2=n2: e.tensor_add(
                    xn[p][:, tsl(n2, 512)], xn[p][:, tsl(n2, 512)], xs[p][:, tsl(n2, 512)]),
                    deps=[c1, lx], sdeps=[c1], label=f"m4c2{t}_{n2}")
                last[("pO", pb)] = c2
                combs.append(c2)
            last[("hTf", p)] = gO
            last[("x", p)] = combs[-1]
            last[("mo", p)] = combs[-1]
            st = B.dma("POOL", d[dst][tsl(t, 128), :], xn[p][:], deps=combs, label=f"m4st{t}")
            last[("xn", p)] = st


def phase_b1(B):
    nc, d, sb = B.nc, B.d, B.sb
    p2 = contextlib.ExitStack()
    T2 = lambda n, s, dt=F32: B.sbuf(p2, n, s, dt)
    xsb = [T2(f"b1x{i}", [128, D]) for i in range(2)]
    ysb = [T2(f"b1y{i}", [128, D]) for i in range(2)]
    xnb = [T2(f"b1n{i}", [128, D]) for i in range(2)]
    xbf = [T2(f"b1b{i}", [128, D], BF16) for i in range(2)]
    sc1 = [T2(f"b1s1{i}", [128, D]) for i in range(2)]
    sm = [T2(f"b1sm{i}", [128, 16]) for i in range(2)]
    gam = T2("b1g", [128, D])
    bet = T2("b1be", [128, D])
    xnT = T2("b1xT", [128, DC * 2048], BF16)
    wsb = [T2(f"b1w{i}", [128, DC * 512], BF16) for i in range(2)]
    ev_sb = [T2(f"b1qe{i}", [128, 512], BF16) for i in range(2)]
    pT = [B.psum(p2, f"b1pT{i}", [128, 1024], BF16) for i in range(2)]
    pP = [B.psum(p2, f"b1pP{i}", [128, 512], F32) for i in range(2)]
    with p2:
        lgm = B.dma("SP", gam[:], d["gammaI"].broadcast_to([128, D]), label="b1lg")
        lbt = B.dma("SP", bet[:], d["betaI"].broadcast_to([128, D]), label="b1lb")
        last = {}
        B.kb_cc, B.vb_cc = {}, {}
        for g2 in range(2):
            tevs = []
            for ti in range(16):
                t = g2 * 16 + ti
                p = t % 2
                lx = B.dma("SP", xsb[p][:], d["resid"][tsl(t, 128), :],
                           deps=[last.get(("x", p))], label=f"b1lx{t}")
                yop = ln_tile(B, sb, xsb[p][:], ysb[p][:], sc1[p][:],
                              sm[p][:, 1:2], sm[p][:, 3:4], sm[p][:, 4:5],
                              sm[p][:, 0:1], sm[p][:, 2:3], lx,
                              y_war=last.get(("y", p)), label=f"b1ln{t}")
                last[("x", p)] = yop
                sy = B.dma("POOL", d["y_d"][tsl(t, 128), :], ysb[p][:],
                           deps=[yop], label=f"b1sy{t}")
                yop2 = ln_tile(B, sb, ysb[p][:], xnb[p][:], sc1[p][:],
                               sm[p][:, 5:6], sm[p][:, 6:7], sm[p][:, 7:8],
                               sm[p][:, 8:9], sm[p][:, 9:10], yop,
                               y_war=last.get(("xnb", p)), scr_war=yop,
                               label=f"b1li{t}")
                last[("y", p)] = [yop2, sy]
                ga = B.op("DVE", lambda e, p=p: e.tensor_mul(xnb[p][:], xnb[p][:], gam[:]),
                          deps=[yop2, lgm], sdeps=[yop2], label=f"b1ga{t}")
                be = B.op("DVE", lambda e, p=p: e.tensor_add(xnb[p][:], xnb[p][:], bet[:]),
                          deps=[ga, lbt], sdeps=[ga], label=f"b1be{t}")
                cop = B.op("ACT", lambda e, p=p: e.copy(xbf[p][:], xnb[p][:]),
                           deps=[be, last.get(("xbf", p))], label=f"b1c{t}")
                last[("xnb", p)] = cop
                for half in range(2):
                    g = tr_group(B, [(pT[half][:, tsl(i, 128)],
                                      xbf[p][:, tsl(half * 8 + i, 128)]) for i in range(8)],
                                 sb["ident"][:],
                                 deps=[cop, last.get(("pT", half))], label=f"b1tr{t}_{half}")
                    ev = B.op("DVE", lambda e, ti=ti, half=half: e.tensor_copy(
                        xnT[:].rearrange("p (c n) -> p c n", c=DC)
                        [:, tsl(half, 8), ti * 128: ti * 128 + 128],
                        pT[half][:].rearrange("p (c n) -> p c n", c=8)),
                        deps=[g, last.get("xnTw")], label=f"b1ev{t}_{half}")
                    last[("pT", half)] = ev
                    last[("xbf", p)] = g
                    tevs.append(ev)
            wlast = None
            for slab in range(12):
                wp = slab % 2
                wl = B.dma("POOL", wsb[wp][:].rearrange("p (c n) -> p c n", c=DC),
                           d["qkvw"].rearrange("(c dd) n -> dd c n", dd=128)[:, :, tsl(slab, 512)],
                           deps=[last.get(("w", wp))], label=f"b1w{g2}_{slab}")
                if slab < 8:
                    for ob in range(4):
                        for tsl_ in range(4):
                            pb = (ob * 4 + tsl_) % 2
                            grp = mm_group(B, pP[pb][:],
                                           [(wsb[wp][:, c * 512 + ob * 128: c * 512 + ob * 128 + 128],
                                             xnT[:, c * 2048 + tsl_ * 512: c * 2048 + tsl_ * 512 + 512])
                                            for c in range(DC)],
                                           deps=tevs + [wl, last.get(("pP", pb))],
                                           label=f"b1g{g2}_{slab}_{ob}_{tsl_}")
                            ev = B.op("DVE", lambda e, pb=pb: e.tensor_copy(
                                ev_sb[pb][:], pP[pb][:]),
                                deps=[grp, last.get(("evs", pb))], label=f"b1pe{g2}_{slab}_{ob}_{tsl_}")
                            last[("pP", pb)] = ev
                            o0 = slab * 512 + ob * 128 if slab < 4 else (slab - 4) * 512 + ob * 128
                            if slab < 4:
                                sts = []
                                for i in range(4):
                                    tile = g2 * 16 + tsl_ * 4 + i
                                    st = B.dma("POOL",
                                               d["QTb"][tile][o0:o0 + 128, :],
                                               ev_sb[pb][:, tsl(i, 128)],
                                               deps=[ev], label=f"b1sq{g2}_{slab}_{ob}_{tsl_}_{i}")
                                    sts.append(st)
                                last[("evs", pb)] = sts[-1]
                            else:
                                blk = g2 * 2 + tsl_ // 2
                                col0 = (tsl_ % 2) * 512
                                st = B.dma("POOL", d[f"kb_in{blk}"][o0:o0 + 128, col0:col0 + 512],
                                           ev_sb[pb][:], deps=[ev],
                                           label=f"b1sk{g2}_{slab}_{ob}_{tsl_}")
                                last[("evs", pb)] = st
                                last.setdefault(("kst", blk), []).append(st)
                else:
                    for ti in range(16):
                        pb = ti % 2
                        grp = mm_group(B, pP[pb][:],
                                       [(xnT[:, c * 2048 + ti * 128: c * 2048 + ti * 128 + 128],
                                         wsb[wp][:, tsl(c, 512)])
                                        for c in range(DC)],
                                       deps=tevs + [wl, last.get(("pP", pb))],
                                       label=f"b1v{g2}_{slab}_{ti}")
                        ev = B.op("DVE", lambda e, pb=pb: e.tensor_copy(ev_sb[pb][:], pP[pb][:]),
                                  deps=[grp, last.get(("evs", pb))], label=f"b1ve{g2}_{slab}_{ti}")
                        last[("pP", pb)] = ev
                        blk = g2 * 2 + ti // 8
                        r0 = (ti % 8) * 128
                        st = B.dma("POOL", d[f"vb_in{blk}"][r0:r0 + 128, tsl(slab - 8, 512)],
                                   ev_sb[pb][:], deps=[ev], label=f"b1sv{g2}_{slab}_{ti}")
                        last[("evs", pb)] = st
                        last.setdefault(("vst", blk), []).append(st)
                last[("w", wp)] = grp
                wlast = grp
            last["xnTw"] = wlast
            for blk in (g2 * 2, g2 * 2 + 1):
                B.kb_cc[blk] = B.collective("AllGather", ALU.bypass,
                                            [[0, 1], [2, 3], [4, 5], [6, 7]],
                                            d[f"kb_in{blk}"], d[f"kbag{blk}"],
                                            deps=last[("kst", blk)], label=f"b1agk{blk}")
                B.vb_cc[blk] = B.collective("AllGather", ALU.bypass,
                                            [[0, 1], [2, 3], [4, 5], [6, 7]],
                                            d[f"vb_in{blk}"], d[f"vbag{blk}"],
                                            deps=last[("vst", blk)], label=f"b1agv{blk}")
                if f"kbag{blk}_dbg" in d:
                    B.dma("POOL", d[f"kbag{blk}_dbg"], d[f"kbag{blk}"],
                          deps=[B.kb_cc[blk]], label=f"dbgk{blk}")
                if f"vbag{blk}_dbg" in d:
                    B.dma("POOL", d[f"vbag{blk}_dbg"], d[f"vbag{blk}"],
                          deps=[B.vb_cc[blk]], label=f"dbgv{blk}")


_NC_CACHE = {}


def _get_nc(upto):
    if upto not in _NC_CACHE:
        _NC_CACHE[upto] = build_nc(upto)
    return _NC_CACHE[upto]


def _host_constants():
    c = {}
    c["ident"] = np.eye(128, dtype=ml_dtypes.bfloat16)
    c["lt128"] = np.triu(np.ones((128, 128), np.float32), k=1)
    c["ones128"] = np.ones((128, 128), np.float32)
    c["lts8"] = np.triu(np.ones((SLOTS, SLOTS), np.float32), k=1).T.copy()
    c["lts8"] = np.triu(np.ones((SLOTS, SLOTS), np.float32), k=1)
    return c


def _core_constants(core):
    b, par = core // 2, core % 2
    cset = SETS[par]
    pos = np.concatenate([np.arange(c * CH, (c + 1) * CH) for c in cset]).astype(np.float64)
    idx = np.arange(HD // 2, dtype=np.float64)
    inv_freq = 1.0 / ROPE_THETA ** (2.0 * idx / HD)
    fr = pos[:, None] * inv_freq[None, :]
    cosT = np.cos(fr).astype(np.float32)
    sinT = np.sin(fr).astype(np.float32)
    mask = np.zeros((SLOTS * CH, 1024), np.float32)
    for j in range(SLOTS):
        cg = cset[j]
        bound = (j + 1) * 1024
        k_global = np.arange(bound - 1024, bound)
        q_global = cg * CH + np.arange(CH)
        mask[j * CH:(j + 1) * CH, :] = (k_global[None, :] <= q_global[:, None])
    sel = np.zeros((NCORE * SLOTS, SLOTS), np.float32)
    for r in range(NCORE):
        for jj in range(SLOTS):
            gc_r = (r // 2) * NCH + SETS[r % 2][jj]
            for j in range(SLOTS):
                gc_me = b * NCH + cset[j]
                if gc_r < gc_me:
                    sel[r * SLOTS + jj, j] = 1.0
    return dict(cosT=cosT, sinT=sinT, maskT=mask, selT=sel), cset


def prep_inmaps(L0_attn_a_x, L0_attn_a_q_proj, L0_attn_a_k_proj, L0_attn_a_v_proj,
                L0_attn_a_o_proj, L0_attn_a_q_norm, L0_attn_a_k_norm,
                L0_attn_b_qkv, L0_attn_b_o_proj, L0_attn_b_gamma, L0_attn_b_beta,
                L0_ffn_Wg, L0_ffn_W1, L0_ffn_W3, L0_ffn_W2,
                L0_ffn_sw1, L0_ffn_sw3, L0_ffn_sw2):
    cst = _host_constants()
    asc = np.ascontiguousarray
    shared = {
        "qTw": asc(np.asarray(L0_attn_a_q_proj).T),
        "kTw": asc(np.asarray(L0_attn_a_k_proj).T),
        "vTw": asc(np.asarray(L0_attn_a_v_proj).T),
        "oTw": asc(np.asarray(L0_attn_a_o_proj).T),
        "qkvw": asc(np.asarray(L0_attn_b_qkv)),
        "obw": asc(np.asarray(L0_attn_b_o_proj)),
        "Wg": asc(np.asarray(L0_ffn_Wg)),
        "W1": asc(np.asarray(L0_ffn_W1)),
        "W3": asc(np.asarray(L0_ffn_W3)),
        "W2": asc(np.asarray(L0_ffn_W2)),
        "sw1": asc(np.asarray(L0_ffn_sw1)),
        "sw3": asc(np.asarray(L0_ffn_sw3)),
        "sw2": asc(np.asarray(L0_ffn_sw2)),
        "qgain": asc(np.asarray(L0_attn_a_q_norm).reshape(1, HD)),
        "kgain": asc(np.asarray(L0_attn_a_k_norm).reshape(1, HD)),
        "gammaI": asc(np.asarray(L0_attn_b_gamma).reshape(1, D)),
        "betaI": asc(np.asarray(L0_attn_b_beta).reshape(1, D)),
    }
    shared.update({k: v for k, v in cst.items()})
    x = np.asarray(L0_attn_a_x)
    in_maps = []
    csets = []
    for core in range(NCORE):
        b, par = core // 2, core % 2
        cc, cset = _core_constants(core)
        csets.append(cset)
        xc = x[b].reshape(NCH, CH, D)[cset].reshape(TOK, D)
        m = dict(shared)
        m.update(cc)
        m["x_in"] = asc(xc)
        in_maps.append(m)
    return in_maps, csets


def assemble(res_list, csets, key="out_x"):
    out = np.empty((BATCH, SEQ, D), np.float32)
    for core in range(NCORE):
        b = core // 2
        oc = np.asarray(res_list[core][key]).reshape(SLOTS, CH, D)
        for j, cg in enumerate(csets[core]):
            out[b, cg * CH:(cg + 1) * CH] = oc[j]
    return out


def kernel(**inputs):
    nc = _get_nc(UPTO)
    in_maps, csets = prep_inmaps(**inputs)
    res = run_bass_kernel_spmd(nc, in_maps, core_ids=list(range(NCORE)))
    return assemble(res.results, csets)


# version 04 (final)
# speedup vs baseline: 1.1036x; 1.0959x over previous
import contextlib
import numpy as np
import ml_dtypes
import concourse.bass as bass
import concourse.mybir as mybir
from concourse.bass_utils import run_bass_kernel_spmd

F32 = mybir.dt.float32
BF16 = mybir.dt.bfloat16
U32 = mybir.dt.uint32
AF = mybir.ActivationFunctionType
ALU = mybir.AluOpType
AX = mybir.AxisListType
IOA = bass.IndirectOffsetOnAxis

BATCH, SEQ, D = 4, 8192, 2048
NCORE = 8
NCH = 16
CH = SEQ // NCH
TOK = SEQ * BATCH // NCORE
NT = TOK // 128
SLOTS = 8
TPS = CH // 128
DC = D // 128
HEADS, HD = 8, 256
ROPE_THETA = 1.0e6
E, HFF = 8, 2816
HCH = [512] * 5 + [256]
NHC = len(HCH)
HC = HFF // 128
CAP = 4096
KPAD = 640
KTILES = KPAD // 128
DISP_ROWS = E * KPAD + 128
EPS_LN = 1e-5
EPS_RMS = 1e-6
A_SET = [0, 3, 4, 7, 8, 11, 12, 15]
B_SET = [1, 2, 5, 6, 9, 10, 13, 14]
SETS = (A_SET, B_SET)

UPTO = 4
STRICT_EDGES = True


def tsl(i, n):
    return slice(i * n, (i + 1) * n)


class Op:
    __slots__ = ("q", "idx", "kind", "emit", "deps", "sdeps", "label")

    def __init__(self, q, idx, kind, emit, deps, sdeps, label):
        self.q, self.idx, self.kind = q, idx, kind
        self.emit, self.deps, self.sdeps, self.label = emit, deps, sdeps, label


class Queue:
    NROT = 8

    def __init__(self, B, name, attr):
        self.B, self.name, self.attr = B, name, attr
        self.ops = []
        self.sem = B.es.enter_context(B.nc.semaphore(f"q_{name}"))
        self.rsem = [B.es.enter_context(B.nc.semaphore(f"r_{name}{i}"))
                     for i in range(self.NROT)]
        self.n_dma = 0
        self.n_comp = 0

    def add(self, kind, emit, deps=(), sdeps=(), label=""):
        deps = [d for d in deps if d is not None]
        sdeps = [d for d in sdeps if d is not None]
        if kind == "dma":
            idx = self.n_dma
            self.n_dma += 1
        else:
            idx = self.n_comp
            self.n_comp += 1
        o = Op(self, idx, kind, emit, deps, sdeps, label)
        self.ops.append(o)
        return o


class Builder:
    def __init__(self, nc, es):
        self.nc, self.es = nc, es
        self.d = {}
        self.Q = {}
        self._uid = 0
        for name, attr in (("PE", "tensor"), ("DVE", "vector"), ("ACT", "scalar"),
                           ("POOL", "gpsimd"), ("SP", "sync")):
            self.Q[name] = Queue(self, name, attr)

    def sbuf(self, ph, name, shape, dt=F32):
        self._uid += 1
        return ph.enter_context(self.nc.sbuf_tensor(f"{name}_{self._uid}", shape, dt))

    def psum(self, ph, name, shape, dt=F32):
        self._uid += 1
        return ph.enter_context(self.nc.psum_tensor(f"{name}_{self._uid}", shape, dt))

    def dma(self, qname, out, in_, deps=(), label="", **kw):
        q = self.Q[qname]
        return q.add("dma", lambda e, out=out, in_=in_: e.dma_start(out=out, in_=in_, **kw),
                     deps=deps, label=label)

    def op(self, qname, emit, deps=(), sdeps=(), label=""):
        return self.Q[qname].add("comp", emit, deps=deps, sdeps=sdeps, label=label)

    def finalize(self, block):
        for q in self.Q.values():
            self._emit_queue(block, q)

    def _emit_queue(self, block, q):
        if not q.ops:
            return
        section = getattr(block, q.attr)

        def body(eng):
            hw = {}

            def wait(sem, val):
                k = id(sem)
                if hw.get(k, 0) >= val:
                    return
                eng.wait_ge(sem, val)
                hw[k] = val

            for o in q.ops:
                if o.kind == "dma" and o.idx >= Queue.NROT:
                    wait(q.rsem[o.idx % Queue.NROT], 16 * (o.idx // Queue.NROT))
                deps = list(o.deps) + list(o.sdeps)
                for didx, dep in enumerate(deps):
                    same_q = dep.q is q
                    forced = didx >= len(o.deps)
                    if (same_q and not forced and not STRICT_EDGES
                            and dep.kind == "comp"):
                        continue
                    if dep.kind == "dma":
                        sem = dep.q.rsem[dep.idx % Queue.NROT]
                        val = 16 * (dep.idx // Queue.NROT + 1)
                    elif dep.kind == "cc":
                        sem = self.cc_sem
                        val = dep.idx + 1
                    else:
                        sem = dep.q.sem
                        val = dep.idx + 1
                    wait(sem, val)
                ins = o.emit(eng)
                if o.kind == "dma":
                    ins.then_inc(q.rsem[o.idx % Queue.NROT], 16)
                elif o.kind == "cc":
                    ins.then_inc(self.cc_sem, 1)
                else:
                    ins.then_inc(q.sem, 1)

        section(body)

    def collective(self, kind, op, groups, ins_ap, outs_ap, deps=(), label=""):
        q = self.Q["POOL"]
        if not hasattr(self, "cc_sem"):
            self.cc_sem = self.es.enter_context(self.nc.semaphore("cc"))
            self.cc_n = 0
            self.cc_last = None
        idx = self.cc_n
        self.cc_n += 1
        deps = [d for d in deps if d is not None]
        if self.cc_last is not None:
            deps.append(self.cc_last)
        o = Op(q, idx, "cc",
               lambda e: e.collective_compute(kind, op, replica_groups=groups,
                                              ins=[ins_ap], outs=[outs_ap]),
               deps, [], label)
        q.ops.append(o)
        self.cc_last = o
        return o

    def fence(self, label="fence"):
        lasts = []
        for q in self.Q.values():
            lastc = next((o for o in reversed(q.ops) if o.kind != "dma"), None)
            lastd = next((o for o in reversed(q.ops) if o.kind == "dma"), None)
            if lastc is not None:
                lasts.append(lastc)
            if lastd is not None:
                lasts.append(lastd)
        for q in self.Q.values():
            q.add("comp", lambda e: e.nop(), deps=list(lasts), label=label)


def mm_group(B, ps_ap, pairs, deps, label=""):
    n = len(pairs)

    def emit(e):
        last = None
        for i, (l, r) in enumerate(pairs):
            last = e.matmul(ps_ap, l, r, start=(i == 0), stop=(i == n - 1))
        return last
    return B.op("PE", emit, deps=deps, label=label)


def tr_group(B, items, ident, deps, label=""):
    def emit(e):
        last = None
        for o, i in items:
            last = e.transpose(o, i, ident)
        return last
    return B.op("PE", emit, deps=deps, label=label)


def rsqrt(B, out_ap, in_ap, eps, scale, dep, label="rsqrt"):
    eps_ap = B.sb["eps"][:, 0:1] if eps == EPS_LN else B.sb["eps"][:, 1:2]
    o1 = B.op("ACT", lambda e: e.activation(out_ap, in_ap, AF.Ln, bias=eps_ap, scale=scale),
              deps=[dep, B.eps_dep], label=label + "_ln")
    return B.op("ACT", lambda e: e.activation(out_ap, out_ap, AF.Exp, scale=-0.5),
                deps=[o1], sdeps=[o1], label=label + "_exp")


def ln_tile(B, sb, x_ap, y_ap, scratch_ap, mean_ap, var_ap, rstd_ap, sum_ap, ss_ap,
            x_dep, y_war=None, scr_war=None, label="ln"):
    s1 = B.op("DVE", lambda e: e.reduce_sum(sum_ap, x_ap, axis=AX.X),
              deps=[x_dep], label=label + "_sum")
    s2 = B.op("DVE", lambda e: e.scalar_tensor_tensor(out=scratch_ap, in0=x_ap, scalar=1.0,
                                                      in1=x_ap, op0=ALU.bypass, op1=ALU.mult,
                                                      accum_out=ss_ap),
              deps=[x_dep, scr_war], label=label + "_ss")
    m = B.op("DVE", lambda e: e.tensor_scalar(out=mean_ap, in0=sum_ap, scalar1=1.0 / D,
                                              scalar2=None, op0=ALU.mult),
             deps=[s1], sdeps=[s1], label=label + "_mean")
    m2v = B.op("DVE", lambda e: e.tensor_tensor(out=var_ap, in0=mean_ap, in1=mean_ap, op=ALU.mult),
               deps=[m], sdeps=[m], label=label + "_m2")
    var = B.op("DVE", lambda e: e.scalar_tensor_tensor(out=var_ap, in0=ss_ap, scalar=1.0 / D,
                                                       in1=var_ap, op0=ALU.mult, op1=ALU.subtract),
               deps=[s2, m2v], sdeps=[s2, m2v], label=label + "_var")
    r = rsqrt(B, rstd_ap, var_ap, EPS_LN, 1.0, var, label=label + "_r")
    yw = y_war if isinstance(y_war, (list, tuple)) else [y_war]
    y = B.op("DVE", lambda e: e.tensor_scalar(out=y_ap, in0=x_ap, scalar1=mean_ap,
                                              scalar2=rstd_ap, op0=ALU.subtract, op1=ALU.mult),
             deps=[r] + list(yw), sdeps=[m], label=label + "_y")
    return y


def build_nc(upto=None, dbg=()):
    if upto is None:
        upto = UPTO
    nc = bass.Bass("TRN2", target_bir_lowering=False, debug=False)
    es = contextlib.ExitStack()
    B = Builder(nc, es)
    d = B.d

    def din(name, shape, dt=F32):
        d[name] = nc.dram_tensor(name, shape, dt, kind="ExternalInput").ap()

    B.dbg = set(dbg)

    def dint(name, shape, dt):
        kind = "ExternalOutput" if (name in dbg and "bag" not in name) else "Internal"
        d[name] = nc.dram_tensor(name, shape, dt, kind=kind).ap()
        if name in dbg and "bag" in name:
            d[name + "_dbg"] = nc.dram_tensor(name + "_dbg", shape, dt,
                                              kind="ExternalOutput").ap()

    din("x_in", [TOK, D])
    din("qTw", [D, D]); din("kTw", [D, HD]); din("vTw", [D, HD]); din("oTw", [D, D])
    din("qkvw", [D, 3 * D]); din("obw", [D, D])
    din("Wg", [D, E])
    din("W1", [E, D, HFF]); din("W3", [E, D, HFF]); din("W2", [E, HFF, D])
    din("sw1", [D, HFF]); din("sw3", [D, HFF]); din("sw2", [HFF, D])
    din("qgain", [1, HD]); din("kgain", [1, HD])
    din("gammaI", [1, D]); din("betaI", [1, D])
    din("cosT", [TOK, 128]); din("sinT", [TOK, 128])
    din("maskT", [SLOTS * CH, 1024])
    din("selT", [NCORE * SLOTS, E]); din("lts8", [SLOTS, SLOTS])
    din("lt128", [128, 128]); din("ones128", [128, 128])
    din("ident", [128, 128], BF16)
    d["out_x"] = nc.dram_tensor("out_x", [TOK, D], F32, kind="ExternalOutput").ap()

    dint("resid", [TOK, D], F32)
    dint("y_d", [TOK, D], F32)
    dint("QTa", [NT, D, 128], BF16)
    dint("QTb", [NT, D, 128], BF16)
    dint("ctxT_d", [NT, D, 128], BF16)
    dint("ka_in", [HD, TOK], BF16)
    dint("kag", [2 * HD, TOK], BF16)
    dint("va_in", [TOK, HD], BF16)
    dint("vag", [2 * TOK, HD], BF16)
    for g in range(4):
        dint(f"kb_in{g}", [D, 1024], BF16)
        dint(f"kbag{g}", [2 * D, 1024], BF16)
        dint(f"vb_in{g}", [1024, D], BF16)
        dint(f"vbag{g}", [2048, D], BF16)
    dint("counts_in", [SLOTS, E], F32)
    dint("counts_ag", [NCORE * SLOTS, E], F32)
    dint("base_d", [2 * SLOTS, E], F32)
    dint("xnorm_d", [TOK, D], BF16)
    dint("yT_d", [NT, D, 128], BF16)
    dint("hT_d", [NT, HFF, 128], BF16)
    dint("disp_d", [DISP_ROWS, D], BF16)
    dint("dispo_d", [DISP_ROWS, D], BF16)

    with es:
        with nc.Block() as block:
            sb = {}
            sb["ident"] = es.enter_context(nc.sbuf_tensor("c_ident", [128, 128], BF16))
            sb["iota8"] = es.enter_context(nc.sbuf_tensor("c_iota8", [128, 8], F32))
            sb["ones1"] = es.enter_context(nc.sbuf_tensor("c_ones1", [1, 128], BF16))
            sb["lt128"] = es.enter_context(nc.sbuf_tensor("c_lt128", [128, 128], BF16))
            sb["on128"] = es.enter_context(nc.sbuf_tensor("c_on128", [128, 128], BF16))
            sb["zs"] = es.enter_context(nc.sbuf_tensor("c_zs", [128, D], BF16))
            sb["onesc"] = es.enter_context(nc.sbuf_tensor("c_onesc", [128, 1], BF16))
            sb["eps"] = es.enter_context(nc.sbuf_tensor("c_eps", [128, 2], F32))
            B.sb = sb
            ld_i = B.dma("SP", sb["ident"][:], d["ident"], label="ld_ident")
            ld_lt = B.dma("POOL", sb["lt128"][:], d["lt128"], label="ld_lt")
            ld_on = B.dma("POOL", sb["on128"][:], d["ones128"], label="ld_on")
            B.op("POOL", lambda e: e.iota(sb["iota8"][:], pattern=[[1, E]], base=0,
                                          channel_multiplier=0,
                                          allow_small_or_imprecise_dtypes=True),
                 label="iota8")
            B.op("DVE", lambda e: e.memset(sb["ones1"][:], 1.0), label="ones1")
            B.op("DVE", lambda e: e.memset(sb["zs"][:], 0.0), label="zs")
            B.op("DVE", lambda e: e.memset(sb["onesc"][:], 1.0), label="onesc")
            e1_ = B.op("DVE", lambda e: e.memset(sb["eps"][:, 0:1], EPS_LN), label="eps1")
            B.eps_dep = B.op("DVE", lambda e: e.memset(sb["eps"][:, 1:2], EPS_RMS),
                             deps=[e1_], label="eps2")
            B.setup_deps = dict(ident=ld_i, lt=ld_lt, on=ld_on)
            B.fence("setup")

            phase_a1(B)
            B.fence("a1")
            B.ka_cc = B.collective("AllGather", ALU.bypass,
                                   [[0, 1], [2, 3], [4, 5], [6, 7]],
                                   d["ka_in"], d["kag"], label="ag_ka")
            B.va_cc = B.collective("AllGather", ALU.bypass,
                                   [[0, 1], [2, 3], [4, 5], [6, 7]],
                                   d["va_in"], d["vag"], label="ag_va")
            phase_flash(B, "a")
            B.fence("a2")
            phase_oproj(B, "oTw", "ctxT_d", src="x_in",
                        dst=("out_x" if upto == 1 else "resid"), add_y=False)
            B.fence("a3")
            if upto >= 2:
                phase_moe(B, src="resid", dst=("out_x" if upto == 2 else "resid"))
                B.fence("moe1")
            if upto >= 3:
                phase_b1(B)
                B.fence("b1x")
                phase_flash(B, "b")
                B.fence("b2")
                phase_oproj(B, "obw", "ctxT_d", src="resid",
                            dst=("out_x" if upto == 3 else "resid"), add_y=True)
                B.fence("b3")
            if upto >= 4:
                phase_moe(B, src="resid", dst="out_x")
                B.fence("moe2")
            B.fence("final")
            B.finalize(block)
    return nc


def phase_a1(B):
    nc, d, sb = B.nc, B.d, B.sb
    ph = contextlib.ExitStack()
    T = lambda n, s, dt=F32: B.sbuf(ph, n, s, dt)
    xsb = [T(f"a1x{i}", [128, D]) for i in range(2)]
    cs = [T(f"a1c{i}", [128, 128]) for i in range(2)]
    sn = [T(f"a1s{i}", [128, 128]) for i in range(2)]
    ysb = [T(f"a1y{i}", [128, D]) for i in range(2)]
    ybf = [T(f"a1yb{i}", [128, D], BF16) for i in range(2)]
    yT = [T(f"a1yT{i}", [128, D], BF16) for i in range(2)]
    sc1 = [T(f"a1t1{i}", [128, D]) for i in range(2)]
    qf = [T(f"a1qf{i}", [128, D]) for i in range(2)]
    qn_ = ysb
    qrot = [T(f"a1qr{i}", [128, D], BF16) for i in range(2)]
    qTr = [T(f"a1qTr{i}", [128, D], BF16) for i in range(2)]
    kv = [T(f"a1kv{i}", [128, 4 * HD]) for i in range(2)]
    krot = [T(f"a1kr{i}", [128, HD], BF16) for i in range(2)]
    kTr = [T(f"a1kTr{i}", [128, HD], BF16) for i in range(2)]
    vbf = [T(f"a1vb{i}", [128, HD], BF16) for i in range(2)]
    sm = [T(f"a1sm{i}", [128, 32]) for i in range(2)]
    qw = T("a1qw", [128, DC * D], BF16)
    kw = T("a1kw", [128, DC * HD], BF16)
    vw = T("a1vw", [128, DC * HD], BF16)
    qg = T("a1qg", [128, HD])
    kg = T("a1kg", [128, HD])
    pT = [B.psum(ph, f"a1pT{i}", [128, 1024], BF16) for i in range(2)]
    pQ = [B.psum(ph, f"a1pQ{i}", [128, 512], F32) for i in range(2)]
    pK = B.psum(ph, "a1pK", [128, HD], F32)
    pV = B.psum(ph, "a1pV", [128, HD], F32)

    with ph:
        wq = B.dma("POOL", qw[:].rearrange("p (c n) -> p c n", c=DC),
                   d["qTw"].rearrange("(c dd) n -> dd c n", dd=128), label="a1_wq")
        wk = B.dma("POOL", kw[:].rearrange("p (c n) -> p c n", c=DC),
                   d["kTw"].rearrange("(c dd) n -> dd c n", dd=128), label="a1_wk")
        wv = B.dma("POOL", vw[:].rearrange("p (c n) -> p c n", c=DC),
                   d["vTw"].rearrange("(c dd) n -> dd c n", dd=128), label="a1_wv")
        gq = B.dma("SP", qg[:], d["qgain"].broadcast_to([128, HD]), label="a1_gq")
        gk = B.dma("SP", kg[:], d["kgain"].broadcast_to([128, HD]), label="a1_gk")

        last = {}

        for t in range(NT):
            p = t % 2
            lx = B.dma("SP", xsb[p][:], d["x_in"][tsl(t, 128), :],
                       deps=[last.get(("x", p))], label=f"a1_lx{t}")
            lc = B.dma("SP", cs[p][:], d["cosT"][tsl(t, 128), :],
                       deps=[last.get(("rope", p))], label=f"a1_lc{t}")
            ls = B.dma("SP", sn[p][:], d["sinT"][tsl(t, 128), :],
                       deps=[last.get(("rope", p))], label=f"a1_ls{t}")
            yop = ln_tile(B, sb, xsb[p][:], ysb[p][:], sc1[p][:],
                          sm[p][:, 1:2], sm[p][:, 3:4], sm[p][:, 4:5],
                          sm[p][:, 0:1], sm[p][:, 2:3], lx,
                          y_war=last.get(("ybf", p)), scr_war=last.get(("sq", p)),
                          label=f"a1ln{t}")
            last[("x", p)] = yop
            cop = B.op("ACT", lambda e, p=p: e.copy(ybf[p][:], ysb[p][:]),
                       deps=[yop, last.get(("ybf", p))], label=f"a1cast{t}")
            trs = []
            for half in range(2):
                items = [(pT[half][:, tsl(i, 128)],
                          ybf[p][:, tsl(half * 8 + i, 128)]) for i in range(8)]
                g = tr_group(B, items, sb["ident"][:],
                             deps=[cop, B.setup_deps["ident"], last.get(("pT", half))],
                             label=f"a1trY{t}_{half}")
                ev = B.op("DVE", lambda e, p=p, half=half:
                          e.tensor_copy(yT[p][:, tsl(half, 1024)], pT[half][:]),
                          deps=[g, last.get(("yT", p))], label=f"a1evY{t}_{half}")
                last[("pT", half)] = ev
                last[("ybf", p)] = g
                trs.append(ev)
            qevs = []
            for nb in range(4):
                pb = nb % 2
                grp = mm_group(B, pQ[pb][:],
                               [(yT[p][:, tsl(c, 128)], qw[:, c * D + nb * 512: c * D + nb * 512 + 512])
                                for c in range(DC)],
                               deps=trs + [wq, last.get(("pQ", pb))], label=f"a1q{t}_{nb}")
                ev = B.op("DVE", lambda e, p=p, nb=nb, pb=pb:
                          e.tensor_copy(qf[p][:, tsl(nb, 512)], pQ[pb][:]),
                          deps=[grp, last.get(("qf", p))], label=f"a1qe{t}_{nb}")
                last[("pQ", pb)] = ev
                qevs.append(ev)
            kgrp = mm_group(B, pK[:],
                            [(yT[p][:, tsl(c, 128)], kw[:, tsl(c, HD)]) for c in range(DC)],
                            deps=trs + [wk, last.get("pK")], label=f"a1k{t}")
            kev = B.op("DVE", lambda e, p=p: e.tensor_copy(kv[p][:, 0:HD], pK[:]),
                       deps=[kgrp, last.get(("kv", p))], label=f"a1ke{t}")
            last["pK"] = kev
            vgrp = mm_group(B, pV[:],
                            [(yT[p][:, tsl(c, 128)], vw[:, tsl(c, HD)]) for c in range(DC)],
                            deps=trs + [wv, last.get("pV")], label=f"a1v{t}")
            vev = B.op("DVE", lambda e, p=p: e.tensor_copy(vbf[p][:], pV[:]),
                       deps=[vgrp, last.get(("vbf", p))], label=f"a1ve{t}")
            last["pV"] = vev
            last[("yT", p)] = vgrp
            sq = B.op("ACT", lambda e, p=p: e.square(sc1[p][:], qf[p][:]),
                      deps=qevs + [yop, last.get(("ropesc", p))],
                      label=f"a1sq{t}")
            qsss = []
            for h in range(HEADS):
                r = B.op("DVE", lambda e, p=p, h=h:
                         e.reduce_sum(sm[p][:, 8 + h: 9 + h], sc1[p][:, tsl(h, HD)], axis=AX.X),
                         deps=[sq], label=f"a1qss{t}_{h}")
                qsss.append(r)
            rq = rsqrt(B, sm[p][:, 16:24], sm[p][:, 8:16], EPS_RMS, 1.0 / HD,
                       qsss[-1], label=f"a1rq{t}")
            rq_all = [rq] + qsss
            qns = []
            for h in range(HEADS):
                o = B.op("DVE", lambda e, p=p, h=h:
                         e.scalar_tensor_tensor(out=qn_[p][:, tsl(h, HD)], in0=qf[p][:, tsl(h, HD)],
                                                scalar=sm[p][:, 16 + h: 17 + h], in1=qg[:],
                                                op0=ALU.mult, op1=ALU.mult),
                         deps=[rq, gq], sdeps=qsss, label=f"a1qn{t}_{h}")
                qns.append(o)
            last[("qf", p)] = qns[-1]
            kss = B.op("DVE", lambda e, p=p:
                       e.scalar_tensor_tensor(out=kv[p][:, HD: 2 * HD], in0=kv[p][:, 0:HD],
                                              scalar=1.0, in1=kv[p][:, 0:HD],
                                              op0=ALU.bypass, op1=ALU.mult,
                                              accum_out=sm[p][:, 24:25]),
                       deps=[kev], label=f"a1kss{t}")
            rk = rsqrt(B, sm[p][:, 25:26], sm[p][:, 24:25], EPS_RMS, 1.0 / HD,
                       kss, label=f"a1rk{t}")
            kn = B.op("DVE", lambda e, p=p:
                      e.scalar_tensor_tensor(out=kv[p][:, HD: 2 * HD], in0=kv[p][:, 0:HD],
                                             scalar=sm[p][:, 25:26], in1=kg[:],
                                             op0=ALU.mult, op1=ALU.mult),
                      deps=[rk, gk, kss], sdeps=[kss], label=f"a1kn{t}")
            last[("kv", p)] = kn
            H2 = HD // 2
            ropes = []

            def rope_emit(src_ap_fn, dst_ap_fn, scr_ap, n_heads, extra_deps):
                outs = []
                for h in range(n_heads):
                    s0 = src_ap_fn(h, 0)
                    s1 = src_ap_fn(h, 1)
                    t1 = B.op("DVE", lambda e, s0=s0, scr=scr_ap(0): e.tensor_mul(scr, s0, cs[p][:]),
                              deps=extra_deps, label=f"a1ro{t}")
                    t2 = B.op("DVE", lambda e, s1=s1, scr=scr_ap(1): e.tensor_mul(scr, s1, sn[p][:]),
                              deps=extra_deps, label=f"a1ro{t}")
                    o1 = B.op("DVE", lambda e, d0=dst_ap_fn(h, 0), a=scr_ap(0), b=scr_ap(1):
                              e.tensor_sub(d0, a, b), deps=[t1, t2], label=f"a1ro{t}")
                    t3 = B.op("DVE", lambda e, s0=s0, scr=scr_ap(0): e.tensor_mul(scr, s0, sn[p][:]),
                              deps=[o1], label=f"a1ro{t}")
                    t4 = B.op("DVE", lambda e, s1=s1, scr=scr_ap(1): e.tensor_mul(scr, s1, cs[p][:]),
                              deps=[o1], label=f"a1ro{t}")
                    o2 = B.op("DVE", lambda e, d1=dst_ap_fn(h, 1), a=scr_ap(0), b=scr_ap(1):
                              e.tensor_add(d1, a, b), deps=[t3, t4], label=f"a1ro{t}")
                    outs.append(o2)
                return outs

            q_src = lambda h, half: qn_[p][:, h * HD + half * H2: h * HD + half * H2 + H2]
            q_dst = lambda h, half: qrot[p][:, h * HD + half * H2: h * HD + half * H2 + H2]
            q_scr = lambda i: sc1[p][:, i * H2:(i + 1) * H2]
            ropes = rope_emit(q_src, q_dst, q_scr, HEADS,
                              [qns[0], qns[-1], lc, ls, last.get(("qrot", p)), sq])
            k_src = lambda h, half: kv[p][:, HD + half * H2: HD + half * H2 + H2]
            k_dst = lambda h, half: krot[p][:, half * H2: half * H2 + H2]
            k_scr = lambda i: kv[p][:, 2 * HD + i * H2: 2 * HD + (i + 1) * H2]
            kroped = rope_emit(k_src, k_dst, k_scr, 1,
                               [kn, lc, ls, last.get(("krot", p))])
            last[("rope", p)] = kroped[-1]
            last[("ropesc", p)] = ropes[-1]
            last[("sq", p)] = ropes[-1]
            qst = []
            for half in range(2):
                items = [(pT[half][:, tsl(i, 128)],
                          qrot[p][:, tsl(half * 8 + i, 128)]) for i in range(8)]
                g = tr_group(B, items, sb["ident"][:],
                             deps=ropes + [last.get(("pT", half))], label=f"a1trQ{t}_{half}")
                ev = B.op("DVE", lambda e, p=p, half=half:
                          e.tensor_copy(qTr[p][:, tsl(half, 1024)], pT[half][:]),
                          deps=[g, last.get(("qTr", p))], label=f"a1evQ{t}_{half}")
                last[("pT", half)] = ev
                qst.append(ev)
            last[("qrot", p)] = qst[-1]
            sq_dma = B.dma("POOL", d["QTa"][t].rearrange("(c dd) n -> dd c n", dd=128),
                           qTr[p][:].rearrange("p (c n) -> p c n", c=DC),
                           deps=qst, label=f"a1stQ{t}")
            last[("qTr", p)] = sq_dma
            gk2 = tr_group(B, [(pT[0][:, tsl(i, 128)], krot[p][:, tsl(i, 128)]) for i in range(2)],
                           sb["ident"][:], deps=kroped + [last.get(("pT", 0))],
                           label=f"a1trK{t}")
            kev2 = B.op("DVE", lambda e, p=p: e.tensor_copy(kTr[p][:], pT[0][:, 0:HD]),
                        deps=[gk2, last.get(("kTr", p))], label=f"a1evK{t}")
            last[("pT", 0)] = kev2
            last[("krot", p)] = kev2
            sk_dma = B.dma("POOL", d["ka_in"].rearrange("(c dd) n -> dd c n", dd=128)[:, :, tsl(t, 128)],
                           kTr[p][:].rearrange("p (c n) -> p c n", c=2),
                           deps=[kev2], label=f"a1stK{t}")
            last[("kTr", p)] = sk_dma
            sv_dma = B.dma("POOL", d["va_in"][tsl(t, 128), :], vbf[p][:],
                           deps=[vev], label=f"a1stV{t}")
            last[("vbf", p)] = sv_dma


def chunk_par(c):
    return 0 if c in A_SET else 1


def chunk_lslot(c):
    return SETS[chunk_par(c)].index(c)


def phase_flash(B, kind):
    nc, d, sb = B.nc, B.d, B.sb
    ph = contextlib.ExitStack()
    T = lambda n, s, dt=F32: B.sbuf(ph, n, s, dt)
    heads = HEADS if kind == "a" else 1
    hd = HD if kind == "a" else D
    scale = (1.0 / 16.0) if kind == "a" else (1.0 / float(np.sqrt(2048.0)))
    NU = TPS * heads
    q_dram = d["QTa"] if kind == "a" else d["QTb"]

    mask_sb = T("f_mask", [128, TPS * 1024])
    qts = [T(f"f_q{i}", [128, D], BF16) for i in range(TPS)]
    ctx = T("f_ctx", [128, 8192])
    stt_ = T("f_st", [128, 4 * 64])
    P = [T(f"f_P{i}", [128, 1024], BF16) for i in range(2)]
    PT = [T(f"f_PT{i}", [128, 1024], BF16) for i in range(2)]
    ctxb = T("f_ctxb", [128, 2048], BF16)
    ctxT = T("f_ctxT", [128, 2048], BF16)
    if kind == "a":
        KT = T("f_KT", [128, 2 * SEQ], BF16)
        VT = T("f_VT", [128, (SEQ // 128) * HD], BF16)
    else:
        KT = [T(f"f_KTb{i}", [128, DC * 1024], BF16) for i in range(2)]
        VT = T("f_VTb", [128, 8 * D], BF16)
    psS2 = [B.psum(ph, f"f_psS{i}", [128, 1024], F32) for i in range(2)]
    psPV = [B.psum(ph, f"f_pv{i}", [128, 512], F32) for i in range(2)]
    psT = [B.psum(ph, f"f_pT{i}", [128, 1024], BF16) for i in range(2)]

    def nmc(u):
        return stt_[:, u:u + 1]

    def lc(u):
        return stt_[:, 64 + u:64 + u + 1]

    def tmpc(u):
        return stt_[:, 128 + u:128 + u + 1]

    def t2c(u):
        return stt_[:, 192 + u:192 + u + 1]

    with ph:
        last = {}
        if kind == "a":
            kvdeps = []
            for g in range(NCH):
                par, ls_ = chunk_par(g), chunk_lslot(g)
                for c in range(2):
                    o = B.dma("SP", KT[:, c * SEQ + g * CH: c * SEQ + (g + 1) * CH],
                              d["kag"].rearrange("(pp c dd) n -> pp c dd n", pp=2, c=2)[par, c][:, tsl(ls_, CH)],
                              deps=[B.ka_cc], label=f"fA_k{g}_{c}")
                    kvdeps.append(o)
                for kt in range(4):
                    o = B.dma("SP", VT[:, (g * 4 + kt) * HD: (g * 4 + kt + 1) * HD],
                              d["vag"][par * TOK + ls_ * CH + kt * 128: par * TOK + ls_ * CH + (kt + 1) * 128, :],
                              deps=[B.va_cc], label=f"fA_v{g}_{kt}")
                    kvdeps.append(o)
            last["kv"] = kvdeps

        pend = [None]

        def stage2(u, up, s, alph, pexp, pmask, plc, jj):
            if s == 0:
                lup = B.op("DVE", lambda e, u=u, plc=plc: e.tensor_copy(lc(u), plc),
                           deps=[pmask or pexp], sdeps=[pmask or pexp],
                           label=f"flu0{jj}_{u}")
            else:
                lup = B.op("DVE", lambda e, u=u, plc=plc: e.scalar_tensor_tensor(
                    out=lc(u), in0=lc(u), scalar=tmpc(u), in1=plc,
                    op0=ALU.mult, op1=ALU.add),
                    deps=[pmask or pexp, alph], sdeps=[pmask or pexp, alph],
                    label=f"flu{jj}_{s}_{u}")
            last[("lupd", u)] = lup
            last[("plc", up)] = lup
            trg = tr_group(B, [(psT[up][:, tsl(i % 8, 128)], P[up][:, tsl(i, 128)])
                               for i in range(8)],
                           sb["ident"][:],
                           deps=[pmask or pexp, last.get(("psT", up))],
                           label=f"ftr{jj}_{s}_{u}")
            ptev = B.op("DVE", lambda e, up=up: e.tensor_copy(PT[up][:], psT[up][:]),
                        deps=[trg, last.get(("PT", up))], label=f"fpte{jj}_{s}_{u}")
            last[("psT", up)] = ptev
            last[("P", up)] = trg
            if kind == "a":
                pvg = mm_group(B, psPV[up][:, 0:256],
                               [(PT[up][:, tsl(kt, 128)],
                                 VT[:, (s * 8 + kt) * HD: (s * 8 + kt + 1) * HD])
                                for kt in range(8)],
                               deps=[ptev, last.get(("psPV", up))] + last["kv"],
                               label=f"fpv{jj}_{s}_{u}")
                if s == 0:
                    cup = B.op("DVE", lambda e, u=u, up=up: e.tensor_copy(
                        ctx[:, tsl(u, 256)], psPV[up][:, 0:256]),
                        deps=[pvg], label=f"fcu0{jj}_{u}")
                else:
                    cup = B.op("DVE", lambda e, u=u, up=up: e.scalar_tensor_tensor(
                        out=ctx[:, tsl(u, 256)], in0=ctx[:, tsl(u, 256)],
                        scalar=tmpc(u), in1=psPV[up][:, 0:256],
                        op0=ALU.mult, op1=ALU.add),
                        deps=[pvg, alph], sdeps=[alph, last.get(("ctx", u))],
                        label=f"fcu{jj}_{s}_{u}")
                last[("psPV", up)] = cup
                last[("ctx", u)] = cup
                last[("PT", up)] = pvg
            else:
                vdeps = last[("KTb_ld", s % 2)]
                cups = []
                for n2 in range(4):
                    pvb = n2 % 2
                    pvg = mm_group(B, psPV[pvb][:],
                                   [(PT[up][:, tsl(kt, 128)],
                                     VT[:, kt * D + n2 * 512: kt * D + n2 * 512 + 512])
                                    for kt in range(8)],
                                   deps=[ptev, last.get(("psPV", pvb))] + vdeps,
                                   label=f"fpvb{jj}_{s}_{u}_{n2}")
                    if s == 0:
                        cup = B.op("DVE", lambda e, u=u, n2=n2, pvb=pvb: e.tensor_copy(
                            ctx[:, u * 2048 + n2 * 512: u * 2048 + (n2 + 1) * 512],
                            psPV[pvb][:]),
                            deps=[pvg], label=f"fcb0{jj}_{u}_{n2}")
                    else:
                        cup = B.op("DVE", lambda e, u=u, n2=n2, pvb=pvb: e.scalar_tensor_tensor(
                            out=ctx[:, u * 2048 + n2 * 512: u * 2048 + (n2 + 1) * 512],
                            in0=ctx[:, u * 2048 + n2 * 512: u * 2048 + (n2 + 1) * 512],
                            scalar=tmpc(u), in1=psPV[pvb][:],
                            op0=ALU.mult, op1=ALU.add),
                            deps=[pvg, alph], sdeps=[alph, last.get(("ctxb", (u, n2)))],
                            label=f"fcb{jj}_{s}_{u}_{n2}")
                    last[("psPV", pvb)] = cup
                    last[("ctxb", (u, n2))] = cup
                    cups.append(cup)
                last[("PT", up)] = cups[-1]
                last["VTb"] = cups[-1]

        ucount = 0
        for j in range(SLOTS):
            lm = B.dma("SP", mask_sb[:].rearrange("p (q k) -> p q k", q=TPS),
                       d["maskT"].rearrange("(q pp) k -> pp q k", pp=128)[:, tsl(j, TPS), :],
                       deps=[last.get("mask")], label=f"f_m{j}")
            qlds = []
            for qt in range(TPS):
                tile = j * TPS + qt
                o = B.dma("SP", qts[qt][:].rearrange("p (c n) -> p c n", c=DC),
                          q_dram[tile].rearrange("(c dd) n -> dd c n", dd=128),
                          deps=[last.get(("q", qt))], label=f"f_q{j}_{qt}")
                qlds.append(o)
            for s in range(j + 1):
                islast = (s == j)
                if kind == "b":
                    if pend[0] is not None:
                        stage2(**pend[0])
                        pend[0] = None
                    sp = s % 2
                    kv_ops = []
                    for half in range(2):
                        c = 2 * s + half
                        par, ls_, g = chunk_par(c), chunk_lslot(c), chunk_lslot(c) // 2
                        col0 = (chunk_lslot(c) % 2) * CH
                        for cc_ in range(DC):
                            o = B.dma("SP",
                                      KT[sp][:, cc_ * 1024 + half * CH: cc_ * 1024 + half * CH + CH],
                                      d[f"kbag{g}"].rearrange("(pp c dd) n -> pp c dd n", pp=2, c=DC)[par, cc_][:, col0:col0 + CH],
                                      deps=[B.kb_cc[g], last.get(("KTb", sp))],
                                      label=f"fB_k{j}_{s}_{half}_{cc_}")
                            kv_ops.append(o)
                        for kt in range(4):
                            o = B.dma("SP",
                                      VT[:, (half * 4 + kt) * D: (half * 4 + kt + 1) * D],
                                      d[f"vbag{g}"][par * 1024 + col0 + kt * 128: par * 1024 + col0 + (kt + 1) * 128, :],
                                      deps=[B.vb_cc[g], last.get("VTb")],
                                      label=f"fB_v{j}_{s}_{half}_{kt}")
                            kv_ops.append(o)
                    last[("KTb_ld", sp)] = kv_ops
                for qt in range(TPS):
                    for h in range(heads):
                        u = qt * heads + h
                        up = ucount % 2
                        ucount += 1
                        psS = psS2[up]
                        qkg = []
                        for kb in range(2):
                            if kind == "a":
                                pairs = [(qts[qt][:, tsl(2 * h + c, 128)],
                                          KT[:, c * SEQ + (s * 1024 + kb * 512): c * SEQ + (s * 1024 + kb * 512) + 512])
                                         for c in range(2)]
                                kdeps = last["kv"]
                            else:
                                pairs = [(qts[qt][:, tsl(c, 128)],
                                          KT[s % 2][:, c * 1024 + kb * 512: c * 1024 + kb * 512 + 512])
                                         for c in range(DC)]
                                kdeps = last[("KTb_ld", s % 2)]
                            g = mm_group(B, psS[:, tsl(kb, 512)], pairs,
                                         deps=[qlds[qt]] + kdeps +
                                              [last.get(("psS", up))], label=f"fqk{j}_{s}_{u}_{kb}")
                            qkg.append(g)
                        if kind == "b":
                            last[("KTb", s % 2)] = qkg[-1]
                        bm = B.op("DVE", lambda e, u=u, psS=psS: e.reduce_max(
                            tmpc(u), psS[:], axis=AX.X, negate=True),
                            deps=qkg, label=f"fbm{j}_{s}_{u}")
                        if s == 0:
                            nmu = B.op("DVE", lambda e, u=u: e.tensor_copy(nmc(u), tmpc(u)),
                                       deps=[bm], sdeps=[bm], label=f"fnm0{j}_{u}")
                            bco = B.op("DVE", lambda e, u=u: e.tensor_scalar(
                                out=t2c(u), in0=nmc(u), scalar1=scale, scalar2=None, op0=ALU.mult),
                                deps=[nmu], sdeps=[nmu], label=f"fbc{j}_{u}")
                            alph = None
                        else:
                            dlt = B.op("DVE", lambda e, u=u: e.tensor_scalar(
                                out=t2c(u), in0=tmpc(u), scalar1=nmc(u), scalar2=0.0,
                                op0=ALU.subtract, op1=ALU.min),
                                deps=[bm], sdeps=[bm, last.get(("lupd", u))], label=f"fdl{j}_{s}_{u}")
                            alph = B.op("ACT", lambda e, u=u: e.activation(
                                tmpc(u), t2c(u), AF.Exp, scale=scale),
                                deps=[dlt], label=f"fal{j}_{s}_{u}")
                            nmu = B.op("DVE", lambda e, u=u: e.tensor_tensor(
                                out=nmc(u), in0=nmc(u), in1=t2c(u), op=ALU.add),
                                deps=[dlt, alph], sdeps=[dlt], label=f"fnm{j}_{s}_{u}")
                            bco = B.op("DVE", lambda e, u=u: e.tensor_scalar(
                                out=t2c(u), in0=nmc(u), scalar1=scale, scalar2=None, op0=ALU.mult),
                                deps=[nmu, alph], sdeps=[nmu], label=f"fbc{j}_{s}_{u}")
                        plc = stt_[:, 252 + up:253 + up]
                        pexp = B.op("ACT", lambda e, u=u, up=up, psS=psS, plc=plc: e.activation(
                            P[up][:], psS[:], AF.Exp, bias=t2c(u), scale=scale,
                            accum_out=plc),
                            deps=[bco, last.get(("P", up)), last.get(("plc", up))],
                            label=f"fex{j}_{s}_{u}")
                        last[("psS", up)] = pexp
                        pmask = None
                        if islast:
                            pmask = B.op("DVE", lambda e, u=u, up=up, qt=qt, plc=plc: e.scalar_tensor_tensor(
                                out=P[up][:], in0=P[up][:], scalar=1.0,
                                in1=mask_sb[:, tsl(qt, 1024)], op0=ALU.bypass, op1=ALU.mult,
                                accum_out=plc),
                                deps=[pexp, lm], label=f"fpm{j}_{s}_{u}")
                        if pend[0] is not None:
                            stage2(**pend[0])
                        pend[0] = dict(u=u, up=up, s=s, alph=alph, pexp=pexp,
                                       pmask=pmask, plc=plc, jj=j)
            if pend[0] is not None:
                stage2(**pend[0])
                pend[0] = None
            for qt in range(TPS):
                tile = j * TPS + qt
                fins = []
                for h in range(heads):
                    u = qt * heads + h
                    rl = B.op("DVE", lambda e, u=u: e.reciprocal(tmpc(u), lc(u)),
                              deps=[last[("lupd", u)]], sdeps=[last[("lupd", u)]],
                              label=f"frl{j}_{u}")
                    span = 256 if kind == "a" else 2048
                    cb = B.op("DVE", lambda e, u=u, h=h, span=span: e.tensor_scalar(
                        out=ctxb[:, h * span:(h + 1) * span] if kind == "a" else ctxb[:],
                        in0=ctx[:, u * span:(u + 1) * span],
                        scalar1=tmpc(u), scalar2=None, op0=ALU.mult),
                        deps=[rl, last.get(("ctx", u)), last.get("ctxb_war")],
                        sdeps=[rl], label=f"fcb{j}_{u}")
                    fins.append(cb)
                trs = []
                for half in range(2):
                    items = [(psT[half][:, tsl(i, 128)], ctxb[:, tsl(half * 8 + i, 128)])
                             for i in range(8)]
                    g = tr_group(B, items, sb["ident"][:],
                                 deps=fins + [last.get(("psT", half))], label=f"fct{j}_{qt}_{half}")
                    ev = B.op("DVE", lambda e, half=half: e.tensor_copy(
                        ctxT[:, tsl(half, 1024)], psT[half][:]),
                        deps=[g, last.get("ctxT")], label=f"fcte{j}_{qt}_{half}")
                    last[("psT", half)] = ev
                    trs.append(ev)
                last["ctxb_war"] = trs[-1]
                st = B.dma("POOL", d["ctxT_d"][tile].rearrange("(c dd) n -> dd c n", dd=128),
                           ctxT[:].rearrange("p (c n) -> p c n", c=DC),
                           deps=trs, label=f"fst{j}_{qt}")
                last["ctxT"] = st
                last[("q", qt)] = st
            last["mask"] = st


def phase_oproj(B, wname, ctx_name, src, dst, add_y):
    nc, d, sb = B.nc, B.d, B.sb
    ph = contextlib.ExitStack()
    T = lambda n, s, dt=F32: B.sbuf(ph, n, s, dt)
    wsb = T("o_w", [128, DC * D], BF16)
    ctxT = [T(f"o_c{i}", [128, D], BF16) for i in range(2)]
    xs = [T(f"o_x{i}", [128, D]) for i in range(2)]
    ys = [T(f"o_y{i}", [128, D]) for i in range(2)] if add_y else None
    xn = [T(f"o_n{i}", [128, D]) for i in range(2)]
    pO = [B.psum(ph, f"o_p{i}", [128, 512], F32) for i in range(2)]
    with ph:
        wl = B.dma("POOL", wsb[:].rearrange("p (c n) -> p c n", c=DC),
                   d[wname].rearrange("(c dd) n -> dd c n", dd=128), label="o_wl")
        last = {}
        for t in range(NT):
            p = t % 2
            lc_ = B.dma("SP", ctxT[p][:].rearrange("p (c n) -> p c n", c=DC),
                        d[ctx_name][t].rearrange("(c dd) n -> dd c n", dd=128),
                        deps=[last.get(("ct", p))], label=f"o_lc{t}")
            lx = B.dma("SP", xs[p][:], d[src][tsl(t, 128), :],
                       deps=[last.get(("x", p))], label=f"o_lx{t}")
            ly = None
            if add_y:
                ly = B.dma("SP", ys[p][:], d["y_d"][tsl(t, 128), :],
                           deps=[last.get(("y", p))], label=f"o_ly{t}")
            combs = []
            for nb in range(4):
                pb = nb % 2
                grp = mm_group(B, pO[pb][:],
                               [(ctxT[p][:, tsl(c, 128)], wsb[:, c * D + nb * 512: c * D + nb * 512 + 512])
                                for c in range(DC)],
                               deps=[lc_, wl, last.get(("pO", pb))], label=f"o_g{t}_{nb}")
                c1 = B.op("DVE", lambda e, p=p, nb=nb, pb=pb: e.scalar_tensor_tensor(
                    out=xn[p][:, tsl(nb, 512)], in0=xs[p][:, tsl(nb, 512)], scalar=1.0,
                    in1=pO[pb][:], op0=ALU.bypass, op1=ALU.add),
                    deps=[grp, lx, last.get(("xn", p))], label=f"o_c1{t}_{nb}")
                last[("pO", pb)] = c1
                if add_y:
                    c2 = B.op("DVE", lambda e, p=p, nb=nb: e.tensor_add(
                        xn[p][:, tsl(nb, 512)], xn[p][:, tsl(nb, 512)], ys[p][:, tsl(nb, 512)]),
                        deps=[c1, ly], label=f"o_c2{t}_{nb}")
                    combs.append(c2)
                else:
                    combs.append(c1)
            last[("ct", p)] = combs[-1]
            last[("x", p)] = combs[-1]
            if add_y:
                last[("y", p)] = combs[-1]
            st = B.dma("POOL", d[dst][tsl(t, 128), :], xn[p][:], deps=combs, label=f"o_st{t}")
            last[("xn", p)] = st


def phase_moe(B, src, dst):
    nc, d, sb = B.nc, B.d, B.sb
    TRASH = float(E * KPAD)
    ph = contextlib.ExitStack()
    T = lambda n, s, dt=F32: B.sbuf(ph, n, s, dt)
    eidx_sb = T("m_eidx", [128, NT])
    gate_sb = T("m_gate", [128, NT])
    gk_sb = T("m_gk", [128, NT])
    idx_sb = T("m_idx", [128, NT], U32)
    with ph:
        _moe_m1(B, ph, src, eidx_sb, gate_sb)
        B.fence("m1")
        _moe_m1b(B, ph)
        B.fence("m1b")
        _moe_m2(B, ph, eidx_sb, gate_sb, gk_sb, idx_sb)
        B.fence("m2")
        _moe_m3(B, ph)
        B.fence("m3")
        _moe_m4(B, ph, src, dst, gk_sb, idx_sb)


def _moe_m1(B, ph, src, eidx_sb, gate_sb):
    nc, d, sb = B.nc, B.d, B.sb
    T = lambda n, s, dt=F32: B.sbuf(ph, n, s, dt)
    p2 = contextlib.ExitStack()
    T2 = lambda n, s, dt=F32: B.sbuf(p2, n, s, dt)
    xsb = [T2(f"m1x{i}", [128, D]) for i in range(2)]
    ysb = [T2(f"m1y{i}", [128, D]) for i in range(2)]
    ybf = [T2(f"m1yb{i}", [128, D], BF16) for i in range(2)]
    ylo = [T2(f"m1yl{i}", [128, D]) for i in range(2)]
    ylob = [T2(f"m1ylb{i}", [128, D], BF16) for i in range(2)]
    yT = [T2(f"m1yT{i}", [128, D], BF16) for i in range(2)]
    yloT = [T2(f"m1yLT{i}", [128, D], BF16) for i in range(2)]
    sc1 = [T2(f"m1s1{i}", [128, D]) for i in range(2)]
    sm = [T2(f"m1sm{i}", [128, 48]) for i in range(2)]
    oh_sb = T2("m1oh", [128, TPS * E], BF16)
    wg_hi = T2("m1wgh", [128, DC * E], BF16)
    wg_lo = T2("m1wgl", [128, DC * E], BF16)
    wg_f = T2("m1wgf", [128, DC * E])
    pT = [B.psum(p2, f"m1pT{i}", [128, 1024], BF16) for i in range(2)]
    pZ = B.psum(p2, "m1pZ", [128, E], F32)
    pC = B.psum(p2, "m1pC", [1, E], F32)
    with p2:
        wgl = B.dma("SP", wg_f[:].rearrange("p (c n) -> p c n", c=DC),
                    d["Wg"].rearrange("(c dd) n -> dd c n", dd=128), label="m1_wg")
        h1 = B.op("DVE", lambda e: e.tensor_copy(wg_hi[:], wg_f[:]), deps=[wgl], label="m1_wgh")
        h2 = B.op("DVE", lambda e: e.tensor_sub(wg_f[:], wg_f[:], wg_hi[:]),
                  deps=[h1], label="m1_wgs")
        h3 = B.op("DVE", lambda e: e.tensor_copy(wg_lo[:], wg_f[:]), deps=[h2], label="m1_wgl")
        last = {}
        zero_st = []
        for zt in range(DISP_ROWS // 128):
            o = B.dma("POOL", d["disp_d"][tsl(zt, 128), :], sb["zs"][:], label=f"m1_z{zt}")
            zero_st.append(o)
        for t in range(NT):
            p = t % 2
            slot, tin = t // TPS, t % TPS
            lx = B.dma("SP", xsb[p][:], d[src][tsl(t, 128), :],
                       deps=[last.get(("x", p))], label=f"m1_lx{t}")
            yop = ln_tile(B, sb, xsb[p][:], ysb[p][:], sc1[p][:],
                          sm[p][:, 1:2], sm[p][:, 3:4], sm[p][:, 4:5],
                          sm[p][:, 0:1], sm[p][:, 2:3], lx,
                          y_war=last.get(("y", p)), label=f"m1ln{t}")
            last[("x", p)] = yop
            cop = B.op("ACT", lambda e, p=p: e.copy(ybf[p][:], ysb[p][:]),
                       deps=[yop, last.get(("ybf", p))], label=f"m1c{t}")
            lo1 = B.op("DVE", lambda e, p=p: e.tensor_sub(ylo[p][:], ysb[p][:], ybf[p][:]),
                       deps=[cop, last.get(("ylo", p))], label=f"m1lo{t}")
            lo2 = B.op("ACT", lambda e, p=p: e.copy(ylob[p][:], ylo[p][:]),
                       deps=[lo1, last.get(("ylob", p))], label=f"m1lo2{t}")
            last[("ylo", p)] = lo2
            last[("y", p)] = lo1
            sx = B.dma("POOL", d["xnorm_d"][tsl(t, 128), :], ybf[p][:],
                       deps=[cop], label=f"m1sx{t}")
            trs = []
            for half in range(2):
                g = tr_group(B, [(pT[half][:, tsl(i, 128)], ybf[p][:, tsl(half * 8 + i, 128)])
                                 for i in range(8)], sb["ident"][:],
                             deps=[cop, last.get(("pT", half))], label=f"m1tr{t}_{half}")
                ev = B.op("DVE", lambda e, p=p, half=half: e.tensor_copy(
                    yT[p][:, tsl(half, 1024)], pT[half][:]),
                    deps=[g, last.get(("yT", p))], label=f"m1ev{t}_{half}")
                last[("pT", half)] = ev
                trs.append(ev)
            last[("ybf", p)] = sx
            sy = B.dma("POOL", d["yT_d"][t].rearrange("(c dd) n -> dd c n", dd=128),
                       yT[p][:].rearrange("p (c n) -> p c n", c=DC),
                       deps=trs, label=f"m1sy{t}")
            trl = []
            for half in range(2):
                g = tr_group(B, [(pT[half][:, tsl(i, 128)], ylob[p][:, tsl(half * 8 + i, 128)])
                                 for i in range(8)], sb["ident"][:],
                             deps=[lo2, last.get(("pT", half))], label=f"m1trl{t}_{half}")
                ev = B.op("DVE", lambda e, p=p, half=half: e.tensor_copy(
                    yloT[p][:, tsl(half, 1024)], pT[half][:]),
                    deps=[g, last.get(("yloT", p))], label=f"m1evl{t}_{half}")
                last[("pT", half)] = ev
                last[("ylob", p)] = g
                trl.append(ev)
            pairs = ([(yT[p][:, tsl(c, 128)], wg_hi[:, tsl(c, E)]) for c in range(DC)] +
                     [(yT[p][:, tsl(c, 128)], wg_lo[:, tsl(c, E)]) for c in range(DC)] +
                     [(yloT[p][:, tsl(c, 128)], wg_hi[:, tsl(c, E)]) for c in range(DC)])
            zg = mm_group(B, pZ[:], pairs,
                          deps=trs + trl + [h1, h3, last.get("pZ")], label=f"m1z{t}")
            last[("yT", p)] = zg
            last[("yloT", p)] = zg
            zev = B.op("DVE", lambda e, p=p: e.tensor_copy(sm[p][:, 8:16], pZ[:]),
                       deps=[zg], label=f"m1ze{t}")
            last["pZ"] = zev
            mx = B.op("DVE", lambda e, p=p: e.reduce_max(sm[p][:, 16:17], sm[p][:, 8:16], axis=AX.X),
                      deps=[zev], sdeps=[zev], label=f"m1mx{t}")
            ng = B.op("DVE", lambda e, p=p: e.tensor_scalar(
                out=sm[p][:, 17:18], in0=sm[p][:, 16:17], scalar1=-1.0, scalar2=None, op0=ALU.mult),
                deps=[mx], sdeps=[mx], label=f"m1ng{t}")
            ex = B.op("ACT", lambda e, p=p: e.activation(
                sm[p][:, 24:32], sm[p][:, 8:16], AF.Exp, bias=sm[p][:, 17:18],
                accum_out=sm[p][:, 18:19]),
                deps=[ng], label=f"m1ex{t}")
            gt = B.op("DVE", lambda e, p=p, t=t: e.reciprocal(gate_sb[:, t:t + 1], sm[p][:, 18:19]),
                      deps=[ex], sdeps=[ex], label=f"m1g{t}")
            oh = B.op("DVE", lambda e, p=p, tin=tin: e.tensor_scalar(
                out=oh_sb[:, tsl(tin, E)], in0=sm[p][:, 8:16], scalar1=sm[p][:, 16:17],
                scalar2=None, op0=ALU.is_equal),
                deps=[mx, last.get(("oh", tin))], sdeps=[mx], label=f"m1oh{t}")
            ei = B.op("DVE", lambda e, p=p, tin=tin, t=t: e.scalar_tensor_tensor(
                out=sm[p][:, 24:32], in0=oh_sb[:, tsl(tin, E)], scalar=1.0, in1=sb["iota8"][:],
                op0=ALU.bypass, op1=ALU.mult, accum_out=eidx_sb[:, t:t + 1]),
                deps=[oh, ex], sdeps=[oh], label=f"m1ei{t}")
            pass
            cg = B.op("PE", lambda e, tin=tin: e.matmul(
                pC[:], sb["onesc"][:], oh_sb[:, tsl(tin, E)],
                start=(tin == 0), stop=(tin == TPS - 1)),
                deps=[oh, last.get("pC")], label=f"m1cg{t}")
            last[("oh", tin)] = cg
            if tin == TPS - 1:
                cev = B.op("DVE", lambda e, p=p: e.tensor_copy(sm[p][0:1, 32:40], pC[:]),
                           deps=[cg], label=f"m1ce{t}")
                last["pC"] = cev
                stc = B.dma("POOL", d["counts_in"][slot:slot + 1, :], sm[p][0:1, 32:40],
                            deps=[cev], label=f"m1stc{t}")
                last[("cnt", p)] = stc
            else:
                last["pC"] = cg


def _moe_m1b(B, ph):
    nc, d, sb = B.nc, B.d, B.sb
    p2 = contextlib.ExitStack()
    T2 = lambda n, s, dt=F32: B.sbuf(p2, n, s, dt)
    cag = T2("mbc", [64, E])
    sel = T2("mbs", [64, SLOTS])
    own = T2("mbo", [SLOTS, E])
    lts = T2("mbl", [SLOTS, SLOTS])
    bg = T2("mbg", [SLOTS, E])
    pB_ = B.psum(p2, "mbp", [SLOTS, E], F32)
    with p2:
        cc = B.collective("AllGather", ALU.bypass, [list(range(NCORE))],
                          d["counts_in"], d["counts_ag"], label="m1b_ag")
        l1 = B.dma("SP", cag[:], d["counts_ag"], deps=[cc], label="m1b_l1")
        l2 = B.dma("SP", sel[:], d["selT"], label="m1b_l2")
        l3 = B.dma("SP", own[:], d["counts_in"], deps=[cc], label="m1b_l3")
        l4 = B.dma("SP", lts[:], d["lts8"], label="m1b_l4")
        g1 = mm_group(B, pB_[:], [(sel[:], cag[:])], deps=[l1, l2], label="m1b_g1")
        e1 = B.op("DVE", lambda e: e.tensor_copy(bg[:], pB_[:]), deps=[g1], label="m1b_e1")
        s1 = B.dma("POOL", d["base_d"][0:SLOTS, :], bg[:], deps=[e1], label="m1b_s1")
        g2 = mm_group(B, pB_[:], [(lts[:], own[:])], deps=[l3, l4, e1], label="m1b_g2")
        e2 = B.op("DVE", lambda e: e.tensor_copy(bg[:], pB_[:]), deps=[g2, s1], label="m1b_e2")
        B.dma("POOL", d["base_d"][SLOTS:2 * SLOTS, :], bg[:], deps=[e2], label="m1b_s2")


def _moe_m2(B, ph, eidx_sb, gate_sb, gk_sb, idx_sb):
    nc, d, sb = B.nc, B.d, B.sb
    TRASH = float(E * KPAD)
    p2 = contextlib.ExitStack()
    T2 = lambda n, s, dt=F32: B.sbuf(p2, n, s, dt)
    bsg = T2("m2bg", [128, E])
    bsl = T2("m2bl", [128, E])
    oh = [T2(f"m2oh{i}", [128, E], BF16) for i in range(TPS)]
    smv = [T2(f"m2sm{i}", [128, 32]) for i in range(2)]
    xnb = [T2(f"m2xn{i}", [128, D], BF16) for i in range(2)]
    pU = B.psum(p2, "m2pU", [128, E], F32)
    pV_ = B.psum(p2, "m2pV", [128, E], F32)
    with p2:
        last = {}
        for slot in range(SLOTS):
            lg = B.dma("SP", bsg[:], d["base_d"][slot:slot + 1, :].broadcast_to([128, E]),
                       deps=[last.get("baseg")], label=f"m2lg{slot}")
            ll = B.dma("SP", bsl[:], d["base_d"][SLOTS + slot:SLOTS + slot + 1, :].broadcast_to([128, E]),
                       deps=[last.get("basel")], label=f"m2ll{slot}")
            ohops = []
            for tin in range(TPS):
                t = slot * TPS + tin
                o = B.op("DVE", lambda e, tin=tin, t=t: e.tensor_scalar(
                    out=oh[tin][:], in0=sb["iota8"][:], scalar1=eidx_sb[:, t:t + 1],
                    scalar2=None, op0=ALU.is_equal),
                    deps=[last.get(("oh", tin))], label=f"m2oh{t}")
                ohops.append(o)
            for tin in range(TPS):
                t = slot * TPS + tin
                p = tin % 2

                def emit(e, tin=tin, ps_=None):
                    last_i = None
                    last_i = e.matmul(ps_[:], sb["lt128"][:], oh[tin][:],
                                      start=True, stop=(tin == 0))
                    for q in range(tin):
                        last_i = e.matmul(ps_[:], sb["on128"][:], oh[q][:],
                                          start=False, stop=(q == tin - 1))
                    return last_i
                gg = B.op("PE", lambda e, tin=tin: emit(e, tin=tin, ps_=pU),
                          deps=ohops[:tin + 1] + [last.get(("ps", "g"))],
                          label=f"m2pfg{t}")
                gl = B.op("PE", lambda e, tin=tin: emit(e, tin=tin, ps_=pV_),
                          deps=ohops[:tin + 1] + [last.get(("ps", "l"))],
                          label=f"m2pfl{t}")
                tg = B.op("DVE", lambda e, p=p: e.tensor_add(smv[p][:, 8:16], pU[:], bsg[:]),
                          deps=[gg, lg, last.get(("tmp", p))], label=f"m2tg{t}")
                gpos = B.op("DVE", lambda e, p=p, tin=tin: e.scalar_tensor_tensor(
                    out=smv[p][:, 16:24], in0=oh[tin][:], scalar=1.0, in1=smv[p][:, 8:16],
                    op0=ALU.bypass, op1=ALU.mult, accum_out=smv[p][:, 0:1]),
                    deps=[tg], sdeps=[tg], label=f"m2pos{t}")
                last[("ps", "g")] = tg
                tl = B.op("DVE", lambda e, p=p: e.tensor_add(smv[p][:, 8:16], pV_[:], bsl[:]),
                          deps=[gl, ll, gpos], sdeps=[gpos], label=f"m2tl{t}")
                lpos = B.op("DVE", lambda e, p=p, tin=tin: e.scalar_tensor_tensor(
                    out=smv[p][:, 16:24], in0=oh[tin][:], scalar=1.0, in1=smv[p][:, 8:16],
                    op0=ALU.bypass, op1=ALU.mult, accum_out=smv[p][:, 1:2]),
                    deps=[tl], sdeps=[tl, gpos], label=f"m2lr{t}")
                last[("ps", "l")] = tl
                last[("tmp", p)] = lpos
                last["lastgl"] = gl
                last["baseg"] = tg
                last["basel"] = tl
                kp = B.op("DVE", lambda e, p=p: e.tensor_scalar(
                    out=smv[p][:, 2:3], in0=smv[p][:, 0:1], scalar1=float(CAP),
                    scalar2=None, op0=ALU.is_lt),
                    deps=[gpos], sdeps=[gpos], label=f"m2kp{t}")
                gk = B.op("DVE", lambda e, p=p, t=t: e.tensor_tensor(
                    out=gk_sb[:, t:t + 1], in0=gate_sb[:, t:t + 1], in1=smv[p][:, 2:3],
                    op=ALU.mult),
                    deps=[kp], sdeps=[kp], label=f"m2gk{t}")
                a1 = B.op("DVE", lambda e, p=p, t=t: e.scalar_tensor_tensor(
                    out=smv[p][:, 3:4], in0=eidx_sb[:, t:t + 1], scalar=float(KPAD),
                    in1=smv[p][:, 1:2], op0=ALU.mult, op1=ALU.add),
                    deps=[lpos], sdeps=[lpos], label=f"m2a{t}")
                mk = B.op("DVE", lambda e, p=p: e.tensor_scalar(
                    out=smv[p][:, 4:5], in0=smv[p][:, 1:2], scalar1=float(KPAD),
                    scalar2=None, op0=ALU.is_lt),
                    deps=[lpos], sdeps=[lpos], label=f"m2mk{t}")
                i1 = B.op("DVE", lambda e, p=p: e.tensor_scalar(
                    out=smv[p][:, 5:6], in0=smv[p][:, 3:4], scalar1=-TRASH, scalar2=None,
                    op0=ALU.add),
                    deps=[a1], sdeps=[a1], label=f"m2i1{t}")
                i2 = B.op("DVE", lambda e, p=p: e.tensor_tensor(
                    out=smv[p][:, 5:6], in0=smv[p][:, 5:6], in1=smv[p][:, 4:5], op=ALU.mult),
                    deps=[i1, mk], sdeps=[i1, mk], label=f"m2i2{t}")
                i3 = B.op("DVE", lambda e, p=p: e.tensor_scalar(
                    out=smv[p][:, 5:6], in0=smv[p][:, 5:6], scalar1=TRASH, scalar2=None,
                    op0=ALU.add),
                    deps=[i2], sdeps=[i2], label=f"m2i3{t}")
                iu = B.op("DVE", lambda e, p=p, t=t: e.tensor_copy(idx_sb[:, t:t + 1],
                                                                  smv[p][:, 5:6]),
                          deps=[i3], sdeps=[i3], label=f"m2iu{t}")
                lx = B.dma("SP", xnb[p][:], d["xnorm_d"][tsl(t, 128), :],
                           deps=[last.get(("xnb", p))], label=f"m2lx{t}")
                sc_ = B.Q["POOL"].add("dma", lambda e, p=p, t=t: e.indirect_dma_start(
                    out=d["disp_d"], out_offset=IOA(idx_sb[:, t:t + 1], 0),
                    in_=xnb[p][:], in_offset=None),
                    deps=[iu, lx], label=f"m2sc{t}")
                last[("xnb", p)] = sc_
            for tin in range(TPS):
                last[("oh", tin)] = last["lastgl"]


def _moe_m3(B, ph):
    nc, d, sb = B.nc, B.d, B.sb
    p2 = contextlib.ExitStack()
    T2 = lambda n, s, dt=F32: B.sbuf(p2, n, s, dt)
    xdT = [T2(f"m3x{i}", [128, D], BF16) for i in range(KTILES)]
    xrow = [T2(f"m3r{i}", [128, D], BF16) for i in range(2)]
    w1 = T2("m3w1", [128, DC * 512], BF16)
    w3 = T2("m3w3", [128, DC * 512], BF16)
    w2 = T2("m3w2", [128, HC * D], BF16)
    hsil = [T2(f"m3h{i}", [128, 512], BF16) for i in range(2)]
    htmp = [T2(f"m3ht{i}", [128, 512], BF16) for i in range(2)]
    hT = [T2(f"m3hT{i}", [128, HC * 128], BF16) for i in range(KTILES)]
    oe = [T2(f"m3oe{i}", [128, D], BF16) for i in range(2)]
    pH = [B.psum(p2, f"m3pH{i}", [128, 512], F32) for i in range(4)]
    pTT = [B.psum(p2, f"m3pT{i}", [128, 1024], BF16) for i in range(2)]
    pO = [B.psum(p2, f"m3pO{i}", [128, 512], F32) for i in range(2)]
    with p2:
        last = {}
        for ex in range(E):
            w2l = B.dma("POOL", w2[:].rearrange("p (c n) -> p c n", c=HC),
                        d["W2"][ex].rearrange("(c dd) n -> dd c n", dd=128),
                        deps=[last.get("w2")], label=f"m3w2{ex}")
            for kt in range(KTILES):
                p = kt % 2
                lr = B.dma("SP", xrow[p][:], d["disp_d"][tsl(ex * KTILES + kt, 128), :],
                           deps=[last.get(("xr", p))], label=f"m3lr{ex}_{kt}")
                evs = []
                for half in range(2):
                    g = tr_group(B, [(pTT[half][:, tsl(i, 128)],
                                      xrow[p][:, tsl(half * 8 + i, 128)]) for i in range(8)],
                                 sb["ident"][:],
                                 deps=[lr, last.get(("pTT", half))], label=f"m3tr{ex}_{kt}_{half}")
                    ev = B.op("DVE", lambda e, kt=kt, half=half: e.tensor_copy(
                        xdT[kt][:, tsl(half, 1024)], pTT[half][:]),
                        deps=[g, last.get(("xdT", kt))], label=f"m3te{ex}_{kt}_{half}")
                    last[("pTT", half)] = ev
                    evs.append(ev)
                last[("xr", p)] = evs[-1]
                last[("xdTw", kt)] = evs
            hc0 = 0
            for nh, ncols in enumerate(HCH):
                l1 = B.dma("POOL", w1[:, 0:DC * ncols].rearrange("p (c n) -> p c n", c=DC),
                           d["W1"][ex].rearrange("(c dd) n -> dd c n", dd=128)[:, :, nh * 512: nh * 512 + ncols],
                           deps=[last.get("w1")], label=f"m3l1{ex}_{nh}")
                l3 = B.dma("POOL", w3[:, 0:DC * ncols].rearrange("p (c n) -> p c n", c=DC),
                           d["W3"][ex].rearrange("(c dd) n -> dd c n", dd=128)[:, :, nh * 512: nh * 512 + ncols],
                           deps=[last.get("w3")], label=f"m3l3{ex}_{nh}")
                nturn = ncols // 128
                g1l = g3l = None
                for kt in range(KTILES):
                    hp_ = kt % 2
                    b0 = hp_ * 2
                    g1 = mm_group(B, pH[b0][:, 0:ncols],
                                  [(xdT[kt][:, tsl(c, 128)], w1[:, c * ncols: (c + 1) * ncols])
                                   for c in range(DC)],
                                  deps=last[("xdTw", kt)] + [l1, last.get(("pH", b0))],
                                  label=f"m3g1{ex}_{nh}_{kt}")
                    g3 = mm_group(B, pH[b0 + 1][:, 0:ncols],
                                  [(xdT[kt][:, tsl(c, 128)], w3[:, c * ncols: (c + 1) * ncols])
                                   for c in range(DC)],
                                  deps=last[("xdTw", kt)] + [l3, last.get(("pH", b0 + 1))],
                                  label=f"m3g3{ex}_{nh}_{kt}")
                    sil = B.op("ACT", lambda e, hp_=hp_, b0=b0, ncols=ncols: e.activation(
                        hsil[hp_][:, 0:ncols], pH[b0][:, 0:ncols], AF.Silu),
                        deps=[g1, last.get(("hsil", hp_))], label=f"m3si{ex}_{nh}_{kt}")
                    hp = B.op("DVE", lambda e, hp_=hp_, b0=b0, ncols=ncols: e.tensor_tensor(
                        out=htmp[hp_][:, 0:ncols], in0=pH[b0 + 1][:, 0:ncols],
                        in1=hsil[hp_][:, 0:ncols], op=ALU.mult),
                        deps=[g3, sil, last.get(("htmp", hp_))], label=f"m3hp{ex}_{nh}_{kt}")
                    last[("pH", b0)] = hp
                    last[("pH", b0 + 1)] = hp
                    last[("hsil", hp_)] = hp
                    g = tr_group(B, [(pTT[hp_][:, tsl(i, 128)], htmp[hp_][:, tsl(i, 128)])
                                     for i in range(nturn)],
                                 sb["ident"][:],
                                 deps=[hp, last.get(("pTT", hp_))], label=f"m3trh{ex}_{nh}_{kt}")
                    ev = B.op("DVE", lambda e, kt=kt, hc0=hc0, nturn=nturn, hp_=hp_: e.tensor_copy(
                        hT[kt][:, hc0 * 128:(hc0 + nturn) * 128], pTT[hp_][:, 0:nturn * 128]),
                        deps=[g, last.get(("hT", kt))], label=f"m3the{ex}_{nh}_{kt}")
                    last[("pTT", hp_)] = ev
                    last[("htmp", hp_)] = g
                    last.setdefault(("hTw", kt), []).append(ev)
                last["w1"] = g1
                last["w3"] = g3
                hc0 += nturn
            for kt in range(KTILES):
                p = kt % 2
                oevs = []
                for n2 in range(4):
                    pb = n2 % 2
                    gO = mm_group(B, pO[pb][:],
                                  [(hT[kt][:, tsl(hc, 128)], w2[:, hc * D + n2 * 512: hc * D + n2 * 512 + 512])
                                   for hc in range(HC)],
                                  deps=last[("hTw", kt)] + [w2l, last.get(("pO", pb))],
                                  label=f"m3go{ex}_{kt}_{n2}")
                    ev = B.op("DVE", lambda e, p=p, n2=n2, pb=pb: e.tensor_copy(
                        oe[p][:, tsl(n2, 512)], pO[pb][:]),
                        deps=[gO, last.get(("oe", p))], label=f"m3oe{ex}_{kt}_{n2}")
                    last[("pO", pb)] = ev
                    oevs.append(ev)
                st = B.dma("POOL", d["dispo_d"][tsl(ex * KTILES + kt, 128), :], oe[p][:],
                           deps=oevs, label=f"m3st{ex}_{kt}")
                last[("oe", p)] = st
                last[("hT", kt)] = gO
            last["w2"] = gO
            for kt in range(KTILES):
                last[("xdT", kt)] = last[("hTw", kt)][-1]
                del last[("hTw", kt)]
        B.dma("POOL", d["dispo_d"][E * KPAD:E * KPAD + 128, :], sb["zs"][:],
              deps=[], label="m3ztrash")


def _moe_m4(B, ph, src, dst, gk_sb, idx_sb):
    nc, d, sb = B.nc, B.d, B.sb
    p2 = contextlib.ExitStack()
    T2 = lambda n, s, dt=F32: B.sbuf(p2, n, s, dt)
    yTt = [T2(f"m4y{i}", [128, D], BF16) for i in range(2)]
    w1 = T2("m4w1", [128, DC * 512], BF16)
    w3 = T2("m4w3", [128, DC * 512], BF16)
    w2 = T2("m4w2", [128, HC * D], BF16)
    hsil = [T2(f"m4h{i}", [128, 512], BF16) for i in range(2)]
    htmp = [T2(f"m4ht{i}", [128, 512], BF16) for i in range(2)]
    hTt = [T2(f"m4hT{i}", [128, HC * 128], BF16) for i in range(2)]
    mo = [T2(f"m4mo{i}", [128, D], BF16) for i in range(2)]
    xs = [T2(f"m4x{i}", [128, D]) for i in range(2)]
    xn = [T2(f"m4xn{i}", [128, D]) for i in range(2)]
    pH = [B.psum(p2, f"m4pH{i}", [128, 512], F32) for i in range(4)]
    pTT = [B.psum(p2, f"m4pT{i}", [128, 1024], BF16) for i in range(2)]
    pO = [B.psum(p2, f"m4pO{i}", [128, 512], F32) for i in range(2)]
    with p2:
        last = {}
        hc0 = 0
        for nh, ncols in enumerate(HCH):
            nturn = ncols // 128
            l1 = B.dma("POOL", w1[:, 0:DC * ncols].rearrange("p (c n) -> p c n", c=DC),
                       d["sw1"].rearrange("(c dd) n -> dd c n", dd=128)[:, :, nh * 512: nh * 512 + ncols],
                       deps=[last.get("w1")], label=f"m4l1{nh}")
            l3 = B.dma("POOL", w3[:, 0:DC * ncols].rearrange("p (c n) -> p c n", c=DC),
                       d["sw3"].rearrange("(c dd) n -> dd c n", dd=128)[:, :, nh * 512: nh * 512 + ncols],
                       deps=[last.get("w3")], label=f"m4l3{nh}")
            for t in range(NT):
                p = t % 2
                ly = B.dma("SP", yTt[p][:].rearrange("p (c n) -> p c n", c=DC),
                           d["yT_d"][t].rearrange("(c dd) n -> dd c n", dd=128),
                           deps=[last.get(("yT", p))], label=f"m4ly{nh}_{t}")
                b0 = p * 2
                g1 = mm_group(B, pH[b0][:, 0:ncols],
                              [(yTt[p][:, tsl(c, 128)], w1[:, c * ncols:(c + 1) * ncols])
                               for c in range(DC)],
                              deps=[ly, l1, last.get(("pH", b0))], label=f"m4g1{nh}_{t}")
                g3 = mm_group(B, pH[b0 + 1][:, 0:ncols],
                              [(yTt[p][:, tsl(c, 128)], w3[:, c * ncols:(c + 1) * ncols])
                               for c in range(DC)],
                              deps=[ly, l3, last.get(("pH", b0 + 1))], label=f"m4g3{nh}_{t}")
                last[("yT", p)] = g3
                sil = B.op("ACT", lambda e, p=p, b0=b0, ncols=ncols: e.activation(
                    hsil[p][:, 0:ncols], pH[b0][:, 0:ncols], AF.Silu),
                    deps=[g1, last.get(("hsil", p))], label=f"m4si{nh}_{t}")
                hp = B.op("DVE", lambda e, p=p, b0=b0, ncols=ncols: e.tensor_tensor(
                    out=htmp[p][:, 0:ncols], in0=pH[b0 + 1][:, 0:ncols],
                    in1=hsil[p][:, 0:ncols], op=ALU.mult),
                    deps=[g3, sil, last.get(("htmp", p))], label=f"m4hp{nh}_{t}")
                last[("pH", b0)] = hp
                last[("pH", b0 + 1)] = hp
                last[("hsil", p)] = hp
                g = tr_group(B, [(pTT[p][:, tsl(i, 128)], htmp[p][:, tsl(i, 128)])
                                 for i in range(nturn)],
                             sb["ident"][:],
                             deps=[hp, last.get(("pTT", p))], label=f"m4trh{nh}_{t}")
                ev = B.op("DVE", lambda e, p=p, nturn=nturn: e.tensor_copy(
                    hTt[p][:, 0:nturn * 128], pTT[p][:, 0:nturn * 128]),
                    deps=[g, last.get(("hTt", p))], label=f"m4the{nh}_{t}")
                last[("pTT", p)] = ev
                last[("htmp", p)] = g
                sh = B.dma("POOL", d["hT_d"][t][hc0 * 128:(hc0 + nturn) * 128, :]
                           .rearrange("(c dd) n -> dd c n", dd=128),
                           hTt[p][:, 0:nturn * 128].rearrange("p (c n) -> p c n", c=nturn),
                           deps=[ev], label=f"m4sh{nh}_{t}")
                last[("hTt", p)] = sh
            last["w1"] = g1
            last["w3"] = g3
            hc0 += nturn
        w2l = B.dma("POOL", w2[:].rearrange("p (c n) -> p c n", c=HC),
                    d["sw2"].rearrange("(c dd) n -> dd c n", dd=128),
                    label="m4w2")
        hTf = [T2(f"m4hf{i}", [128, HC * 128], BF16) for i in range(2)]
        for t in range(NT):
            p = t % 2
            lh = B.dma("SP", hTf[p][:].rearrange("p (c n) -> p c n", c=HC),
                       d["hT_d"][t].rearrange("(c dd) n -> dd c n", dd=128),
                       deps=[last.get(("hTf", p))], label=f"m4lh{t}")
            lx = B.dma("SP", xs[p][:], d[src][tsl(t, 128), :],
                       deps=[last.get(("x", p))], label=f"m4lx{t}")
            lmo = B.Q["POOL"].add("dma", lambda e, p=p, t=t: e.indirect_dma_start(
                out=mo[p][:], out_offset=None,
                in_=d["dispo_d"], in_offset=IOA(idx_sb[:, t:t + 1], 0)),
                deps=[last.get(("mo", p))], label=f"m4gm{t}")
            combs = []
            for n2 in range(4):
                pb = n2 % 2
                gO = mm_group(B, pO[pb][:],
                              [(hTf[p][:, tsl(hc, 128)], w2[:, hc * D + n2 * 512: hc * D + n2 * 512 + 512])
                               for hc in range(HC)],
                              deps=[lh, w2l, last.get(("pO", pb))], label=f"m4go{t}_{n2}")
                c1 = B.op("DVE", lambda e, p=p, n2=n2, pb=pb, t=t: e.scalar_tensor_tensor(
                    out=xn[p][:, tsl(n2, 512)], in0=mo[p][:, tsl(n2, 512)],
                    scalar=gk_sb[:, t:t + 1], in1=pO[pb][:], op0=ALU.mult, op1=ALU.add),
                    deps=[gO, lmo, last.get(("xn", p))], label=f"m4c1{t}_{n2}")
                c2 = B.op("DVE", lambda e, p=p, n2=n2: e.tensor_add(
                    xn[p][:, tsl(n2, 512)], xn[p][:, tsl(n2, 512)], xs[p][:, tsl(n2, 512)]),
                    deps=[c1, lx], sdeps=[c1], label=f"m4c2{t}_{n2}")
                last[("pO", pb)] = c2
                combs.append(c2)
            last[("hTf", p)] = gO
            last[("x", p)] = combs[-1]
            last[("mo", p)] = combs[-1]
            st = B.dma("POOL", d[dst][tsl(t, 128), :], xn[p][:], deps=combs, label=f"m4st{t}")
            last[("xn", p)] = st


def phase_b1(B):
    nc, d, sb = B.nc, B.d, B.sb
    p2 = contextlib.ExitStack()
    T2 = lambda n, s, dt=F32: B.sbuf(p2, n, s, dt)
    xsb = [T2(f"b1x{i}", [128, D]) for i in range(2)]
    ysb = [T2(f"b1y{i}", [128, D]) for i in range(2)]
    xnb = [T2(f"b1n{i}", [128, D]) for i in range(2)]
    xbf = [T2(f"b1b{i}", [128, D], BF16) for i in range(2)]
    sc1 = [T2(f"b1s1{i}", [128, D]) for i in range(2)]
    sm = [T2(f"b1sm{i}", [128, 16]) for i in range(2)]
    gam = T2("b1g", [128, D])
    bet = T2("b1be", [128, D])
    xnT = T2("b1xT", [128, DC * 2048], BF16)
    wsb = [T2(f"b1w{i}", [128, DC * 512], BF16) for i in range(2)]
    ev_sb = [T2(f"b1qe{i}", [128, 512], BF16) for i in range(2)]
    pT = [B.psum(p2, f"b1pT{i}", [128, 1024], BF16) for i in range(2)]
    pP = [B.psum(p2, f"b1pP{i}", [128, 512], F32) for i in range(2)]
    with p2:
        lgm = B.dma("SP", gam[:], d["gammaI"].broadcast_to([128, D]), label="b1lg")
        lbt = B.dma("SP", bet[:], d["betaI"].broadcast_to([128, D]), label="b1lb")
        last = {}
        B.kb_cc, B.vb_cc = {}, {}
        for g2 in range(2):
            tevs = []
            for ti in range(16):
                t = g2 * 16 + ti
                p = t % 2
                lx = B.dma("SP", xsb[p][:], d["resid"][tsl(t, 128), :],
                           deps=[last.get(("x", p))], label=f"b1lx{t}")
                yop = ln_tile(B, sb, xsb[p][:], ysb[p][:], sc1[p][:],
                              sm[p][:, 1:2], sm[p][:, 3:4], sm[p][:, 4:5],
                              sm[p][:, 0:1], sm[p][:, 2:3], lx,
                              y_war=last.get(("y", p)), label=f"b1ln{t}")
                last[("x", p)] = yop
                sy = B.dma("POOL", d["y_d"][tsl(t, 128), :], ysb[p][:],
                           deps=[yop], label=f"b1sy{t}")
                yop2 = ln_tile(B, sb, ysb[p][:], xnb[p][:], sc1[p][:],
                               sm[p][:, 5:6], sm[p][:, 6:7], sm[p][:, 7:8],
                               sm[p][:, 8:9], sm[p][:, 9:10], yop,
                               y_war=last.get(("xnb", p)), scr_war=yop,
                               label=f"b1li{t}")
                last[("y", p)] = [yop2, sy]
                ga = B.op("DVE", lambda e, p=p: e.tensor_mul(xnb[p][:], xnb[p][:], gam[:]),
                          deps=[yop2, lgm], sdeps=[yop2], label=f"b1ga{t}")
                be = B.op("DVE", lambda e, p=p: e.tensor_add(xnb[p][:], xnb[p][:], bet[:]),
                          deps=[ga, lbt], sdeps=[ga], label=f"b1be{t}")
                cop = B.op("ACT", lambda e, p=p: e.copy(xbf[p][:], xnb[p][:]),
                           deps=[be, last.get(("xbf", p))], label=f"b1c{t}")
                last[("xnb", p)] = cop
                for half in range(2):
                    g = tr_group(B, [(pT[half][:, tsl(i, 128)],
                                      xbf[p][:, tsl(half * 8 + i, 128)]) for i in range(8)],
                                 sb["ident"][:],
                                 deps=[cop, last.get(("pT", half))], label=f"b1tr{t}_{half}")
                    ev = B.op("DVE", lambda e, ti=ti, half=half: e.tensor_copy(
                        xnT[:].rearrange("p (c n) -> p c n", c=DC)
                        [:, tsl(half, 8), ti * 128: ti * 128 + 128],
                        pT[half][:].rearrange("p (c n) -> p c n", c=8)),
                        deps=[g, last.get("xnTw")], label=f"b1ev{t}_{half}")
                    last[("pT", half)] = ev
                    last[("xbf", p)] = g
                    tevs.append(ev)
            wlast = None
            for slab in range(12):
                wp = slab % 2
                wl = B.dma("POOL", wsb[wp][:].rearrange("p (c n) -> p c n", c=DC),
                           d["qkvw"].rearrange("(c dd) n -> dd c n", dd=128)[:, :, tsl(slab, 512)],
                           deps=[last.get(("w", wp))], label=f"b1w{g2}_{slab}")
                if slab < 8:
                    for ob in range(4):
                        for tsl_ in range(4):
                            pb = (ob * 4 + tsl_) % 2
                            grp = mm_group(B, pP[pb][:],
                                           [(wsb[wp][:, c * 512 + ob * 128: c * 512 + ob * 128 + 128],
                                             xnT[:, c * 2048 + tsl_ * 512: c * 2048 + tsl_ * 512 + 512])
                                            for c in range(DC)],
                                           deps=tevs + [wl, last.get(("pP", pb))],
                                           label=f"b1g{g2}_{slab}_{ob}_{tsl_}")
                            ev = B.op("DVE", lambda e, pb=pb: e.tensor_copy(
                                ev_sb[pb][:], pP[pb][:]),
                                deps=[grp, last.get(("evs", pb))], label=f"b1pe{g2}_{slab}_{ob}_{tsl_}")
                            last[("pP", pb)] = ev
                            o0 = slab * 512 + ob * 128 if slab < 4 else (slab - 4) * 512 + ob * 128
                            if slab < 4:
                                sts = []
                                for i in range(4):
                                    tile = g2 * 16 + tsl_ * 4 + i
                                    st = B.dma("POOL",
                                               d["QTb"][tile][o0:o0 + 128, :],
                                               ev_sb[pb][:, tsl(i, 128)],
                                               deps=[ev], label=f"b1sq{g2}_{slab}_{ob}_{tsl_}_{i}")
                                    sts.append(st)
                                last[("evs", pb)] = sts[-1]
                            else:
                                blk = g2 * 2 + tsl_ // 2
                                col0 = (tsl_ % 2) * 512
                                st = B.dma("POOL", d[f"kb_in{blk}"][o0:o0 + 128, col0:col0 + 512],
                                           ev_sb[pb][:], deps=[ev],
                                           label=f"b1sk{g2}_{slab}_{ob}_{tsl_}")
                                last[("evs", pb)] = st
                                last.setdefault(("kst", blk), []).append(st)
                else:
                    for ti in range(16):
                        pb = ti % 2
                        grp = mm_group(B, pP[pb][:],
                                       [(xnT[:, c * 2048 + ti * 128: c * 2048 + ti * 128 + 128],
                                         wsb[wp][:, tsl(c, 512)])
                                        for c in range(DC)],
                                       deps=tevs + [wl, last.get(("pP", pb))],
                                       label=f"b1v{g2}_{slab}_{ti}")
                        ev = B.op("DVE", lambda e, pb=pb: e.tensor_copy(ev_sb[pb][:], pP[pb][:]),
                                  deps=[grp, last.get(("evs", pb))], label=f"b1ve{g2}_{slab}_{ti}")
                        last[("pP", pb)] = ev
                        blk = g2 * 2 + ti // 8
                        r0 = (ti % 8) * 128
                        st = B.dma("POOL", d[f"vb_in{blk}"][r0:r0 + 128, tsl(slab - 8, 512)],
                                   ev_sb[pb][:], deps=[ev], label=f"b1sv{g2}_{slab}_{ti}")
                        last[("evs", pb)] = st
                        last.setdefault(("vst", blk), []).append(st)
                last[("w", wp)] = grp
                wlast = grp
            last["xnTw"] = wlast
            for blk in (g2 * 2, g2 * 2 + 1):
                B.kb_cc[blk] = B.collective("AllGather", ALU.bypass,
                                            [[0, 1], [2, 3], [4, 5], [6, 7]],
                                            d[f"kb_in{blk}"], d[f"kbag{blk}"],
                                            deps=last[("kst", blk)], label=f"b1agk{blk}")
                B.vb_cc[blk] = B.collective("AllGather", ALU.bypass,
                                            [[0, 1], [2, 3], [4, 5], [6, 7]],
                                            d[f"vb_in{blk}"], d[f"vbag{blk}"],
                                            deps=last[("vst", blk)], label=f"b1agv{blk}")
                if f"kbag{blk}_dbg" in d:
                    B.dma("POOL", d[f"kbag{blk}_dbg"], d[f"kbag{blk}"],
                          deps=[B.kb_cc[blk]], label=f"dbgk{blk}")
                if f"vbag{blk}_dbg" in d:
                    B.dma("POOL", d[f"vbag{blk}_dbg"], d[f"vbag{blk}"],
                          deps=[B.vb_cc[blk]], label=f"dbgv{blk}")


_NC_CACHE = {}


def _get_nc(upto):
    if upto not in _NC_CACHE:
        _NC_CACHE[upto] = build_nc(upto)
    return _NC_CACHE[upto]


def _host_constants():
    c = {}
    c["ident"] = np.eye(128, dtype=ml_dtypes.bfloat16)
    c["lt128"] = np.triu(np.ones((128, 128), np.float32), k=1)
    c["ones128"] = np.ones((128, 128), np.float32)
    c["lts8"] = np.triu(np.ones((SLOTS, SLOTS), np.float32), k=1).T.copy()
    c["lts8"] = np.triu(np.ones((SLOTS, SLOTS), np.float32), k=1)
    return c


def _core_constants(core):
    b, par = core // 2, core % 2
    cset = SETS[par]
    pos = np.concatenate([np.arange(c * CH, (c + 1) * CH) for c in cset]).astype(np.float64)
    idx = np.arange(HD // 2, dtype=np.float64)
    inv_freq = 1.0 / ROPE_THETA ** (2.0 * idx / HD)
    fr = pos[:, None] * inv_freq[None, :]
    cosT = np.cos(fr).astype(np.float32)
    sinT = np.sin(fr).astype(np.float32)
    mask = np.zeros((SLOTS * CH, 1024), np.float32)
    for j in range(SLOTS):
        cg = cset[j]
        bound = (j + 1) * 1024
        k_global = np.arange(bound - 1024, bound)
        q_global = cg * CH + np.arange(CH)
        mask[j * CH:(j + 1) * CH, :] = (k_global[None, :] <= q_global[:, None])
    sel = np.zeros((NCORE * SLOTS, SLOTS), np.float32)
    for r in range(NCORE):
        for jj in range(SLOTS):
            gc_r = (r // 2) * NCH + SETS[r % 2][jj]
            for j in range(SLOTS):
                gc_me = b * NCH + cset[j]
                if gc_r < gc_me:
                    sel[r * SLOTS + jj, j] = 1.0
    return dict(cosT=cosT, sinT=sinT, maskT=mask, selT=sel), cset


def prep_inmaps(L0_attn_a_x, L0_attn_a_q_proj, L0_attn_a_k_proj, L0_attn_a_v_proj,
                L0_attn_a_o_proj, L0_attn_a_q_norm, L0_attn_a_k_norm,
                L0_attn_b_qkv, L0_attn_b_o_proj, L0_attn_b_gamma, L0_attn_b_beta,
                L0_ffn_Wg, L0_ffn_W1, L0_ffn_W3, L0_ffn_W2,
                L0_ffn_sw1, L0_ffn_sw3, L0_ffn_sw2):
    cst = _host_constants()
    asc = np.ascontiguousarray
    shared = {
        "qTw": asc(np.asarray(L0_attn_a_q_proj).T),
        "kTw": asc(np.asarray(L0_attn_a_k_proj).T),
        "vTw": asc(np.asarray(L0_attn_a_v_proj).T),
        "oTw": asc(np.asarray(L0_attn_a_o_proj).T),
        "qkvw": asc(np.asarray(L0_attn_b_qkv)),
        "obw": asc(np.asarray(L0_attn_b_o_proj)),
        "Wg": asc(np.asarray(L0_ffn_Wg)),
        "W1": asc(np.asarray(L0_ffn_W1)),
        "W3": asc(np.asarray(L0_ffn_W3)),
        "W2": asc(np.asarray(L0_ffn_W2)),
        "sw1": asc(np.asarray(L0_ffn_sw1)),
        "sw3": asc(np.asarray(L0_ffn_sw3)),
        "sw2": asc(np.asarray(L0_ffn_sw2)),
        "qgain": asc(np.asarray(L0_attn_a_q_norm).reshape(1, HD)),
        "kgain": asc(np.asarray(L0_attn_a_k_norm).reshape(1, HD)),
        "gammaI": asc(np.asarray(L0_attn_b_gamma).reshape(1, D)),
        "betaI": asc(np.asarray(L0_attn_b_beta).reshape(1, D)),
    }
    shared.update({k: v for k, v in cst.items()})
    x = np.asarray(L0_attn_a_x)
    in_maps = []
    csets = []
    for core in range(NCORE):
        b, par = core // 2, core % 2
        cc, cset = _core_constants(core)
        csets.append(cset)
        xc = x[b].reshape(NCH, CH, D)[cset].reshape(TOK, D)
        m = dict(shared)
        m.update(cc)
        m["x_in"] = asc(xc)
        in_maps.append(m)
    return in_maps, csets


def assemble(res_list, csets, key="out_x"):
    out = np.empty((BATCH, SEQ, D), np.float32)
    for core in range(NCORE):
        b = core // 2
        oc = np.asarray(res_list[core][key]).reshape(SLOTS, CH, D)
        for j, cg in enumerate(csets[core]):
            out[b, cg * CH:(cg + 1) * CH] = oc[j]
    return out


def kernel(**inputs):
    nc = _get_nc(UPTO)
    in_maps, csets = prep_inmaps(**inputs)
    res = run_bass_kernel_spmd(nc, in_maps, core_ids=list(range(NCORE)))
    return assemble(res.results, csets)
